# version 20
# baseline (speedup 1.0000x reference)
import numpy as np
import concourse.bass as bass
import concourse.mybir as mybir

F32 = mybir.dt.float32
BF16 = mybir.dt.bfloat16
U8 = mybir.dt.uint8
AF = mybir.ActivationFunctionType
ALU = mybir.AluOpType
AX = mybir.AxisListType

ENGINES = ("pe", "act", "dve", "pool", "sp")
BLOCK_ATTR = {"pe": "tensor", "act": "scalar", "dve": "vector", "pool": "gpsimd", "sp": "sync"}
DMA_SLOTS = 10

_ESIZE = {F32: 4, BF16: 2, U8: 1, mybir.dt.int32: 4}


class Buf:
    __slots__ = ("name", "last_w", "readers", "pending")

    def __init__(self, name):
        self.name = name
        self.last_w = None
        self.readers = []
        self.pending = []

    def __repr__(self):
        return f"Buf({self.name})"


class Op:
    __slots__ = ("idx", "eng", "emit", "deps", "sem_key", "amount", "cum", "tag",
                 "self_inc")

    def __init__(self, idx, eng, emit, deps, sem_key, amount, tag):
        self.idx = idx
        self.eng = eng
        self.emit = emit
        self.deps = deps
        self.sem_key = sem_key
        self.amount = amount
        self.cum = None
        self.tag = tag
        self.self_inc = False


class Prog:
    def __init__(self):
        self.ops = []
        self.eng_ops = {e: [] for e in ENGINES}
        self.dma_count = {}
        self.slot_last = {}
        self.sem_keys = set(("eng", e) for e in ENGINES)
        self.n_waits = 0

    def op(self, eng, emit, reads=(), writes=(), tag="", extra_deps=()):
        return self._record(eng, emit, reads, writes, ("eng", eng), 1, tag,
                            extra_deps=extra_deps)

    def dma(self, queue, emit, reads=(), writes=(), tag=""):
        i = self.dma_count.get(queue, 0)
        self.dma_count[queue] = i + 1
        slot = i % DMA_SLOTS
        key = ("dma", queue, slot)
        self.sem_keys.add(key)
        o = self._record(queue, emit, reads, writes, key, 16, tag,
                         extra_dep=self.slot_last.get((queue, slot)))
        self.slot_last[(queue, slot)] = o
        return o

    def dma_multi(self, queue, emit, n_dma, reads=(), writes=(), tag=""):
        i = self.dma_count.get(queue, 0)
        self.dma_count[queue] = i + 1
        slot = i % DMA_SLOTS
        key = ("dma", queue, slot)
        self.sem_keys.add(key)
        o = self._record(queue, emit, reads, writes, key, 16 * n_dma, tag,
                         extra_dep=self.slot_last.get((queue, slot)))
        o.self_inc = True
        self.slot_last[(queue, slot)] = o
        return o

    def collective(self, emit, reads=(), writes=(), tag="cc"):
        key = ("cc",)
        self.sem_keys.add(key)
        return self._record("pool", emit, reads, writes, key, 1, tag)

    def _record(self, eng, emit, reads, writes, sem_key, amount, tag, extra_dep=None,
                extra_deps=()):
        deps = set()
        for b in reads:
            if b.last_w is not None:
                deps.add(b.last_w)
            deps.update(b.pending)
        for b in writes:
            if b.last_w is not None:
                deps.add(b.last_w)
            deps.update(b.readers)
            deps.update(b.pending)
        if extra_dep is not None:
            deps.add(extra_dep)
        deps.update(extra_deps)
        o = Op(len(self.ops), eng, emit, deps, sem_key, amount, tag)
        self.ops.append(o)
        self.eng_ops[eng].append(o)
        for b in reads:
            b.readers.append(o)
        for b in writes:
            b.last_w = o
            b.readers = []
            b.pending = []
        return o

    def emit(self, nc, es):
        sems = {}
        for k in sorted(self.sem_keys, key=repr):
            sems[k] = es.enter_context(nc.semaphore("s" + "_".join(map(str, k))))
        run = {k: 0 for k in self.sem_keys}
        for o in self.ops:
            run[o.sem_key] += o.amount
            o.cum = run[o.sem_key]
        self.final_counts = dict(run)
        self.sems = sems

        block = es.enter_context(nc.Block())
        for eng_name in ENGINES:
            ops = self.eng_ops[eng_name]

            def body(engobj, ops=ops, eng_name=eng_name):
                pe_key = ("eng", "pe")
                hw = {}
                for o in ops:
                    waits = {}
                    for d in o.deps:
                        if eng_name == "pe" and d.sem_key == pe_key:
                            continue
                        if waits.get(d.sem_key, 0) < d.cum:
                            waits[d.sem_key] = d.cum
                    todo = [(k, v) for k, v in sorted(waits.items(), key=lambda kv: repr(kv[0]))
                            if hw.get(k, 0) < v]
                    if o.self_inc:
                        for k, v in todo:
                            engobj.wait_ge(sems[k], v)
                            hw[k] = v
                            self.n_waits += 1
                        o.emit(engobj, sems[o.sem_key])
                        continue
                    for k, v in todo[:-1]:
                        engobj.wait_ge(sems[k], v)
                        hw[k] = v
                        self.n_waits += 1
                    res = o.emit(engobj)
                    assert res is not None, f"op {o.tag} emit returned None"
                    first, last = res if isinstance(res, tuple) else (res, res)
                    if todo:
                        k, v = todo[-1]
                        first._wait_ge(sems[k], v)
                        hw[k] = v
                        self.n_waits += 1
                    last.then_inc(sems[o.sem_key], o.amount)

            getattr(block, BLOCK_ATTR[eng_name])(body)
        return sems


class Region:
    __slots__ = ("arena", "off", "size", "buf", "freed", "tile_bufs")

    def __init__(self, arena, off, size, name):
        self.arena = arena
        self.off = off
        self.size = size
        self.buf = Buf(name)
        self.freed = False

    def view(self, dtype, shape, byte_off=0):
        es = _ESIZE[dtype]
        n = int(np.prod(shape[1:])) * es
        assert byte_off + n <= self.size, f"{self.buf.name}: view {byte_off}+{n} > {self.size}"
        o = self.off + byte_off
        assert o % es == 0
        ap = self.arena.h.ap()[0:shape[0], o:o + n].bitcast(dtype)
        if len(shape) > 2:
            names = [f"d{i}" for i in range(len(shape) - 1)]
            ap = ap.rearrange(f"p ({' '.join(names)}) -> p {' '.join(names)}",
                              **{nm: shape[i + 1] for i, nm in enumerate(names)})
        return ap


class SbufArena:
    def __init__(self, nc, es, name, bytes_per_partition):
        self.h = es.enter_context(nc.sbuf_tensor(name, [128, bytes_per_partition], U8))
        self.cap = bytes_per_partition
        self.free_ranges = [(0, bytes_per_partition)]
        self.ghosts = []
        self.peak = 0

    def alloc(self, name, nbytes, align=64):
        nbytes = (nbytes + align - 1) // align * align
        for i, (off, size) in enumerate(self.free_ranges):
            aoff = (off + align - 1) // align * align
            if aoff + nbytes <= off + size:
                pre = (off, aoff - off)
                post = (aoff + nbytes, off + size - aoff - nbytes)
                repl = [r for r in (pre, post) if r[1] > 0]
                self.free_ranges[i:i + 1] = repl
                r = Region(self, aoff, nbytes, name)
                keep = []
                for g_off, g_size, g_deps in self.ghosts:
                    if g_off < aoff + nbytes and aoff < g_off + g_size:
                        r.buf.pending.extend(g_deps)
                        if g_off < aoff:
                            keep.append((g_off, aoff - g_off, g_deps))
                        if g_off + g_size > aoff + nbytes:
                            keep.append((aoff + nbytes, g_off + g_size - aoff - nbytes, g_deps))
                    else:
                        keep.append((g_off, g_size, g_deps))
                self.ghosts = keep
                self.peak = max(self.peak, self.used())
                return r
        raise MemoryError(f"SBUF arena: no room for {name} ({nbytes}B/part); free={self.free_ranges}")

    def free(self, region):
        assert not region.freed
        region.freed = True
        deps = list(region.buf.readers)
        if region.buf.last_w is not None:
            deps.append(region.buf.last_w)
        deps.extend(region.buf.pending)
        if deps:
            self.ghosts.append((region.off, region.size, deps))
        self.free_ranges.append((region.off, region.size))
        self.free_ranges.sort()
        merged = []
        for off, size in self.free_ranges:
            if merged and merged[-1][0] + merged[-1][1] == off:
                merged[-1] = (merged[-1][0], merged[-1][1] + size)
            else:
                merged.append((off, size))
        self.free_ranges = [tuple(x) for x in merged]

    def used(self):
        return self.cap - sum(s for _, s in self.free_ranges)


class PsumPool:

    def __init__(self, nc, es):
        self.h = es.enter_context(nc.psum_tensor("ps", [128, 8 * 512], F32))
        self.free_banks = list(range(8))
        self.ghosts = {}

    def alloc(self, name):
        assert self.free_banks, f"psum pool empty allocating {name}"
        b = self.free_banks.pop(0)
        r = PsumRegion(self, b, name)
        r.buf.pending.extend(self.ghosts.pop(b, []))
        return r

    def free(self, region):
        assert not region.freed
        region.freed = True
        deps = list(region.buf.readers)
        if region.buf.last_w is not None:
            deps.append(region.buf.last_w)
        deps.extend(region.buf.pending)
        self.ghosts[region.bank] = deps
        self.free_banks.append(region.bank)


class PsumRegion:
    __slots__ = ("pool", "bank", "buf", "freed")

    def __init__(self, pool, bank, name):
        self.pool = pool
        self.bank = bank
        self.buf = Buf(name)
        self.freed = False

    def view(self, dtype=F32, shape=(128, 512)):
        es = _ESIZE[dtype]
        n = int(np.prod(shape[1:]))
        assert n * es <= 2048
        ap = self.pool.h.ap()[0:shape[0], self.bank * 512:self.bank * 512 + (n * es) // 4]
        if dtype != F32:
            ap = ap.bitcast(dtype)
        if len(shape) > 2:
            names = [f"d{i}" for i in range(len(shape) - 1)]
            ap = ap.rearrange(f"p ({' '.join(names)}) -> p {' '.join(names)}",
                              **{nm: shape[i + 1] for i, nm in enumerate(names)})
        return ap


"""Phase builders for the mk_hybrid_d1 megakernel. All sizes per core.

Layouts:
  x (residual): DRAM [1024, 2048] f32, token-major, updated in place.
  xn_fm: SBUF [128, 16cc, 1024t] bf16 (feature-major; partition = c within cc).
  token-major tiles: [128 t, free] with 8 ttiles of 128 tokens.
"""
import numpy as np
bass_ts = bass.ts
IndirectOffsetOnAxis = bass.IndirectOffsetOnAxis


def tile_buf(b, t):
    return b[t] if isinstance(b, (list, tuple)) else b


def reg_reader(buf, op):
    buf.readers.append(op)

TCORE = 1024
NTT = TCORE // 128
NCC = 16
D = 2048
NI = 11
NIS = 22
NE = 8


def ceil_div(a, b):
    return (a + b - 1) // b


class Ctx:

    def __init__(self, nc, prog, arena, psum):
        self.nc = nc
        self.P = prog
        self.A = arena
        self.PS = psum
        self.const = {}
        self.dram = {}

    def dram_in(self, name, shape, dtype=F32):
        h = self.nc.dram_tensor(name, list(shape), dtype, kind="ExternalInput")
        b = Buf("dram_" + name)
        self.dram[name] = (h, b)
        return h.ap(), b

    def dram_out(self, name, shape, dtype=F32):
        h = self.nc.dram_tensor(name, list(shape), dtype, kind="ExternalOutput")
        b = Buf("dram_" + name)
        self.dram[name] = (h, b)
        return h.ap(), b

    def dram_tmp(self, name, shape, dtype=F32, addr_space="Local"):
        h = self.nc.dram_tensor(name, list(shape), dtype, kind="Internal", addr_space=addr_space)
        b = Buf("dramtmp_" + name)
        self.dram[name] = (h, b)
        return h.ap(), b

    CONST_VALS = (1e-5, -1e30, 0.08838834764831845, 0.17677669529663687)

    def setup_const_vals(self, src_ap, src_buf):
        n = len(self.CONST_VALS)
        r = self.A.alloc("cvals", n * 4)
        ap = r.view(F32, (128, n))
        self.P.dma("sp", lambda e, o=ap, i=src_ap: e.dma_start(out=o, in_=i),
                   reads=[src_buf], writes=[r.buf], tag="ld_cvals")
        for j, v in enumerate(self.CONST_VALS):
            self.nc.const_aps.register(F32, v, ap[:, j:j + 1])
        self.cval_buf = r.buf

    def prologue_barrier(self, bufs):
        for eng in ENGINES:
            self.P.op(eng, lambda e: e.nop(), reads=list(bufs), tag=f"prolog_{eng}")

    def load_const(self, name, src_ap, src_buf, shape, dtype, queue="sp"):
        nbytes = int(np.prod(shape[1:])) * _ESIZE[dtype]
        r = self.A.alloc("const_" + name, nbytes)
        ap = r.view(dtype, shape)
        self.P.dma(queue, lambda e, o=ap, i=src_ap: e.dma_start(out=o, in_=i),
                   reads=[src_buf], writes=[r.buf], tag="ldconst_" + name)
        self.const[name] = (r, ap)
        return ap, r.buf


def build_ln(C, x_dram_ap, x_dram_buf, xn_fm_r, tag=""):
    P, A, PS = C.P, C.A, C.PS
    ident = C.const["ident"][1]
    ident_b = C.const["ident"][0].buf

    xs = [A.alloc(f"lnx{tag}{i}", 2048 * 4) for i in range(2)]
    sc = A.alloc(f"lnsc{tag}", 32 * 4)
    xn_tm = [A.alloc(f"lnxn{tag}{i}", 2048 * 2) for i in range(2)]
    xn_fm_ap = xn_fm_r.view(BF16, (128, NCC, TCORE))
    xn_fm_r.tile_bufs = [Buf(f"xnfm{tag}_t{t}") for t in range(NTT)]

    for tt in range(NTT):
        xb = xs[tt % 2]
        x_ap = xb.view(F32, (128, 2048))
        P.dma("sp", lambda e, o=x_ap, i=x_dram_ap[tt * 128:(tt + 1) * 128, :]:
              e.dma_start(out=o, in_=i), reads=[tile_buf(x_dram_buf, tt)],
              writes=[xb.buf], tag=f"ln_ldx{tag}{tt}")
        st = sc.view(F32, (128, 24))
        mv = sc.view(F32, (128, 2), byte_off=96)
        rstd = sc.view(F32, (128, 1), byte_off=104)

        def em_stats(e, x_ap=x_ap, st=st):
            first = None
            for j in range(4):
                last = e.bn_stats(st[:, j * 6:(j + 1) * 6], x_ap[:, j * 512:(j + 1) * 512])
                first = first or last
            return (first, last)
        P.op("dve", em_stats, reads=[xb.buf], writes=[sc.buf], tag=f"ln_stats{tag}{tt}")
        P.op("dve", lambda e, st=st, mv=mv: e.bn_aggr(mv, st),
             reads=[sc.buf], writes=[sc.buf], tag=f"ln_aggr{tag}{tt}")
        P.op("act", lambda e, mv=mv, rstd=rstd: e.activation(rstd, mv[:, 1:2], AF.Sqrt, bias=1e-5),
             reads=[sc.buf], writes=[sc.buf], tag=f"ln_sqrt{tag}{tt}")
        P.op("dve", lambda e, rstd=rstd: e.reciprocal(rstd, rstd),
             reads=[sc.buf], writes=[sc.buf], tag=f"ln_recip{tag}{tt}")
        xnb = xn_tm[tt % 2]
        xn_ap = xnb.view(BF16, (128, 2048))
        P.op("dve", lambda e, x_ap=x_ap, xn_ap=xn_ap, mv=mv, rstd=rstd:
             e.tensor_scalar(xn_ap, x_ap, mv[:, 0:1], rstd[:, 0:1], ALU.subtract, ALU.mult),
             reads=[xb.buf, sc.buf], writes=[xnb.buf], tag=f"ln_norm{tag}{tt}")
        for g in range(4):
            bank = PS.alloc(f"lnT{tag}{tt}{g}")
            bap = bank.view(BF16, (128, 4, 128))

            def em_tr(e, xn_ap=xn_ap, bap=bap, g=g, ident=ident):
                first = None
                for j in range(4):
                    cc = g * 4 + j
                    last = e.transpose(bap[:, j, :], xn_ap[:, cc * 128:(cc + 1) * 128], ident)
                    first = first or last
                return (first, last)
            P.op("pe", em_tr, reads=[xnb.buf, ident_b], writes=[bank.buf], tag=f"ln_tr{tag}{tt}{g}")
            dst = xn_fm_ap[:, g * 4:(g + 1) * 4, tt * 128:(tt + 1) * 128]
            P.op("dve", lambda e, bap=bap, dst=dst: e.tensor_copy(dst, bap),
                 reads=[bank.buf], writes=[xn_fm_r.buf, xn_fm_r.tile_bufs[tt]],
                 tag=f"ln_trc{tag}{tt}{g}")
            PS.free(bank)
    for b in xs + xn_tm + [sc]:
        A.free(b)


def build_ffn(C, layer, xn_fm_r, x_dram_ap, x_dram_buf, out_dram_ap, out_dram_buf, tag="",
              pre=None):
    P, A, PS = C.P, C.A, C.PS
    nc = C.nc
    xn_fm = xn_fm_r.view(BF16, (128, NCC, TCORE))
    w = layer

    gw_ap = C.const["gate_wT"][1]
    gw_b = C.const["gate_wT"][0].buf
    comb = A.alloc(f"comb{tag}", NTT * NE * 4)
    comb_ap = comb.view(F32, (128, NTT, NE))
    rsc = A.alloc(f"rsc{tag}", 64 * 4)
    for tt in range(NTT):
        bank = PS.alloc(f"rt{tag}{tt}")
        ps = bank.view(F32, (128, NE))

        def em_rt(e, ps=ps, tt=tt):
            first = None
            for cc in range(NCC):
                last = e.matmul(ps, xn_fm[:, cc, tt * 128:(tt + 1) * 128], gw_ap[:, cc, :],
                                start=(cc == 0), stop=(cc == NCC - 1))
                first = first or last
            return (first, last)
        P.op("pe", em_rt, reads=[xn_fm_r.buf, gw_b], writes=[bank.buf], tag=f"rt_mm{tag}{tt}")
        ex = rsc.view(F32, (128, NE))
        s1 = rsc.view(F32, (128, 1), byte_off=32)
        sc_v = rsc.view(F32, (128, NE), byte_off=48)
        m1 = rsc.view(F32, (128, 1), byte_off=96)
        msk = rsc.view(F32, (128, NE), byte_off=112)
        m2 = rsc.view(F32, (128, 1), byte_off=160)
        ge = rsc.view(F32, (128, NE), byte_off=176)
        b = rsc.buf
        cv = comb_ap[:, tt, :]
        P.op("act", lambda e, ps=ps, ex=ex: e.activation(ex, ps, AF.Exp),
             reads=[bank.buf], writes=[b], tag=f"rt_exp{tag}{tt}")
        P.op("dve", lambda e, ex=ex, s1=s1: e.reduce_sum(s1, ex, axis=AX.X), reads=[b], writes=[b])
        P.op("dve", lambda e, s1=s1: e.reciprocal(s1, s1), reads=[b], writes=[b])
        P.op("dve", lambda e, ex=ex, s1=s1, sc_v=sc_v: e.tensor_scalar(sc_v, ex, s1, None, ALU.mult),
             reads=[b], writes=[b])
        P.op("dve", lambda e, sc_v=sc_v, m1=m1: e.reduce_max(m1, sc_v, axis=AX.X), reads=[b], writes=[b])
        P.op("dve", lambda e, sc_v=sc_v, m1=m1, msk=msk: e.tensor_scalar(msk, sc_v, m1, None, ALU.is_equal),
             reads=[b], writes=[b])
        P.op("dve", lambda e, msk=msk, sc_v=sc_v: e.scalar_tensor_tensor(msk, msk, -1e30, sc_v, ALU.mult, ALU.add),
             reads=[b], writes=[b])
        P.op("dve", lambda e, msk=msk, m2=m2: e.reduce_max(m2, msk, axis=AX.X), reads=[b], writes=[b])
        P.op("dve", lambda e, sc_v=sc_v, m2=m2, ge=ge: e.tensor_scalar(ge, sc_v, m2, None, ALU.is_ge),
             reads=[b], writes=[b])
        P.op("dve", lambda e, sc_v=sc_v, ge=ge, cv=cv: e.tensor_mul(cv, sc_v, ge),
             reads=[b], writes=[comb.buf], tag=f"rt_comb{tag}{tt}")
        PS.free(bank)
    A.free(rsc)

    y = A.alloc(f"y{tag}", NTT * 2048 * 4)
    y_ap = y.view(F32, (128, NTT, 2048))
    P.op("dve", lambda e: e.memset(y_ap[:, :, :].rearrange("p a b -> p (a b)"), 0.0),
         writes=[y.buf], tag=f"y0{tag}")

    if pre is not None:
        wgb, wub, n_pre = pre
    else:
        wgb = [A.alloc(f"wg{tag}{i}", NCC * 128 * 2) for i in range(3)]
        wub = [A.alloc(f"wu{tag}{i}", NCC * 128 * 2) for i in range(3)]
        n_pre = 0
    hb = A.alloc(f"h{tag}", NI * TCORE * 2)
    sg = A.alloc(f"sg{tag}", 4 * 512 * 2)

    def expert(n_ib, wg_src, wu_src, wd_src, comb_scalar, h_r, su):
        h_ap = h_r.view(BF16, (128, n_ib, TCORE))
        wdb = [A.alloc(f"wd{tag}{su}{i}", 2048 * 2) for i in range(n_ib)]
        for ib in range(n_ib):
            wg_r = wgb[ib % 3]
            wu_r = wub[ib % 3]
            dstg = wg_r.view(BF16, (128, NCC, 128))
            dstu = wu_r.view(BF16, (128, NCC, 128))
            if not (su == "e0" and ib < n_pre):
                ap_g, bg = wg_src(ib)
                ap_u, bu = wu_src(ib)
                P.dma("pool", lambda e, o=dstg, i=ap_g: e.dma_start(out=o, in_=i),
                      reads=[bg], writes=[wg_r.buf], tag=f"ldwg{tag}{su}{ib}")
                P.dma("pool", lambda e, o=dstu, i=ap_u: e.dma_start(out=o, in_=i),
                      reads=[bu], writes=[wu_r.buf], tag=f"ldwu{tag}{su}{ib}")
            psg = [PS.alloc(f"g{su}{ib}{tc}") for tc in range(2)]
            psu = [PS.alloc(f"u{su}{ib}{tc}") for tc in range(2)]

            def em_gu(e, dstg=dstg, dstu=dstu, psg=psg, psu=psu):
                first = None
                for cc in range(NCC):
                    for tc in range(2):
                        i0 = e.matmul(psg[tc].view(F32, (128, 512)), dstg[:, cc, :],
                                      xn_fm[:, cc, tc * 512:(tc + 1) * 512],
                                      start=(cc == 0), stop=(cc == NCC - 1))
                        first = first or i0
                    for tc in range(2):
                        last = e.matmul(psu[tc].view(F32, (128, 512)), dstu[:, cc, :],
                                        xn_fm[:, cc, tc * 512:(tc + 1) * 512],
                                        start=(cc == 0), stop=(cc == NCC - 1))
                return (first, last)
            P.op("pe", em_gu, reads=[wg_r.buf, wu_r.buf, xn_fm_r.buf],
                 writes=[r.buf for r in psg + psu], tag=f"gu_mm{tag}{su}{ib}")
            for tc in range(2):
                sg_ap = sg.view(BF16, (128, 4, 512))[:, (ib * 2 + tc) % 4, :]
                P.op("act", lambda e, p=psg[tc].view(F32, (128, 512)), o=sg_ap:
                     e.activation(o, p, AF.Silu),
                     reads=[psg[tc].buf], writes=[sg.buf], tag=f"silu{tag}{su}{ib}{tc}")
                hdst = h_ap[:, ib, tc * 512:(tc + 1) * 512]
                P.op("dve", lambda e, p=psu[tc].view(F32, (128, 512)), s=sg_ap, o=hdst:
                     e.tensor_mul(o, p, s),
                     reads=[psu[tc].buf, sg.buf], writes=[h_r.buf], tag=f"hmul{tag}{su}{ib}{tc}")
            for r in psg + psu:
                PS.free(r)
        for ib in range(n_ib):
            wd_r = wdb[ib]
            wd_ap = wd_r.view(BF16, (128, 2048))
            ap_d, bd = wd_src(ib)
            P.dma("pool", lambda e, o=wd_ap, i=ap_d: e.dma_start(out=o, in_=i),
                  reads=[bd], writes=[wd_r.buf], tag=f"ldwd{tag}{su}{ib}")
        for tt in range(NTT):
            banks = [PS.alloc(f"eo{su}{tt}{hc}") for hc in range(4)]
            pss = [b.view(F32, (128, 512)) for b in banks]

            def em_dn(e, pss=pss, tt=tt, n_ib=n_ib, h_ap=h_ap):
                first = None
                for ib in range(n_ib):
                    wd_ap = wdb[ib].view(BF16, (128, 2048))
                    for hc in range(4):
                        last = e.matmul(pss[hc], h_ap[:, ib, tt * 128:(tt + 1) * 128],
                                        wd_ap[:, hc * 512:(hc + 1) * 512],
                                        start=(ib == 0), stop=(ib == n_ib - 1))
                        first = first or last
                return (first, last)
            P.op("pe", em_dn, reads=[h_r.buf] + [wdb[ib].buf for ib in range(n_ib)],
                 writes=[b.buf for b in banks], tag=f"dn_mm{tag}{su}{tt}")
            for hc in range(4):
                yv = y_ap[:, tt, hc * 512:(hc + 1) * 512]
                if comb_scalar is not None:
                    cs = comb_scalar(tt)
                    P.op("dve", lambda e, ps=pss[hc], yv=yv, cs=cs:
                         e.scalar_tensor_tensor(yv, ps, cs, yv, ALU.mult, ALU.add),
                         reads=[banks[hc].buf, comb.buf, y.buf], writes=[y.buf],
                         tag=f"dn_acc{tag}{su}{tt}{hc}")
                else:
                    P.op("dve", lambda e, ps=pss[hc], yv=yv: e.tensor_tensor(yv, ps, yv, ALU.add),
                         reads=[banks[hc].buf, y.buf], writes=[y.buf],
                         tag=f"dn_acc{tag}{su}{tt}{hc}")
            for b in banks:
                PS.free(b)
        for r in wdb:
            A.free(r)

    wgl, wgl_b = w["wg"]
    wul, wul_b = w["wu"]
    wdl, wdl_b = w["wd"]
    for e_i in range(NE):
        expert(NI,
               lambda ib, e_i=e_i: (wgl[e_i, ib], wgl_b),
               lambda ib, e_i=e_i: (wul[e_i, ib], wul_b),
               lambda ib, e_i=e_i: (wdl[e_i, ib], wdl_b),
               lambda tt, e_i=e_i: comb_ap[:, tt, e_i:e_i + 1],
               hb, su=f"e{e_i}")
    sgl, sgl_b = w["swg"]
    sul, sul_b = w["swu"]
    sdl, sdl_b = w["swd"]
    for k in range(2):
        expert(NI,
               lambda ib, k=k: (sgl[k * NI + ib], sgl_b),
               lambda ib, k=k: (sul[k * NI + ib], sul_b),
               lambda ib, k=k: (sdl[k * NI + ib], sdl_b),
               None, hb, su=f"sh{k}")
    for r in [hb] + wgb + wub + [sg, comb]:
        A.free(r)

    xs = [A.alloc(f"rx{tag}{i}", 2048 * 4) for i in range(2)]
    for tt in range(NTT):
        xb = xs[tt % 2]
        x_ap = xb.view(F32, (128, 2048))
        P.dma("sp", lambda e, o=x_ap, i=x_dram_ap[tt * 128:(tt + 1) * 128, :]:
              e.dma_start(out=o, in_=i), reads=[tile_buf(x_dram_buf, tt)],
              writes=[xb.buf], tag=f"res_ldx{tag}{tt}")
        P.op("dve", lambda e, x_ap=x_ap, yv=y_ap[:, tt, :]: e.tensor_tensor(x_ap, x_ap, yv, ALU.add),
             reads=[xb.buf, y.buf], writes=[xb.buf], tag=f"res_add{tag}{tt}")
        P.dma("sp", lambda e, o=out_dram_ap[tt * 128:(tt + 1) * 128, :], i=x_ap:
              e.dma_start(out=o, in_=i), reads=[xb.buf],
              writes=[tile_buf(out_dram_buf, tt)], tag=f"res_st{tag}{tt}")
    for r in xs + [y]:
        A.free(r)


def ffn_weight_inputs(C):
    wg, wg_b = C.dram_in("wgl", [NE, NI, 128, NCC, 128])
    wu, wu_b = C.dram_in("wul", [NE, NI, 128, NCC, 128])
    wd, wd_b = C.dram_in("wdl", [NE, NI, 128, 2048])
    sg_, sg_b = C.dram_in("swgl", [NIS, 128, NCC, 128])
    su_, su_b = C.dram_in("swul", [NIS, 128, NCC, 128])
    sd_, sd_b = C.dram_in("swdl", [NIS, 128, 2048])
    return {
        "wg": (wg, wg_b), "wu": (wu, wu_b), "wd": (wd, wd_b),
        "swg": (sg_, sg_b), "swu": (su_, su_b), "swd": (sd_, sd_b),
    }


def ffn_weight_arrays(ins, pfx="L0_ffn_"):
    w_gate = ins[pfx + "w_gate"]; w_up = ins[pfx + "w_up"]; w_down = ins[pfx + "w_down"]
    swg = ins[pfx + "sw_gate"]; swu = ins[pfx + "sw_up"]; swd = ins[pfx + "sw_down"]
    out = {}
    out["wgl"] = np.ascontiguousarray(w_gate.reshape(NE, NCC, 128, NI, 128).transpose(0, 3, 2, 1, 4))
    out["wul"] = np.ascontiguousarray(w_up.reshape(NE, NCC, 128, NI, 128).transpose(0, 3, 2, 1, 4))
    out["wdl"] = np.ascontiguousarray(w_down.reshape(NE, NI, 128, 2048))
    out["swgl"] = np.ascontiguousarray(swg.reshape(NCC, 128, NIS, 128).transpose(2, 1, 0, 3))
    out["swul"] = np.ascontiguousarray(swu.reshape(NCC, 128, NIS, 128).transpose(2, 1, 0, 3))
    out["swdl"] = np.ascontiguousarray(swd.reshape(NIS, 128, 2048))
    out["gate_wT"] = np.ascontiguousarray(ins[pfx + "gate_w"].T.reshape(NCC, 128, NE).transpose(1, 0, 2))
    return out


NH_CORE = 2
SEQ = 8192
NCH = SEQ // 512


def build_attn_a(C, layer, xn_fm_r, x_dram_ap, x_dram_buf, out_dram_ap, out_dram_buf,
                 core_grp, tag="a"):
    P, A, PS = C.P, C.A, C.PS
    nc = C.nc
    ident = C.const["ident"][1]
    ident_b = C.const["ident"][0].buf
    ones_col = C.const["ones_col"][1]
    ones_col_b = C.const["ones_col"][0].buf
    onesf = C.const["onesf"][1]
    onesf_b = C.const["onesf"][0].buf
    masksa = C.const["masks_a"][1]
    masksa_b = C.const["masks_a"][0].buf

    SC1 = 0.08838834764831845

    HGW = NCC * TCORE // 2
    agi = [C.dram_tmp(f"agin{tag}{h}", [128, HGW], BF16) for h in range(2)]
    ago = [C.dram_tmp(f"agout{tag}{h}", [128 * 8, HGW], BF16, addr_space="Shared")
           for h in range(2)]
    a2_in, a2_in_b = C.dram_tmp(f"a2in{tag}", [SEQ, NH_CORE * 128], BF16)
    a2_out, a2_out_b = C.dram_tmp(f"a2out{tag}", [8 * SEQ, NH_CORE * 128], BF16,
                                  addr_space="Shared")
    agov = [ago[h][0].rearrange("(r p) (cc t) -> r p cc t", p=128, t=TCORE)
            for h in range(2)]

    xn_v = xn_fm_r.view(BF16, (128, NCC, TCORE))
    for tt in range(NTT):
        for h in range(2):
            ov = agi[h][0].rearrange("p (cc t) -> p cc t", t=TCORE)[:, :, tt * 128:(tt + 1) * 128]
            iv = xn_v[:, h * 8:(h + 1) * 8, tt * 128:(tt + 1) * 128]
            o_c = P.dma("sp", lambda e, o=ov, i=iv: e.dma_start(out=o, in_=i),
                        reads=[xn_fm_r.tile_bufs[tt]], writes=[agi[h][1]],
                        tag=f"ctr{tag}{h}{tt}")
            reg_reader(xn_fm_r.buf, o_c)
    for h in range(2):
        P.collective(lambda e, h=h: e.collective_compute(
            "AllGather", ALU.bypass, replica_groups=[core_grp],
            ins=[agi[h][0].opt()], outs=[ago[h][0].opt()]),
            reads=[agi[h][1]], writes=[ago[h][1]], tag=f"ag{tag}{h}")
    A.free(xn_fm_r)

    qT, qT_b = layer["qT"]; kT, kT_b = layer["kT"]; vT, vT_b = layer["vT"]
    wq = A.alloc(f"wq{tag}", NCC * 256 * 2)
    wk = A.alloc(f"wk{tag}", NCC * 256 * 2)
    wv = A.alloc(f"wv{tag}", NCC * 256 * 2)
    for (r, src, sb) in ((wq, qT, qT_b), (wk, kT, kT_b), (wv, vT, vT_b)):
        P.dma("pool", lambda e, o=r.view(BF16, (128, NCC, 256)), i=src: e.dma_start(out=o, in_=i),
              reads=[sb], writes=[r.buf], tag=f"ldw{tag}")
    wq_ap = wq.view(BF16, (128, NCC, 256))
    wk_ap = wk.view(BF16, (128, NCC, 256))
    wv_ap = wv.view(BF16, (128, NCC, 256))

    kfm = A.alloc(f"kfm{tag}", NH_CORE * SEQ * 2)
    vtm = A.alloc(f"vtm{tag}", (SEQ // 128) * 256 * 2)
    kfm_ap = kfm.view(BF16, (128, NH_CORE, SEQ))
    vtm_ap = vtm.view(BF16, (128, SEQ // 128, 256))
    xnc = [[A.alloc(f"xnc{tag}{i}{h}", NCC * 256 * 2) for h in range(2)] for i in range(3)]

    def load_chunk(ch):
        bs = xnc[ch % 3]
        r, t0 = ch // 2, (ch % 2) * 512
        aps = []
        for h in range(2):
            ap = bs[h].view(BF16, (128, NCC // 2, 512))
            P.dma("sp", lambda e, o=ap, i=agov[h][r, :, :, t0:t0 + 512], h=h:
                  e.dma_start(out=o, in_=i),
                  reads=[ago[h][1]], writes=[bs[h].buf], tag=f"ldxn{tag}{ch}{h}")
            aps.append(ap)
        return bs, aps

    _kvcache = {}

    def get_chunk(ch):
        if ch not in _kvcache:
            _kvcache[ch] = load_chunk(ch)
        return _kvcache[ch]

    for ch in range(NCH):
        bs, aps = get_chunk(ch)
        if ch + 1 < NCH:
            get_chunk(ch + 1)
        for fb in range(NH_CORE):
            bank = PS.alloc(f"k{tag}{ch}{fb}")
            ps = bank.view(F32, (128, 512))
            for h in range(2):
                def em_k(e, ps=ps, xap=aps[h], fb=fb, h=h):
                    first = None
                    for c8 in range(8):
                        cc = h * 8 + c8
                        last = e.matmul(ps, wk_ap[:, cc, fb * 128:(fb + 1) * 128],
                                        xap[:, c8, :],
                                        start=(cc == 0), stop=(cc == NCC - 1))
                        first = first or last
                    return (first, last)
                P.op("pe", em_k, reads=[wk.buf, bs[h].buf], writes=[bank.buf],
                     tag=f"kmm{tag}{ch}{fb}{h}")
            dst = kfm_ap[:, fb, ch * 512:(ch + 1) * 512]
            P.op("act", lambda e, ps=ps, dst=dst: e.copy(dst, ps),
                 reads=[bank.buf], writes=[kfm.buf], tag=f"kev{tag}{ch}{fb}")
            PS.free(bank)
        for tb in range(4):
            bank = PS.alloc(f"v{tag}{ch}{tb}")
            ps = bank.view(F32, (128, 256))
            for h in range(2):
                def em_v(e, ps=ps, xap=aps[h], tb=tb, h=h):
                    first = None
                    for c8 in range(8):
                        cc = h * 8 + c8
                        last = e.matmul(ps, xap[:, c8, tb * 128:(tb + 1) * 128], wv_ap[:, cc, :],
                                        start=(cc == 0), stop=(cc == NCC - 1))
                        first = first or last
                    return (first, last)
                P.op("pe", em_v, reads=[wv.buf, bs[h].buf], writes=[bank.buf],
                     tag=f"vmm{tag}{ch}{tb}{h}")
            dst = vtm_ap[:, ch * 4 + tb, :]
            P.op("dve", lambda e, ps=ps, dst=dst: e.tensor_copy(dst, ps),
                 reads=[bank.buf], writes=[vtm.buf], tag=f"vev{tag}{ch}{tb}")
            PS.free(bank)

    qsb = A.alloc(f"q{tag}", NH_CORE * 512 * 2)
    q_ap = qsb.view(BF16, (128, NH_CORE, 512))
    psb = [A.alloc(f"p{tag}{i}", 512 * 2) for i in range(6)]
    osb = A.alloc(f"o{tag}", 512 * 2)
    lsb = A.alloc(f"l{tag}", 512 * 4)
    lcb = A.alloc(f"lc{tag}", 4 * 4)
    zsb = A.alloc(f"z{tag}", 4 * 256 * 2)
    z_ap = zsb.view(BF16, (128, 4, 256))

    _flcache = {}

    def get_chunk_f(qc):
        if qc not in _flcache:
            _flcache[qc] = load_chunk(qc)
        return _flcache[qc]

    for qc in range(NCH):
        bs, aps = get_chunk_f(qc)
        if qc + 1 < NCH:
            get_chunk_f(qc + 1)
        for fb in range(NH_CORE):
            bank = PS.alloc(f"q{tag}{qc}{fb}")
            ps = bank.view(F32, (128, 512))
            for h in range(2):
                def em_q(e, ps=ps, xap=aps[h], fb=fb, h=h):
                    first = None
                    for c8 in range(8):
                        cc = h * 8 + c8
                        last = e.matmul(ps, wq_ap[:, cc, fb * 128:(fb + 1) * 128],
                                        xap[:, c8, :],
                                        start=(cc == 0), stop=(cc == NCC - 1))
                        first = first or last
                    return (first, last)
                P.op("pe", em_q, reads=[wq.buf, bs[h].buf], writes=[bank.buf],
                     tag=f"qmm{tag}{qc}{fb}{h}")
            P.op("act", lambda e, ps=ps, dst=q_ap[:, fb, :]: e.copy(dst, ps),
                 reads=[bank.buf], writes=[qsb.buf], tag=f"qev{tag}{qc}{fb}")
            PS.free(bank)
        nkb = 4 * qc + 4
        for h in range(NH_CORE):
            obank = PS.alloc(f"o{tag}{qc}{h}")
            lbank = PS.alloc(f"l{tag}{qc}{h}")
            ops_v = obank.view(F32, (128, 512))
            lps_v = lbank.view(F32, (1, 512))
            sbanks = [PS.alloc(f"s{tag}{qc}{h}{i}") for i in range(6)]
            G = 3
            groups = [list(range(g, min(g + G, nkb))) for g in range(0, nkb, G)]

            def s_grp(gi, h=h, qc=qc, sbanks=sbanks, groups=groups):
                ks = groups[gi]

                def em(e, ks=ks, h=h):
                    first = None
                    for kb in ks:
                        sv = sbanks[kb % 6].view(F32, (128, 512))
                        last = e.matmul(sv, kfm_ap[:, h, kb * 128:(kb + 1) * 128],
                                        q_ap[:, h, :], start=True, stop=True)
                        first = first or last
                    return (first, last)
                P.op("pe", em, reads=[kfm.buf, qsb.buf],
                     writes=[sbanks[kb % 6].buf for kb in ks],
                     tag=f"sgrp{tag}{qc}{h}{gi}")
            s_grp(0)
            if len(groups) > 1:
                s_grp(1)
            for gi, ks in enumerate(groups):
                def em_exp(e, ks=ks, sbanks=sbanks):
                    first = None
                    for kb in ks:
                        sv = sbanks[kb % 6].view(F32, (128, 512))
                        p_ap = psb[kb % 6].view(BF16, (128, 512))
                        last = e.activation(p_ap, sv, AF.Exp, scale=SC1)
                        first = first or last
                    return (first, last)
                P.op("act", em_exp,
                     reads=[sbanks[kb % 6].buf for kb in ks],
                     writes=[psb[kb % 6].buf for kb in ks], tag=f"egrp{tag}{qc}{h}{gi}")
                mks = [kb for kb in ks if kb >= 4 * qc]
                if mks:
                    def em_msk(e, mks=mks, qc=qc):
                        first = None
                        for kb in mks:
                            p_ap = psb[kb % 6].view(BF16, (128, 512))
                            last = e.tensor_mul(p_ap, p_ap, masksa[:, kb - 4 * qc, :])
                            first = first or last
                        return (first, last)
                    P.op("dve", em_msk,
                         reads=[psb[kb % 6].buf for kb in mks] + [masksa_b],
                         writes=[psb[kb % 6].buf for kb in mks],
                         tag=f"mgrp{tag}{qc}{h}{gi}")
                if gi + 2 < len(groups):
                    s_grp(gi + 2)

                def em_pvl(e, ks=ks, h=h, nkb=nkb, ops_v=ops_v, lps_v=lps_v):
                    first = None
                    for kb in ks:
                        p_ap = psb[kb % 6].view(BF16, (128, 512))
                        i0 = e.matmul(ops_v, vtm_ap[:, kb, h * 128:(h + 1) * 128], p_ap,
                                      start=(kb == 0), stop=(kb == nkb - 1))
                        first = first or i0
                        last = e.matmul(lps_v, ones_col, p_ap,
                                        start=(kb == 0), stop=(kb == nkb - 1))
                    return (first, last)
                P.op("pe", em_pvl,
                     reads=[vtm.buf, ones_col_b] + [psb[kb % 6].buf for kb in ks],
                     writes=[obank.buf, lbank.buf], tag=f"pvlg{tag}{qc}{h}{gi}")
            for r in sbanks:
                PS.free(r)
            o_ap = osb.view(BF16, (128, 512))
            l_ap = lsb.view(F32, (1, 512))
            P.op("act", lambda e, o_ap=o_ap, ops_v=ops_v: e.copy(o_ap, ops_v),
                 reads=[obank.buf], writes=[osb.buf], tag=f"oev{tag}{qc}{h}")
            P.op("act", lambda e, l_ap=l_ap, lps_v=lps_v: e.copy(l_ap, lps_v),
                 reads=[lbank.buf], writes=[lsb.buf], tag=f"lev{tag}{qc}{h}")
            lcbank = PS.alloc(f"lc{tag}{qc}{h}")
            lc_ps = lcbank.view(F32, (128, 4))

            def em_lc(e, l_ap=l_ap, lc_ps=lc_ps):
                first = None
                for j in range(4):
                    last = e.matmul(lc_ps[:, j:j + 1], l_ap[0:1, j * 128:(j + 1) * 128],
                                    onesf, start=True, stop=True)
                    first = first or last
                return (first, last)
            P.op("pe", em_lc, reads=[lsb.buf, onesf_b], writes=[lcbank.buf],
                 tag=f"lc{tag}{qc}{h}")
            lc_ap = lcb.view(F32, (128, 4))
            P.op("dve", lambda e, lc_ap=lc_ap, lc_ps=lc_ps: e.reciprocal(lc_ap, lc_ps),
                 reads=[lcbank.buf], writes=[lcb.buf], tag=f"lcr{tag}{qc}{h}")
            PS.free(lcbank)
            ztbank = PS.alloc(f"zt{tag}{qc}{h}")
            zt_ps = ztbank.view(BF16, (128, 4, 128))

            def em_zt(e, o_ap=o_ap, zt_ps=zt_ps):
                first = None
                for j in range(4):
                    last = e.transpose(zt_ps[:, j, :], o_ap[:, j * 128:(j + 1) * 128], ident)
                    first = first or last
                return (first, last)
            P.op("pe", em_zt, reads=[osb.buf, ident_b], writes=[ztbank.buf],
                 tag=f"zt{tag}{qc}{h}")

            def em_zn(e, zt_ps=zt_ps, lc_ap=lc_ap, h=h):
                first = None
                for j in range(4):
                    last = e.tensor_scalar(z_ap[:, j, h * 128:(h + 1) * 128], zt_ps[:, j, :],
                                           lc_ap[:, j:j + 1], None, ALU.mult)
                    first = first or last
                return (first, last)
            P.op("dve", em_zn, reads=[ztbank.buf, lcb.buf], writes=[zsb.buf],
                 tag=f"zn{tag}{qc}{h}")
            PS.free(ztbank)
            for r in [obank, lbank]:
                PS.free(r)
        a2v = a2_in.rearrange("(qc tb p) f -> qc p tb f", tb=4, p=128)
        P.dma("sp", lambda e, o=a2v[qc], i=z_ap: e.dma_start(out=o, in_=i),
              reads=[zsb.buf], writes=[a2_in_b], tag=f"zctr{tag}{qc}")

    for r in [kfm, vtm, qsb, osb, lsb, lcb, zsb] + psb + [x for pair in xnc for x in pair] + [wq, wk, wv]:
        A.free(r)

    oT_ap_l, oT_b_l = layer["oT"]
    wo_pre = [A.alloc(f"wo{tag}{i}", NCC * 512 * 2) for i in range(2)]
    for oc in range(2):
        wap = wo_pre[oc].view(BF16, (128, NCC, 512))
        P.dma("pool", lambda e, o=wap, i=oT_ap_l[oc]: e.dma_start(out=o, in_=i),
              reads=[oT_b_l], writes=[wo_pre[oc].buf], tag=f"ldwo{tag}{oc}")
    if not hasattr(C, "wo_pre"):
        C.wo_pre = {}
    C.wo_pre[tag] = wo_pre

    P.collective(lambda e: e.collective_compute(
        "AllGather", ALU.bypass, replica_groups=[core_grp],
        ins=[a2_in.opt()], outs=[a2_out.opt()]),
        reads=[a2_in_b], writes=[a2_out_b], tag=f"zag{tag}")

    build_outproj(C, layer["oT"], a2_out, a2_out_b, x_dram_ap, x_dram_buf,
                  out_dram_ap, out_dram_buf, tag=tag)


def build_outproj(C, oT_src, z_dram, z_dram_b, x_dram_ap, x_dram_buf,
                  out_dram_ap, out_dram_buf, tag):
    P, A, PS = C.P, C.A, C.PS
    ident = C.const["ident"][1]
    ident_b = C.const["ident"][0].buf
    oT_ap, oT_b = oT_src

    nc = C.nc
    wo_pre = getattr(C, "wo_pre", {}).pop(tag, None)
    zown = A.alloc(f"zow{tag}", NTT * 8 * 256 * 2)
    zown_ap = zown.view(BF16, (128, NTT, 8, 256))
    zfm = A.alloc(f"zfm{tag}", NCC * TCORE * 2)
    zfm_ap = zfm.view(BF16, (128, NCC, TCORE))
    zv = z_dram.rearrange("(r c tt p) f -> r c tt p f", c=8, tt=NTT, p=128)

    def em_ldz(e, sem):
        import concourse.mybir as _mb
        pid = nc.partition_id([_mb.EngineType.SP])
        for tt in range(NTT):
            iv = zv[:, bass_ts(pid, 1), tt].rearrange("r one p f -> p (r one) f")
            e.dma_start(out=zown_ap[:, tt, :, :], in_=iv).then_inc(sem, 16)
    P.dma_multi("sp", em_ldz, NTT, reads=[z_dram_b], writes=[zown.buf],
                tag=f"ldz{tag}")
    for tt in range(NTT):
        zt_flat = zown_ap[:, tt, :, :].rearrange("p r f -> p (r f)")
        for g in range(4):
            bank = PS.alloc(f"zT{tag}{tt}{g}")
            bap = bank.view(BF16, (128, 4, 128))

            def em_tr(e, bap=bap, g=g, zt_flat=zt_flat):
                first = None
                for j in range(4):
                    cc = g * 4 + j
                    last = e.transpose(bap[:, j, :], zt_flat[:, cc * 128:(cc + 1) * 128], ident)
                    first = first or last
                return (first, last)
            P.op("pe", em_tr, reads=[zown.buf, ident_b], writes=[bank.buf],
                 tag=f"ztr{tag}{tt}{g}")
            dst = zfm_ap[:, g * 4:(g + 1) * 4, tt * 128:(tt + 1) * 128]
            P.op("dve", lambda e, bap=bap, dst=dst: e.tensor_copy(dst, bap),
                 reads=[bank.buf], writes=[zfm.buf], tag=f"ztrc{tag}{tt}{g}")
            PS.free(bank)
    A.free(zown)

    xacc = A.alloc(f"xac{tag}", NTT * 2048 * 4)
    xacc_ap = xacc.view(F32, (128, NTT, 2048))
    for tt in range(NTT):
        P.dma("sp", lambda e, o=xacc_ap[:, tt, :], i=x_dram_ap[tt * 128:(tt + 1) * 128, :]:
              e.dma_start(out=o, in_=i), reads=[tile_buf(x_dram_buf, tt)], writes=[xacc.buf],
              tag=f"opx{tag}{tt}")
    if wo_pre is not None:
        wo = wo_pre
    else:
        wo = [A.alloc(f"wo{tag}{i}", NCC * 512 * 2) for i in range(2)]
    for oc in range(4):
        wr = wo[oc % 2]
        wap = wr.view(BF16, (128, NCC, 512))
        if wo_pre is None or oc >= 2:
            P.dma("pool", lambda e, o=wap, i=oT_ap[oc]: e.dma_start(out=o, in_=i),
                  reads=[oT_b], writes=[wr.buf], tag=f"ldwo{tag}{oc}")
        for tt in range(NTT):
            bank = PS.alloc(f"op{tag}{oc}{tt}")
            ps = bank.view(F32, (128, 512))

            def em_op(e, ps=ps, wap=wap, tt=tt):
                first = None
                for cc in range(NCC):
                    last = e.matmul(ps, zfm_ap[:, cc, tt * 128:(tt + 1) * 128], wap[:, cc, :],
                                    start=(cc == 0), stop=(cc == NCC - 1))
                    first = first or last
                return (first, last)
            P.op("pe", em_op, reads=[zfm.buf, wr.buf], writes=[bank.buf],
                 tag=f"opmm{tag}{oc}{tt}")
            dst = xacc_ap[:, tt, oc * 512:(oc + 1) * 512]
            P.op("dve", lambda e, ps=ps, dst=dst: e.tensor_tensor(dst, dst, ps, ALU.add),
                 reads=[bank.buf, xacc.buf], writes=[xacc.buf], tag=f"opacc{tag}{oc}{tt}")
            PS.free(bank)
    for tt in range(NTT):
        P.dma("sp", lambda e, o=out_dram_ap[tt * 128:(tt + 1) * 128, :], i=xacc_ap[:, tt, :]:
              e.dma_start(out=o, in_=i), reads=[xacc.buf], writes=[tile_buf(out_dram_buf, tt)],
              tag=f"opst{tag}{tt}")
    for r in [zfm, xacc] + wo:
        A.free(r)


def build_attn_b(C, layer, xn_fm_r, x_dram_ap, x_dram_buf, out_dram_ap, out_dram_buf,
                 core_grp, tag="b"):
    P, A, PS = C.P, C.A, C.PS
    nc = C.nc
    ident = C.const["ident"][1]
    ident_b = C.const["ident"][0].buf
    ones_col = C.const["ones_col"][1]
    ones_col_b = C.const["ones_col"][0].buf
    onesf = C.const["onesf"][1]
    onesf_b = C.const["onesf"][0].buf
    masksb = C.const["masks_b"][1]
    masksb_b = C.const["masks_b"][0].buf
    SC1 = 0.08838834764831845
    SC2 = 0.17677669529663687

    GW = NCC * TCORE
    HGW = GW // 2
    agi = [C.dram_tmp(f"agin{tag}{h}", [128, HGW + 8 * h], BF16) for h in range(2)]
    ago = [C.dram_tmp(f"agout{tag}{h}", [128 * 8, HGW + 8 * h], BF16,
                      addr_space="Shared") for h in range(2)]
    z_in, z_in_b = C.dram_tmp(f"zin{tag}", [SEQ, NH_CORE * 128], BF16)
    z_out, z_out_b = C.dram_tmp(f"zout{tag}", [8 * SEQ, NH_CORE * 128], BF16,
                                  addr_space="Shared")
    agov = [ago[h][0].rearrange("(r p) g -> r p g", p=128) for h in range(2)]

    def ldconst(nm, src_key, shape):
        sap, sb = layer[src_key]
        r = A.alloc(f"{nm}{tag}", int(np.prod(shape[1:])) * 2)
        ap = r.view(BF16, shape)
        P.dma("pool", lambda e, o=ap, i=sap: e.dma_start(out=o, in_=i),
              reads=[sb], writes=[r.buf], tag=f"ld{nm}{tag}")
        return r, ap

    wq_r, wq_ap = ldconst("wq", "qT", (128, NCC, 256))
    wk_r, wk_ap = ldconst("wk", "kT", (128, NCC, 256))
    wv_r, wv_ap = ldconst("wv", "vT", (128, NCC, 256))
    vn_r, vn_ap = ldconst("vn", "vNat", (128, NH_CORE, NCC, 128))
    pk_r, pk_ap = ldconst("pk", "pkT", (128, 64))
    pv_r, pv_ap = ldconst("pv", "pvT", (128, 64))
    uo_r, uo_ap = ldconst("uo", "uoT", (64, 128))
    w1_r, w1_ap = ldconst("w1", "w1T", (128, NCC, 512))
    w2_r, w2_ap = ldconst("w2", "w2T", (128, 4, 1))

    xn_fm = xn_fm_r.view(BF16, (128, NCC, TCORE))
    hi_sb = A.alloc(f"hi{tag}", 512 * 2)
    hifm = A.alloc(f"hifm{tag}", 4 * 128 * 2)
    grow = A.alloc(f"grow{tag}", TCORE * 4)
    gcol = A.alloc(f"gcol{tag}", 8 * 2)
    grow_ap = grow.view(F32, (1, TCORE))
    gp_banks = [PS.alloc(f"gp{tag}{i}") for i in range(2)]
    for tt in range(NTT):
        bank = PS.alloc(f"hi{tag}{tt}")
        ps = bank.view(F32, (128, 512))

        def em_hi(e, ps=ps, tt=tt):
            first = None
            for cc in range(NCC):
                last = e.matmul(ps, xn_fm[:, cc, tt * 128:(tt + 1) * 128], w1_ap[:, cc, :],
                                start=(cc == 0), stop=(cc == NCC - 1))
                first = first or last
            return (first, last)
        P.op("pe", em_hi, reads=[xn_fm_r.buf, w1_r.buf], writes=[bank.buf], tag=f"hi_mm{tag}{tt}")
        hi_ap = hi_sb.view(BF16, (128, 512))
        P.op("act", lambda e, ps=ps, hi_ap=hi_ap: e.activation(hi_ap, ps, AF.Gelu),
             reads=[bank.buf], writes=[hi_sb.buf], tag=f"hi_gelu{tag}{tt}")
        PS.free(bank)
        trb = PS.alloc(f"hit{tag}{tt}")
        tr_ps = trb.view(BF16, (128, 4, 128))

        def em_htr(e, tr_ps=tr_ps, hi_ap=hi_ap):
            first = None
            for j in range(4):
                last = e.transpose(tr_ps[:, j, :], hi_ap[:, j * 128:(j + 1) * 128], ident)
                first = first or last
            return (first, last)
        P.op("pe", em_htr, reads=[hi_sb.buf, ident_b], writes=[trb.buf], tag=f"hi_tr{tag}{tt}")
        hifm_ap = hifm.view(BF16, (128, 4, 128))
        P.op("dve", lambda e, tr_ps=tr_ps, hifm_ap=hifm_ap: e.tensor_copy(hifm_ap, tr_ps),
             reads=[trb.buf], writes=[hifm.buf], tag=f"hi_trc{tag}{tt}")
        PS.free(trb)
        gpb = gp_banks[tt // 4]
        gp = gpb.view(F32, (1, 512))

        def em_sc(e, gp=gp, hifm_ap=hifm_ap, tt=tt):
            first = None
            for j in range(4):
                last = e.matmul(gp[:, (tt % 4) * 128:(tt % 4 + 1) * 128],
                                w2_ap[:, j, :], hifm_ap[:, j, :],
                                start=(j == 0), stop=(j == 3))
                first = first or last
            return (first, last)
        P.op("pe", em_sc, reads=[hifm.buf, w2_r.buf], writes=[gpb.buf], tag=f"hi_sc{tag}{tt}")
    for half in range(2):
        gp = gp_banks[half].view(F32, (1, 512))
        P.op("act", lambda e, gp=gp, o=grow_ap[:, half * 512:(half + 1) * 512]:
             e.activation(o, gp, AF.Sigmoid),
             reads=[gp_banks[half].buf], writes=[grow.buf], tag=f"gate_sig{tag}{half}")
        PS.free(gp_banks[half])
    P.op("dve", lambda e: e.tensor_single_scalar(grow_ap, grow_ap, 0.75, ALU.is_gt),
         reads=[grow.buf], writes=[grow.buf], tag=f"gate_gt{tag}")
    gcb = PS.alloc(f"gc{tag}")
    gc_ps = gcb.view(F32, (128, 8))

    def em_gc(e):
        first = None
        for tt in range(NTT):
            last = e.matmul(gc_ps[:, tt:tt + 1], grow_ap[0:1, tt * 128:(tt + 1) * 128],
                            onesf, start=True, stop=True)
            first = first or last
        return (first, last)
    P.op("pe", em_gc, reads=[grow.buf, onesf_b], writes=[gcb.buf], tag=f"gate_col{tag}")
    gcol_ap = gcol.view(BF16, (128, 8))
    P.op("dve", lambda e: e.tensor_copy(gcol_ap, gc_ps),
         reads=[gcb.buf], writes=[gcol.buf], tag=f"gate_colc{tag}")
    PS.free(gcb)
    for r in (hi_sb, hifm, grow):
        A.free(r)

    xn_v2 = xn_fm_r.view(BF16, (128, NCC, TCORE))
    for tt in range(NTT):
        for h in range(2):
            ov = agi[h][0][:, 0:HGW].rearrange("p (cc t) -> p cc t", t=TCORE)[
                :, :, tt * 128:(tt + 1) * 128]
            iv = xn_v2[:, h * 8:(h + 1) * 8, tt * 128:(tt + 1) * 128]
            o_c = P.dma("sp", lambda e, o=ov, i=iv: e.dma_start(out=o, in_=i),
                        reads=[xn_fm_r.tile_bufs[tt]], writes=[agi[h][1]],
                        tag=f"ctr{tag}{h}{tt}")
            reg_reader(xn_fm_r.buf, o_c)
    P.dma("sp", lambda e: e.dma_start(out=agi[1][0][:, HGW:HGW + 8], in_=gcol_ap),
          reads=[gcol.buf], writes=[agi[1][1]], tag=f"ctrg{tag}")
    for h in range(2):
        P.collective(lambda e, h=h: e.collective_compute(
            "AllGather", ALU.bypass, replica_groups=[core_grp],
            ins=[agi[h][0].opt()], outs=[ago[h][0].opt()]),
            reads=[agi[h][1]], writes=[ago[h][1]], tag=f"ag{tag}{h}")
    A.free(gcol)
    A.free(xn_fm_r)
    A.free(w1_r)
    A.free(w2_r)

    wvr = A.alloc(f"wvr{tag}", NH_CORE * NCC * 64 * 2)
    wvr_ap = wvr.view(BF16, (128, NH_CORE, NCC, 64))
    for h in range(NH_CORE):
        for g in range(4):
            bank = PS.alloc(f"wvr{tag}{h}{g}")
            bap = bank.view(F32, (128, 4, 64))

            def em_wvr(e, bap=bap, h=h, g=g):
                first = None
                for j in range(4):
                    cc = g * 4 + j
                    last = e.matmul(bap[:, j, :], vn_ap[:, h, cc, :], pv_ap,
                                    start=True, stop=True)
                    first = first or last
                return (first, last)
            P.op("pe", em_wvr, reads=[vn_r.buf, pv_r.buf], writes=[bank.buf],
                 tag=f"wvr_mm{tag}{h}{g}")
            dst = wvr_ap[:, h, g * 4:(g + 1) * 4, :]
            P.op("dve", lambda e, bap=bap, dst=dst: e.tensor_copy(dst, bap),
                 reads=[bank.buf], writes=[wvr.buf], tag=f"wvr_c{tag}{h}{g}")
            PS.free(bank)
    A.free(vn_r)
    A.free(pv_r)

    kfm = A.alloc(f"kfm{tag}", NH_CORE * SEQ * 2)
    vtm = A.alloc(f"vtm{tag}", (SEQ // 128) * 256 * 2)
    krf = A.alloc(f"krf{tag}", SEQ * 2)
    vrt = A.alloc(f"vrt{tag}", (SEQ // 128) * 130 * 2)
    kfm_ap = kfm.view(BF16, (128, NH_CORE, SEQ))
    vtm_ap = vtm.view(BF16, (128, SEQ // 128, 256))
    krf_ap = krf.view(BF16, (128, SEQ))
    vrt_ap = vrt.view(BF16, (128, SEQ // 128, 130))
    P.op("dve", lambda e: e.memset(vrt.view(BF16, (128, (SEQ // 128) * 130)), 1.0),
         writes=[vrt.buf], tag=f"vrt1{tag}")

    xnc = [[A.alloc(f"xnc{tag}{i}{h}", NCC * 256 * 2) for h in range(2)] for i in range(2)]
    ropeb = [A.alloc(f"rope{tag}{i}", 2 * 4 * 128 * 2) for i in range(3)]
    rsc = A.alloc(f"rsc{tag}", 3 * 256 * 4)
    rcos, rcos_b = layer["rope_cos"]
    rss, rss_b = layer["rope_ss"]

    def load_chunk(ch):
        bs = xnc[ch % 2]
        r, t0 = ch // 2, (ch % 2) * 512
        aps = []
        for h in range(2):
            ap = bs[h].view(BF16, (128, NCC // 2, 512))
            srcv = agov[h][r, :, 0:HGW].rearrange("p (cc t) -> p cc t", t=TCORE)[:, :, t0:t0 + 512]
            P.dma("sp", lambda e, o=ap, i=srcv: e.dma_start(out=o, in_=i),
                  reads=[ago[h][1]], writes=[bs[h].buf], tag=f"ldxn{tag}{ch}{h}")
            aps.append(ap)
        rb = ropeb[ch % 3]
        return bs, aps, rb

    def rope_tiles(rb):
        cos_ap = rb.view(BF16, (128, 2, 4, 128))
        return cos_ap[:, 0], cos_ap[:, 1]

    def load_rope(ch, rb):
        dst = rb.view(BF16, (128, 2, 4, 128))
        P.dma("sp", lambda e, o=dst[:, 0], i=rcos[ch]: e.dma_start(out=o, in_=i),
              reads=[rcos_b], writes=[rb.buf], tag=f"ldcos{tag}{ch}")
        P.dma("sp", lambda e, o=dst[:, 1], i=rss[ch]: e.dma_start(out=o, in_=i),
              reads=[rss_b], writes=[rb.buf], tag=f"ldss{tag}{ch}")

    def project_tm(aps, bufs, w_ap, w_buf, tb, n):
        bank = PS.alloc("ptm")
        ps = bank.view(F32, (128, n))
        for h in range(2):
            def em(e, ps=ps, xap=aps[h], tb=tb, h=h):
                first = None
                for c8 in range(8):
                    cc = h * 8 + c8
                    last = e.matmul(ps, xap[:, c8, tb * 128:(tb + 1) * 128], w_ap[:, cc, 0:n],
                                    start=(cc == 0), stop=(cc == NCC - 1))
                    first = first or last
                return (first, last)
            P.op("pe", em, reads=[bufs[h], w_buf], writes=[bank.buf], tag=f"ptm{h}")
        return bank, ps

    def rope_apply(ps_v, bank, cos_ap, ss_ap, rb, h, out_ap, out_buf):
        t1 = rsc.view(F32, (128, 256))[:, 0:128]
        qs = rsc.view(F32, (128, 256), byte_off=1024)[:, 0:128]
        kv = ps_v[:, h * 128:(h + 1) * 128]
        P.op("dve", lambda e: e.tensor_mul(t1, kv, cos_ap),
             reads=[bank.buf, rb.buf], writes=[rsc.buf], tag="rp1")

        def em_sw(e):
            i0 = e.tensor_copy(qs[:, 0::2], kv[:, 1::2])
            i1 = e.tensor_copy(qs[:, 1::2], kv[:, 0::2])
            return (i0, i1)
        P.op("dve", em_sw, reads=[bank.buf], writes=[rsc.buf], tag="rp2")
        P.op("dve", lambda e: e.tensor_mul(qs, qs, ss_ap),
             reads=[rsc.buf, rb.buf], writes=[rsc.buf], tag="rp3")
        P.op("dve", lambda e: e.tensor_add(out_ap, t1, qs),
             reads=[rsc.buf], writes=[out_buf], tag="rp4")

    ksc = A.alloc(f"ksc{tag}", 256 * 2)

    _kvc = {}

    def get_chunk(ch):
        if ch not in _kvc:
            bs, aps, rb = load_chunk(ch)
            load_rope(ch, rb)
            _kvc[ch] = (bs, aps, rb)
        return _kvc[ch]

    for ch in range(NCH):
        bs, aps, rb = get_chunk(ch)
        if ch + 1 < NCH:
            get_chunk(ch + 1)
        bufs = [bs[0].buf, bs[1].buf]
        cos_all, ss_all = rope_tiles(rb)
        for tb in range(4):
            kbank, kps = project_tm(aps, bufs, wk_ap, wk_r.buf, tb, 256)
            vbank, vps = project_tm(aps, bufs, wv_ap, wv_r.buf, tb, 256)
            P.op("dve", lambda e, vps=vps, dst=vtm_ap[:, ch * 4 + tb, :]:
                 e.tensor_copy(dst, vps),
                 reads=[vbank.buf], writes=[vtm.buf], tag=f"vev{tag}{ch}{tb}")
            PS.free(vbank)
            for h in range(NH_CORE):
                k_ap = ksc.view(BF16, (128, 256))[:, h * 128:(h + 1) * 128]
                rope_apply(kps, kbank, cos_all[:, tb, :], ss_all[:, tb, :], rb, h,
                           k_ap, ksc.buf)
            PS.free(kbank)
            trb2 = PS.alloc(f"ktr{tag}")
            tr_ps = trb2.view(BF16, (128, 2, 128))

            def em_ktr(e, tr_ps=tr_ps):
                i0 = e.transpose(tr_ps[:, 0, :], ksc.view(BF16, (128, 256))[:, 0:128], ident)
                i1 = e.transpose(tr_ps[:, 1, :], ksc.view(BF16, (128, 256))[:, 128:256], ident)
                return (i0, i1)
            P.op("pe", em_ktr, reads=[ksc.buf, ident_b], writes=[trb2.buf],
                 tag=f"ktr{tag}{ch}{tb}")

            def em_kc(e, tr_ps=tr_ps, ch=ch, tb=tb):
                i0 = e.tensor_copy(kfm_ap[:, 0, (ch * 4 + tb) * 128:(ch * 4 + tb + 1) * 128],
                                   tr_ps[:, 0, :])
                i1 = e.tensor_copy(kfm_ap[:, 1, (ch * 4 + tb) * 128:(ch * 4 + tb + 1) * 128],
                                   tr_ps[:, 1, :])
                return (i0, i1)
            P.op("dve", em_kc, reads=[trb2.buf], writes=[kfm.buf], tag=f"kc{tag}{ch}{tb}")
            PS.free(trb2)
        for h in range(NH_CORE):
            bank = PS.alloc(f"kr{tag}")
            ps = bank.view(F32, (64, 512))
            P.op("pe", lambda e, ps=ps, h=h, ch=ch:
                 e.matmul(ps, pk_ap, kfm_ap[:, h, ch * 512:(ch + 1) * 512],
                          start=True, stop=True),
                 reads=[kfm.buf, pk_r.buf], writes=[bank.buf], tag=f"kr_mm{tag}{ch}{h}")
            dst = krf_ap[64 * h:64 * (h + 1), ch * 512:(ch + 1) * 512]
            P.op("dve", lambda e, ps=ps, dst=dst: e.tensor_copy(dst, ps),
                 reads=[bank.buf], writes=[krf.buf], tag=f"kr_c{tag}{ch}{h}")
            PS.free(bank)
        for h in range(NH_CORE):
            bank = PS.alloc(f"vr{tag}")
            ps = bank.view(F32, (64, 512))
            for hh in range(2):
                def em_vr(e, ps=ps, xap=aps[hh], h=h, hh=hh):
                    first = None
                    for c8 in range(8):
                        cc = hh * 8 + c8
                        last = e.matmul(ps, wvr_ap[:, h, cc, :], xap[:, c8, :],
                                        start=(cc == 0), stop=(cc == NCC - 1))
                        first = first or last
                    return (first, last)
                P.op("pe", em_vr, reads=[wvr.buf, bs[hh].buf], writes=[bank.buf],
                     tag=f"vr_mm{tag}{ch}{h}{hh}")
            vrf = rsc.view(BF16, (128, 512), byte_off=2048)[0:64, :]
            P.op("dve", lambda e, ps=ps, vrf=vrf: e.tensor_copy(vrf, ps),
                 reads=[bank.buf], writes=[rsc.buf], tag=f"vr_ev{tag}{ch}{h}")
            PS.free(bank)
            trb3 = PS.alloc(f"vrt{tag}")
            tr_ps = trb3.view(BF16, (128, 4, 128))[:, :, 0:64]

            def em_vtr(e, tr_ps=tr_ps, vrf=vrf):
                first = None
                for j in range(4):
                    last = e.transpose(tr_ps[:, j, :], vrf[:, j * 128:(j + 1) * 128],
                                       ident[0:64, 0:64])
                    first = first or last
                return (first, last)
            P.op("pe", em_vtr, reads=[rsc.buf, ident_b], writes=[trb3.buf],
                 tag=f"vr_tr{tag}{ch}{h}")

            def em_vrc(e, tr_ps=tr_ps, ch=ch, h=h):
                first = None
                for j in range(4):
                    last = e.tensor_copy(vrt_ap[:, ch * 4 + j, h * 65:h * 65 + 64],
                                         tr_ps[:, j, :])
                    first = first or last
                return (first, last)
            P.op("dve", em_vrc, reads=[trb3.buf], writes=[vrt.buf], tag=f"vr_c{tag}{ch}{h}")
            PS.free(trb3)

    qfm = A.alloc(f"qfm{tag}", NH_CORE * 512 * 2)
    qfm_ap = qfm.view(BF16, (128, NH_CORE, 512))
    qr = A.alloc(f"qr{tag}", 512 * 2)
    qr_ap = qr.view(BF16, (128, 512))
    qsc = A.alloc(f"qsc{tag}", 256 * 2)
    psb = [A.alloc(f"p{tag}{i}", 512 * 2) for i in range(6)]
    olsb = A.alloc(f"ol{tag}", 2 * 512 * 2)
    ogsb = A.alloc(f"og{tag}", 512 * 2)
    cgsb = A.alloc(f"cg{tag}", 512 * 4 + 512 * 2)
    lsb = A.alloc(f"l{tag}", 2048 * 4)
    lcb = A.alloc(f"lc{tag}", 8 * 4)
    gq = A.alloc(f"gq{tag}", 4 * 2)
    grg = A.alloc(f"grg{tag}", 4 * 4)
    zsb = A.alloc(f"z{tag}", 4 * 256 * 2)
    z_ap = zsb.view(BF16, (128, 4, 256))

    _flc = {}

    def get_chunk_f(qc):
        if qc not in _flc:
            bs, aps, rb = load_chunk(qc)
            load_rope(qc, rb)
            _flc[qc] = (bs, aps, rb)
        return _flc[qc]

    for qc in range(NCH):
        bs, aps, rb = get_chunk_f(qc)
        if qc + 1 < NCH:
            get_chunk_f(qc + 1)
        bufs = [bs[0].buf, bs[1].buf]
        cos_all, ss_all = rope_tiles(rb)
        r_, t0_ = qc // 2, (qc % 2) * 4
        gq_ap = gq.view(BF16, (128, 4))
        P.dma("sp", lambda e, o=gq_ap, i=agov[1][r_, :, HGW + t0_:HGW + t0_ + 4]:
              e.dma_start(out=o, in_=i), reads=[ago[1][1]], writes=[gq.buf],
              tag=f"ldgq{tag}{qc}")
        for tb in range(4):
            qbank, qps = project_tm(aps, bufs, wq_ap, wq_r.buf, tb, 256)
            for h in range(NH_CORE):
                q_ap = qsc.view(BF16, (128, 256))[:, h * 128:(h + 1) * 128]
                rope_apply(qps, qbank, cos_all[:, tb, :], ss_all[:, tb, :], rb, h,
                           q_ap, qsc.buf)
            PS.free(qbank)
            trbq = PS.alloc(f"qtr{tag}")
            tr_ps = trbq.view(BF16, (128, 2, 128))

            def em_qtr(e, tr_ps=tr_ps):
                i0 = e.transpose(tr_ps[:, 0, :], qsc.view(BF16, (128, 256))[:, 0:128], ident)
                i1 = e.transpose(tr_ps[:, 1, :], qsc.view(BF16, (128, 256))[:, 128:256], ident)
                return (i0, i1)
            P.op("pe", em_qtr, reads=[qsc.buf, ident_b], writes=[trbq.buf],
                 tag=f"qtr{tag}{qc}{tb}")

            def em_qc2(e, tr_ps=tr_ps, tb=tb):
                i0 = e.tensor_copy(qfm_ap[:, 0, tb * 128:(tb + 1) * 128], tr_ps[:, 0, :])
                i1 = e.tensor_copy(qfm_ap[:, 1, tb * 128:(tb + 1) * 128], tr_ps[:, 1, :])
                return (i0, i1)
            P.op("dve", em_qc2, reads=[trbq.buf], writes=[qfm.buf], tag=f"qc{tag}{qc}{tb}")
            PS.free(trbq)
        for h in range(NH_CORE):
            bank = PS.alloc(f"qr{tag}")
            ps = bank.view(F32, (64, 512))
            P.op("pe", lambda e, ps=ps, h=h:
                 e.matmul(ps, pk_ap, qfm_ap[:, h, :], start=True, stop=True),
                 reads=[qfm.buf, pk_r.buf], writes=[bank.buf], tag=f"qr_mm{tag}{qc}{h}")
            dst = qr_ap[64 * h:64 * (h + 1), :]
            P.op("dve", lambda e, ps=ps, dst=dst: e.tensor_copy(dst, ps),
                 reads=[bank.buf], writes=[qr.buf], tag=f"qr_c{tag}{qc}{h}")
            PS.free(bank)

        for h in range(NH_CORE):
            j0 = 4 if qc == 0 else 0
            obank = PS.alloc(f"ol{tag}")
            lbank = PS.alloc(f"ll{tag}")
            ov = obank.view(F32, (128, 512))
            lv = lbank.view(F32, (1, 512))
            sbanks = [PS.alloc(f"s{tag}{i}") for i in range(6)]
            groups = [list(range(g, min(g + 3, 8))) for g in range(j0, 8, 3)]

            def sl_grp(gi, h=h, qc=qc, sbanks=sbanks, groups=groups):
                js = groups[gi]

                def em(e, js=js, h=h, qc=qc):
                    first = None
                    for j in js:
                        kbg = qc * 4 + j - 4
                        sv = sbanks[j % 6].view(F32, (128, 512))
                        last = e.matmul(sv, kfm_ap[:, h, kbg * 128:(kbg + 1) * 128],
                                        qfm_ap[:, h, :], start=True, stop=True)
                        first = first or last
                    return (first, last)
                P.op("pe", em, reads=[kfm.buf, qfm.buf],
                     writes=[sbanks[j % 6].buf for j in js], tag=f"slg{tag}{qc}{h}{gi}")
            sl_grp(0)
            if len(groups) > 1:
                sl_grp(1)
            for gi, js in enumerate(groups):
                def em_exp(e, js=js, sbanks=sbanks):
                    first = None
                    for j in js:
                        sv = sbanks[j % 6].view(F32, (128, 512))
                        p_ap = psb[j % 6].view(BF16, (128, 512))
                        last = e.activation(p_ap, sv, AF.Exp, scale=SC1)
                        first = first or last
                    return (first, last)
                P.op("act", em_exp, reads=[sbanks[j % 6].buf for j in js],
                     writes=[psb[j % 6].buf for j in js], tag=f"lexpg{tag}{qc}{h}{gi}")

                def em_msk(e, js=js):
                    first = None
                    for j in js:
                        p_ap = psb[j % 6].view(BF16, (128, 512))
                        last = e.tensor_mul(p_ap, p_ap, masksb[:, j, :])
                        first = first or last
                    return (first, last)
                P.op("dve", em_msk, reads=[psb[j % 6].buf for j in js] + [masksb_b],
                     writes=[psb[j % 6].buf for j in js], tag=f"lmskg{tag}{qc}{h}{gi}")
                if gi + 2 < len(groups):
                    sl_grp(gi + 2)

                def em_pvl(e, js=js, h=h, qc=qc, ov=ov, lv=lv, j0=j0):
                    first = None
                    for j in js:
                        kbg = qc * 4 + j - 4
                        p_ap = psb[j % 6].view(BF16, (128, 512))
                        i0 = e.matmul(ov, vtm_ap[:, kbg, h * 128:(h + 1) * 128], p_ap,
                                      start=(j == j0), stop=(j == 7))
                        first = first or i0
                        last = e.matmul(lv, ones_col, p_ap, start=(j == j0), stop=(j == 7))
                    return (first, last)
                P.op("pe", em_pvl,
                     reads=[vtm.buf, ones_col_b] + [psb[j % 6].buf for j in js],
                     writes=[obank.buf, lbank.buf], tag=f"lpvlg{tag}{qc}{h}{gi}")
            ol_ap = olsb.view(BF16, (128, 2, 512))[:, h, :]
            P.op("act", lambda e, ol_ap=ol_ap, ov=ov: e.copy(ol_ap, ov),
                 reads=[obank.buf], writes=[olsb.buf], tag=f"olev{tag}{qc}{h}")
            l_ap = lsb.view(F32, (1, 2048))[:, h * 512:(h + 1) * 512]
            P.op("act", lambda e, l_ap=l_ap, lv=lv: e.copy(l_ap, lv),
                 reads=[lbank.buf], writes=[lsb.buf], tag=f"llev{tag}{qc}{h}")
            for r in sbanks + [obank, lbank]:
                PS.free(r)

        NKB = SEQ // 128
        ogbanks = [PS.alloc(f"og{tag}{i}") for i in range(4)]
        ogvs = [b.view(F32, (65, 512)) for b in ogbanks]
        sgbanks = [PS.alloc(f"sg{tag}{i}") for i in range(4)]

        def sg_pair(kb, qc=qc, sgbanks=sgbanks):
            def em(e, kb=kb, sgbanks=sgbanks):
                i0 = e.matmul(sgbanks[(kb % 2) * 2 + 0].view(F32, (128, 512)),
                              krf_ap[0:64, kb * 128:(kb + 1) * 128],
                              qr_ap[0:64, :], start=True, stop=True)
                i1 = e.matmul(sgbanks[(kb % 2) * 2 + 1].view(F32, (128, 512)),
                              krf_ap[64:128, kb * 128:(kb + 1) * 128],
                              qr_ap[64:128, :], start=True, stop=True)
                return (i0, i1)
            P.op("pe", em, reads=[krf.buf, qr.buf],
                 writes=[sgbanks[(kb % 2) * 2].buf, sgbanks[(kb % 2) * 2 + 1].buf],
                 tag=f"sgp{tag}{qc}{kb}")
        sg_pair(0)
        sg_pair(1)
        for kb in range(NKB):
            par = (kb % 2) * 2

            def em_exp(e, par=par, sgbanks=sgbanks):
                i0 = e.activation(psb[par].view(BF16, (128, 512)),
                                  sgbanks[par].view(F32, (128, 512)), AF.Exp, scale=SC2)
                i1 = e.activation(psb[par + 1].view(BF16, (128, 512)),
                                  sgbanks[par + 1].view(F32, (128, 512)), AF.Exp, scale=SC2)
                return (i0, i1)
            P.op("act", em_exp, reads=[sgbanks[par].buf, sgbanks[par + 1].buf],
                 writes=[psb[par].buf, psb[par + 1].buf], tag=f"gexp{tag}{qc}{kb}")
            if kb + 2 < NKB:
                sg_pair(kb + 2)

            def em_gpv(e, kb=kb, par=par, ogvs=ogvs):
                first = None
                for h in range(NH_CORE):
                    p_ap = psb[par + h].view(BF16, (128, 512))
                    for kh in range(2):
                        last = e.matmul(ogvs[h * 2 + kh],
                                        vrt_ap[64 * kh:64 * (kh + 1), kb, h * 65:(h + 1) * 65],
                                        p_ap[64 * kh:64 * (kh + 1), :],
                                        start=(kb == 0), stop=(kb == NKB - 1))
                        first = first or last
                return (first, last)
            P.op("pe", em_gpv,
                 reads=[vrt.buf, psb[par].buf, psb[par + 1].buf],
                 writes=[b.buf for b in ogbanks], tag=f"gpv{tag}{qc}{kb}")
        for r in sgbanks:
            PS.free(r)

        for h in range(NH_CORE):
            ol_ap = olsb.view(BF16, (128, 2, 512))[:, h, :]
            og_ap = ogsb.view(BF16, (64, 512))
            oga, ogb_ = ogvs[h * 2], ogvs[h * 2 + 1]
            og_f32 = cgsb.view(F32, (64, 512))
            P.op("dve", lambda e, og_f32=og_f32, oga=oga: e.tensor_copy(og_f32, oga[0:64, :]),
                 reads=[ogbanks[h * 2].buf], writes=[cgsb.buf], tag=f"oge1{tag}{qc}{h}")
            P.op("dve", lambda e, og_f32=og_f32, ogb_=ogb_, og_ap=og_ap:
                 e.tensor_tensor(og_ap, og_f32, ogb_[0:64, :], ALU.add),
                 reads=[ogbanks[h * 2 + 1].buf, cgsb.buf], writes=[ogsb.buf],
                 tag=f"oge2{tag}{qc}{h}")
            lg_ap = lsb.view(F32, (1, 2048))[:, 1024 + h * 512:1024 + (h + 1) * 512]
            P.op("act", lambda e, lg_ap=lg_ap, oga=oga: e.copy(lg_ap, oga[64:65, :]),
                 reads=[ogbanks[h * 2].buf], writes=[lsb.buf], tag=f"lge1{tag}{qc}{h}")
            P.op("dve", lambda e, lg_ap=lg_ap, ogb_=ogb_:
                 e.tensor_tensor(lg_ap, lg_ap, ogb_[64:65, :], ALU.add),
                 reads=[ogbanks[h * 2 + 1].buf, lsb.buf], writes=[lsb.buf],
                 tag=f"lge2{tag}{qc}{h}")
            cgbank = PS.alloc(f"cg{tag}")
            cgv = cgbank.view(F32, (128, 512))
            P.op("pe", lambda e, cgv=cgv, og_ap=og_ap: e.matmul(cgv, uo_ap, og_ap, start=True, stop=True),
                 reads=[ogsb.buf, uo_r.buf], writes=[cgbank.buf], tag=f"cup{tag}{qc}{h}")
            cg_ap = cgsb.view(BF16, (128, 512), byte_off=2048)
            P.op("act", lambda e, cg_ap=cg_ap, cgv=cgv: e.copy(cg_ap, cgv),
                 reads=[cgbank.buf], writes=[cgsb.buf], tag=f"cupev{tag}{qc}{h}")
            PS.free(cgbank)
            lcbank = PS.alloc(f"lc{tag}")
            lc_ps = lcbank.view(F32, (128, 8))
            lboth = lsb.view(F32, (1, 2048))

            def em_lc(e, lc_ps=lc_ps, lboth=lboth, h=h):
                first = None
                for row in range(2):
                    base = h * 512 if row == 0 else 1024 + h * 512
                    for j in range(4):
                        last = e.matmul(lc_ps[:, row * 4 + j:row * 4 + j + 1],
                                        lboth[0:1, base + j * 128:base + (j + 1) * 128],
                                        onesf, start=True, stop=True)
                        first = first or last
                return (first, last)
            P.op("pe", em_lc, reads=[lsb.buf, onesf_b], writes=[lcbank.buf],
                 tag=f"lc{tag}{qc}{h}")
            lc_ap = lcb.view(F32, (128, 8))
            P.op("dve", lambda e, lc_ap=lc_ap, lc_ps=lc_ps: e.reciprocal(lc_ap, lc_ps),
                 reads=[lcbank.buf], writes=[lcb.buf], tag=f"lcr{tag}{qc}{h}")
            PS.free(lcbank)
            grg_ap = grg.view(F32, (128, 4))
            P.op("dve", lambda e, grg_ap=grg_ap, lc_ap=lc_ap, gq_ap=gq_ap:
                 e.tensor_mul(grg_ap, lc_ap[:, 4:8], gq_ap),
                 reads=[lcb.buf, gq.buf], writes=[grg.buf], tag=f"grg{tag}{qc}{h}")
            t1bank = PS.alloc(f"olT{tag}")
            t2bank = PS.alloc(f"cgT{tag}")
            t1 = t1bank.view(BF16, (128, 4, 128))
            t2 = t2bank.view(BF16, (128, 4, 128))

            def em_trs(e, t1=t1, t2=t2, ol_ap=ol_ap, cg_ap=cg_ap):
                first = None
                for j in range(4):
                    i0 = e.transpose(t1[:, j, :], ol_ap[:, j * 128:(j + 1) * 128], ident)
                    first = first or i0
                for j in range(4):
                    last = e.transpose(t2[:, j, :], cg_ap[:, j * 128:(j + 1) * 128], ident)
                return (first, last)
            P.op("pe", em_trs, reads=[olsb.buf, cgsb.buf, ident_b],
                 writes=[t1bank.buf, t2bank.buf], tag=f"ctrs{tag}{qc}{h}")

            def em_comb(e, t1=t1, lc_ap=lc_ap, h=h):
                first = None
                for j in range(4):
                    i0 = e.tensor_scalar(z_ap[:, j, h * 128:(h + 1) * 128], t1[:, j, :],
                                         lc_ap[:, j:j + 1], None, ALU.mult)
                    first = first or i0
                return (first, i0)
            P.op("dve", em_comb, reads=[t1bank.buf, lcb.buf], writes=[zsb.buf],
                 tag=f"comb1{tag}{qc}{h}")

            def em_comb2(e, t2=t2, grg_ap=grg_ap, h=h):
                first = None
                for j in range(4):
                    i0 = e.scalar_tensor_tensor(z_ap[:, j, h * 128:(h + 1) * 128], t2[:, j, :],
                                                grg_ap[:, j:j + 1],
                                                z_ap[:, j, h * 128:(h + 1) * 128],
                                                ALU.mult, ALU.add)
                    first = first or i0
                return (first, i0)
            P.op("dve", em_comb2, reads=[t2bank.buf, grg.buf, zsb.buf], writes=[zsb.buf],
                 tag=f"comb2{tag}{qc}{h}")
            PS.free(t1bank)
            PS.free(t2bank)
        for r in ogbanks:
            PS.free(r)
        zc_v = z_in.rearrange("(qc tb p) f -> qc p tb f", tb=4, p=128)
        P.dma("sp", lambda e, o=zc_v[qc], i=z_ap: e.dma_start(out=o, in_=i),
              reads=[zsb.buf], writes=[z_in_b], tag=f"zctr{tag}{qc}")

    for r in [kfm, vtm, krf, vrt, qfm, qr, qsc, olsb, ogsb, cgsb, lsb, lcb, gq, grg,
              zsb, ksc, rsc, wvr, wq_r, wk_r, wv_r, pk_r, uo_r] + psb + \
             [x for pair in xnc for x in pair] + ropeb:
        A.free(r)

    oT_ap_l, oT_b_l = layer["oT"]
    wo_pre = [A.alloc(f"wo{tag}{i}", NCC * 512 * 2) for i in range(2)]
    for oc in range(2):
        wap = wo_pre[oc].view(BF16, (128, NCC, 512))
        P.dma("pool", lambda e, o=wap, i=oT_ap_l[oc]: e.dma_start(out=o, in_=i),
              reads=[oT_b_l], writes=[wo_pre[oc].buf], tag=f"ldwo{tag}{oc}")
    if not hasattr(C, "wo_pre"):
        C.wo_pre = {}
    C.wo_pre[tag] = wo_pre

    P.collective(lambda e: e.collective_compute(
        "AllGather", ALU.bypass, replica_groups=[core_grp],
        ins=[z_in.opt()], outs=[z_out.opt()]),
        reads=[z_in_b], writes=[z_out_b], tag=f"zag{tag}")
    build_outproj(C, layer["oT"], z_out, z_out_b, x_dram_ap, x_dram_buf,
                  out_dram_ap, out_dram_buf, tag=tag)


CAP = 512


def build_ln2(C, x_dram_ap, x_dram_buf, xn_fm_r, xn_tm_dram, xn_tm_dram_b, tag=""):
    P, A, PS = C.P, C.A, C.PS
    ident = C.const["ident"][1]
    ident_b = C.const["ident"][0].buf
    xs = [A.alloc(f"lnx{tag}{i}", 2048 * 4) for i in range(2)]
    sc = A.alloc(f"lnsc{tag}", 32 * 4)
    xn_tm = [A.alloc(f"lnxn{tag}{i}", 2048 * 2) for i in range(2)]
    xn_fm_ap = xn_fm_r.view(BF16, (128, NCC, TCORE))
    xn_fm_r.tile_bufs = [Buf(f"xnfm{tag}_t{t}") for t in range(NTT)]
    for tt in range(NTT):
        xb = xs[tt % 2]
        x_ap = xb.view(F32, (128, 2048))
        P.dma("sp", lambda e, o=x_ap, i=x_dram_ap[tt * 128:(tt + 1) * 128, :]:
              e.dma_start(out=o, in_=i), reads=[tile_buf(x_dram_buf, tt)],
              writes=[xb.buf], tag=f"ln_ldx{tag}{tt}")
        st = sc.view(F32, (128, 24))
        mv = sc.view(F32, (128, 2), byte_off=96)
        rstd = sc.view(F32, (128, 1), byte_off=104)

        def em_stats(e, x_ap=x_ap, st=st):
            first = None
            for j in range(4):
                last = e.bn_stats(st[:, j * 6:(j + 1) * 6], x_ap[:, j * 512:(j + 1) * 512])
                first = first or last
            return (first, last)
        P.op("dve", em_stats, reads=[xb.buf], writes=[sc.buf], tag=f"ln_st{tag}{tt}")
        P.op("dve", lambda e, st=st, mv=mv: e.bn_aggr(mv, st),
             reads=[sc.buf], writes=[sc.buf])
        P.op("act", lambda e, mv=mv, rstd=rstd: e.activation(rstd, mv[:, 1:2], AF.Sqrt, bias=1e-5),
             reads=[sc.buf], writes=[sc.buf])
        P.op("dve", lambda e, rstd=rstd: e.reciprocal(rstd, rstd),
             reads=[sc.buf], writes=[sc.buf])
        xnb = xn_tm[tt % 2]
        xn_ap = xnb.view(BF16, (128, 2048))
        P.op("dve", lambda e, x_ap=x_ap, xn_ap=xn_ap, mv=mv, rstd=rstd:
             e.tensor_scalar(xn_ap, x_ap, mv[:, 0:1], rstd[:, 0:1], ALU.subtract, ALU.mult),
             reads=[xb.buf, sc.buf], writes=[xnb.buf], tag=f"ln_nm{tag}{tt}")
        P.dma("sp", lambda e, o=xn_tm_dram[tt * 128:(tt + 1) * 128, :], i=xn_ap:
              e.dma_start(out=o, in_=i), reads=[xnb.buf], writes=[xn_tm_dram_b],
              tag=f"ln_sttm{tag}{tt}")
        for g in range(4):
            bank = PS.alloc(f"lnT{tag}{tt}{g}")
            bap = bank.view(BF16, (128, 4, 128))

            def em_tr(e, xn_ap=xn_ap, bap=bap, g=g):
                first = None
                for j in range(4):
                    cc = g * 4 + j
                    last = e.transpose(bap[:, j, :], xn_ap[:, cc * 128:(cc + 1) * 128], ident)
                    first = first or last
                return (first, last)
            P.op("pe", em_tr, reads=[xnb.buf, ident_b], writes=[bank.buf], tag=f"ln_tr{tag}{tt}{g}")
            dst = xn_fm_ap[:, g * 4:(g + 1) * 4, tt * 128:(tt + 1) * 128]
            P.op("dve", lambda e, bap=bap, dst=dst: e.tensor_copy(dst, bap),
                 reads=[bank.buf], writes=[xn_fm_r.buf], tag=f"ln_tc{tag}{tt}{g}")
            PS.free(bank)
    for b in xs + xn_tm + [sc]:
        A.free(b)


def build_ffn_sparse(C, layer, xn_fm_r, xn_tm_dram, xn_tm_dram_b,
                     x_dram_ap, x_dram_buf, out_dram_ap, out_dram_buf, tag=""):
    P, A, PS = C.P, C.A, C.PS
    nc = C.nc
    xn_fm = xn_fm_r.view(BF16, (128, NCC, TCORE))
    w = layer
    ident = C.const["ident"][1]
    ident_b = C.const["ident"][0].buf
    onesf = C.const["onesf"][1]
    onesf_b = C.const["onesf"][0].buf
    iota_c = C.const["iota_col"][1]
    iota_b = C.const["iota_col"][0].buf
    zrow = C.const["zrow"][1]
    zrow_b = C.const["zrow"][0].buf
    oobrow = C.const["oobrow"][1]
    oobrow_b = C.const["oobrow"][0].buf

    y_dram, y_dram_b = C.dram_tmp(f"ydr{tag}", [TCORE, 2048], F32)
    idxd = [C.dram_tmp(f"idx{tag}{e}", [CAP, 1], mybir.dt.int32) for e in range(NE)]
    wvd = [C.dram_tmp(f"wvd{tag}{e}", [CAP, 1], F32) for e in range(NE)]

    gw_ap = C.const["gate_wT"][1]
    gw_b = C.const["gate_wT"][0].buf
    comb = A.alloc(f"comb{tag}", NTT * NE * 4)
    comb_ap = comb.view(F32, (128, NTT, NE))
    rsc = A.alloc(f"rsc{tag}", 64 * 4)
    for tt in range(NTT):
        bank = PS.alloc(f"rt{tag}{tt}")
        ps = bank.view(F32, (128, NE))

        def em_rt(e, ps=ps, tt=tt):
            first = None
            for cc in range(NCC):
                last = e.matmul(ps, xn_fm[:, cc, tt * 128:(tt + 1) * 128], gw_ap[:, cc, :],
                                start=(cc == 0), stop=(cc == NCC - 1))
                first = first or last
            return (first, last)
        P.op("pe", em_rt, reads=[xn_fm_r.buf, gw_b], writes=[bank.buf], tag=f"rt_mm{tag}{tt}")
        ex = rsc.view(F32, (128, NE))
        s1 = rsc.view(F32, (128, 1), byte_off=32)
        sc_v = rsc.view(F32, (128, NE), byte_off=48)
        m1 = rsc.view(F32, (128, 1), byte_off=96)
        msk = rsc.view(F32, (128, NE), byte_off=112)
        m2 = rsc.view(F32, (128, 1), byte_off=160)
        ge = rsc.view(F32, (128, NE), byte_off=176)
        b = rsc.buf
        cv = comb_ap[:, tt, :]
        P.op("act", lambda e, ps=ps, ex=ex: e.activation(ex, ps, AF.Exp),
             reads=[bank.buf], writes=[b], tag=f"rt_exp{tag}{tt}")
        P.op("dve", lambda e, ex=ex, s1=s1: e.reduce_sum(s1, ex, axis=AX.X), reads=[b], writes=[b])
        P.op("dve", lambda e, s1=s1: e.reciprocal(s1, s1), reads=[b], writes=[b])
        P.op("dve", lambda e, ex=ex, s1=s1, sc_v=sc_v: e.tensor_scalar(sc_v, ex, s1, None, ALU.mult),
             reads=[b], writes=[b])
        P.op("dve", lambda e, sc_v=sc_v, m1=m1: e.reduce_max(m1, sc_v, axis=AX.X), reads=[b], writes=[b])
        P.op("dve", lambda e, sc_v=sc_v, m1=m1, msk=msk: e.tensor_scalar(msk, sc_v, m1, None, ALU.is_equal),
             reads=[b], writes=[b])
        P.op("dve", lambda e, msk=msk, sc_v=sc_v: e.scalar_tensor_tensor(msk, msk, -1e30, sc_v, ALU.mult, ALU.add),
             reads=[b], writes=[b])
        P.op("dve", lambda e, msk=msk, m2=m2: e.reduce_max(m2, msk, axis=AX.X), reads=[b], writes=[b])
        P.op("dve", lambda e, sc_v=sc_v, m2=m2, ge=ge: e.tensor_scalar(ge, sc_v, m2, None, ALU.is_ge),
             reads=[b], writes=[b])
        P.op("dve", lambda e, sc_v=sc_v, ge=ge, cv=cv: e.tensor_mul(cv, sc_v, ge),
             reads=[b], writes=[comb.buf], tag=f"rt_comb{tag}{tt}")
        PS.free(bank)
    A.free(rsc)

    cfm = A.alloc(f"cfm{tag}", NE * TCORE * 4)
    cfm_ap = cfm.view(F32, (NE, TCORE))
    for half in range(2):
        cfb = PS.alloc(f"cf{tag}{half}")
        cf_ps = cfb.view(F32, (NE, 4, 128))

        identf = C.const["identf"][1]
        identf_b = C.const["identf"][0].buf

        def em_ctr(e, cf_ps=cf_ps, half=half, identf=identf):
            first = None
            for j in range(4):
                last = e.transpose(cf_ps[:, j, :], comb_ap[:, half * 4 + j, :], identf)
                first = first or last
            return (first, last)
        P.op("pe", em_ctr, reads=[comb.buf, identf_b], writes=[cfb.buf], tag=f"ctr{tag}{half}")
        dst = cfm_ap.rearrange("p (h t) -> p h t", h=2)[:, half, :]
        P.op("dve", lambda e, cf_ps=cf_ps, dst=dst:
             e.tensor_copy(dst, cf_ps.rearrange("p a b -> p (a b)")),
             reads=[cfb.buf], writes=[cfm.buf], tag=f"ctrc{tag}{half}")
        PS.free(cfb)

    ssc = A.alloc(f"ssc{tag}", 2 * TCORE * 4)
    slotc = A.alloc(f"slc{tag}", NE * NTT * 4)
    slotc_i = slotc.view(mybir.dt.int32, (128, NE, NTT))
    identf = C.const["identf"][1]
    identf_b = C.const["identf"][0].buf
    mrow = ssc.view(F32, (NE, TCORE))
    trow = ssc.view(F32, (NE, TCORE), byte_off=TCORE * 4)
    P.op("dve", lambda e: e.tensor_single_scalar(mrow, cfm_ap, 0.0, ALU.is_gt),
         reads=[cfm.buf], writes=[ssc.buf], tag=f"smask{tag}")
    P.op("dve", lambda e: e.tensor_tensor_scan(trow, mrow, zrow, 0.0, ALU.add, ALU.add),
         reads=[ssc.buf, zrow_b], writes=[ssc.buf], tag=f"sscan{tag}")
    P.op("dve", lambda e: e.tensor_scalar_add(trow, trow, -4097.0),
         reads=[ssc.buf], writes=[ssc.buf])
    P.op("dve", lambda e: e.tensor_mul(trow, trow, mrow),
         reads=[ssc.buf], writes=[ssc.buf])
    P.op("dve", lambda e: e.tensor_scalar_add(trow, trow, 4096.0),
         reads=[ssc.buf], writes=[ssc.buf], tag=f"sfix{tag}")
    for tt in range(NTT):
        scb = PS.alloc(f"sc{tag}")
        sc_ps = scb.view(F32, (128, NE))
        P.op("pe", lambda e, sc_ps=sc_ps, tt=tt:
             e.transpose(sc_ps, trow[:, tt * 128:(tt + 1) * 128], identf[0:NE, 0:NE]),
             reads=[ssc.buf, identf_b], writes=[scb.buf], tag=f"scol{tag}{tt}")
        P.op("dve", lambda e, sc_ps=sc_ps, tt=tt:
             e.tensor_copy(slotc_i[:, :, tt], sc_ps),
             reads=[scb.buf], writes=[slotc.buf], tag=f"scolc{tag}{tt}")
        PS.free(scb)
    for e_i in range(NE):
        P.dma("sp", lambda e, e_i=e_i: e.dma_start(out=wvd[e_i][0].rearrange("c one -> one c"),
                                                   in_=zrow[0:1, 0:CAP]),
              reads=[zrow_b], writes=[wvd[e_i][1]], tag=f"wz{tag}{e_i}")
        P.dma("pool", lambda e, e_i=e_i: e.dma_start(out=idxd[e_i][0].rearrange("c one -> one c"),
                                                     in_=zrow[0:1, 0:CAP], accum_op=ALU.bypass)
              if False else
              e.dma_start(out=idxd[e_i][0].rearrange("c one -> one c"), in_=oobrow[0:1, 0:CAP]),
              reads=[oobrow_b], writes=[idxd[e_i][1]], tag=f"iz{tag}{e_i}")
        for tt in range(NTT):
            off = IndirectOffsetOnAxis(slotc_i[:, e_i, tt:tt + 1], 0)
            P.dma("pool", lambda e, e_i=e_i, tt=tt, off=off:
                  e.indirect_dma_start(out=idxd[e_i][0], out_offset=off,
                                       in_=iota_c[:, tt:tt + 1], in_offset=None,
                                       bounds_check=CAP - 1, oob_is_err=False),
                  reads=[slotc.buf, iota_b], writes=[idxd[e_i][1]], tag=f"sidx{tag}{e_i}{tt}")
            P.dma("pool", lambda e, e_i=e_i, tt=tt, off=off:
                  e.indirect_dma_start(out=wvd[e_i][0], out_offset=off,
                                       in_=comb.view(F32, (128, NTT * NE))[:, tt * NE + e_i:tt * NE + e_i + 1],
                                       in_offset=None,
                                       bounds_check=CAP - 1, oob_is_err=False),
                  reads=[slotc.buf, comb.buf, wvd[e_i][1]], writes=[wvd[e_i][1]],
                  tag=f"swv{tag}{e_i}{tt}")
    A.free(ssc)
    A.free(cfm)

    wgb = [A.alloc(f"wg{tag}{i}", NCC * 128 * 2) for i in range(3)]
    wub = [A.alloc(f"wu{tag}{i}", NCC * 128 * 2) for i in range(3)]
    hb = A.alloc(f"h{tag}", NI * TCORE * 2)
    sg = A.alloc(f"sg{tag}", 4 * 512 * 2)
    eo = A.alloc(f"eo{tag}", 2048 * 4)
    eo_ap = eo.view(F32, (128, 2048))
    idxw = A.alloc(f"ixw{tag}", 2 * 4 * 4)
    xng_t = A.alloc(f"xgt{tag}", 2048 * 2)
    xng = A.alloc(f"xng{tag}", NCC * CAP * 2)
    xng_ap = xng.view(BF16, (128, NCC, CAP))
    P.op("dve", lambda e: e.memset(xng_t.view(BF16, (128, 2048)), 0.0),
         writes=[xng_t.buf], tag=f"xgz{tag}")

    wgl, wgl_b = w["wg"]
    wul, wul_b = w["wu"]
    wdl, wdl_b = w["wd"]
    sgl, sgl_b = w["swg"]
    sul, sul_b = w["swu"]
    sdl, sdl_b = w["swd"]

    def expert(n_ib, n_tok, wg_src, wu_src, wd_src, xn_src_ap, xn_src_buf,
               emit_out, su):
        h_ap = hb.view(BF16, (128, n_ib, TCORE))[:, :, 0:n_tok]
        wdb = [A.alloc(f"wd{tag}{su}{i}", 2048 * 2) for i in range(n_ib)]
        ntc = n_tok // 512
        for ib in range(n_ib):
            wg_r = wgb[ib % 3]
            wu_r = wub[ib % 3]
            dstg = wg_r.view(BF16, (128, NCC, 128))
            dstu = wu_r.view(BF16, (128, NCC, 128))
            if not (su == "e0" and ib < n_pre):
                ap_g, bg = wg_src(ib)
                ap_u, bu = wu_src(ib)
                P.dma("pool", lambda e, o=dstg, i=ap_g: e.dma_start(out=o, in_=i),
                      reads=[bg], writes=[wg_r.buf], tag=f"ldwg{tag}{su}{ib}")
                P.dma("pool", lambda e, o=dstu, i=ap_u: e.dma_start(out=o, in_=i),
                      reads=[bu], writes=[wu_r.buf], tag=f"ldwu{tag}{su}{ib}")
            psg = [PS.alloc(f"g{su}{ib}{tc}") for tc in range(ntc)]
            psu = [PS.alloc(f"u{su}{ib}{tc}") for tc in range(ntc)]

            def em_gu(e, dstg=dstg, dstu=dstu, psg=psg, psu=psu, ntc=ntc,
                      xn_src_ap=xn_src_ap):
                first = None
                for cc in range(NCC):
                    for tc in range(ntc):
                        i0 = e.matmul(psg[tc].view(F32, (128, 512)), dstg[:, cc, :],
                                      xn_src_ap[:, cc, tc * 512:(tc + 1) * 512],
                                      start=(cc == 0), stop=(cc == NCC - 1))
                        first = first or i0
                    for tc in range(ntc):
                        last = e.matmul(psu[tc].view(F32, (128, 512)), dstu[:, cc, :],
                                        xn_src_ap[:, cc, tc * 512:(tc + 1) * 512],
                                        start=(cc == 0), stop=(cc == NCC - 1))
                return (first, last)
            P.op("pe", em_gu, reads=[wg_r.buf, wu_r.buf, xn_src_buf],
                 writes=[r.buf for r in psg + psu], tag=f"gu{tag}{su}{ib}")
            for tc in range(ntc):
                sg_ap = sg.view(BF16, (128, 4, 512))[:, (ib * ntc + tc) % 4, :]
                P.op("act", lambda e, p=psg[tc].view(F32, (128, 512)), o=sg_ap:
                     e.activation(o, p, AF.Silu),
                     reads=[psg[tc].buf], writes=[sg.buf], tag=f"si{tag}{su}{ib}{tc}")
                hdst = h_ap[:, ib, tc * 512:(tc + 1) * 512]
                P.op("dve", lambda e, p=psu[tc].view(F32, (128, 512)), s=sg_ap, o=hdst:
                     e.tensor_mul(o, p, s),
                     reads=[psu[tc].buf, sg.buf], writes=[hb.buf], tag=f"hm{tag}{su}{ib}{tc}")
            for r in psg + psu:
                PS.free(r)
        for ib in range(n_ib):
            wd_r = wdb[ib]
            ap_d, bd = wd_src(ib)
            P.dma("pool", lambda e, o=wd_r.view(BF16, (128, 2048)), i=ap_d:
                  e.dma_start(out=o, in_=i),
                  reads=[bd], writes=[wd_r.buf], tag=f"ldwd{tag}{su}{ib}")
        for tt in range(n_tok // 128):
            banks = [PS.alloc(f"eo{su}{tt}{hc}") for hc in range(4)]
            pss = [b.view(F32, (128, 512)) for b in banks]

            def em_dn(e, pss=pss, tt=tt, n_ib=n_ib, h_ap=h_ap, wdb=wdb):
                first = None
                for ib in range(n_ib):
                    wd_ap = wdb[ib].view(BF16, (128, 2048))
                    for hc in range(4):
                        last = e.matmul(pss[hc], h_ap[:, ib, tt * 128:(tt + 1) * 128],
                                        wd_ap[:, hc * 512:(hc + 1) * 512],
                                        start=(ib == 0), stop=(ib == n_ib - 1))
                        first = first or last
                return (first, last)
            P.op("pe", em_dn, reads=[hb.buf] + [r.buf for r in wdb],
                 writes=[b.buf for b in banks], tag=f"dn{tag}{su}{tt}")
            emit_out(tt, banks, pss)
            for b in banks:
                PS.free(b)
        for r in wdb:
            A.free(r)

    for k in range(2):
        def em_out(tt, banks, pss, k=k):
            for hc in range(4):
                P.op("dve", lambda e, ps=pss[hc], o=eo_ap[:, hc * 512:(hc + 1) * 512]:
                     e.tensor_copy(o, ps),
                     reads=[banks[hc].buf], writes=[eo.buf], tag=f"soev{tag}{k}{tt}{hc}")
            if k == 0:
                P.dma("sp", lambda e, tt=tt: e.dma_start(
                    out=y_dram[tt * 128:(tt + 1) * 128, :], in_=eo_ap),
                    reads=[eo.buf], writes=[y_dram_b], tag=f"yw{tag}{tt}")
            else:
                P.dma("pool", lambda e, tt=tt: e.dma_start(
                    out=y_dram[tt * 128:(tt + 1) * 128, :], in_=eo_ap, accum_op=ALU.add),
                    reads=[eo.buf, y_dram_b], writes=[y_dram_b], tag=f"ya{tag}{tt}")
        expert(NI, TCORE,
               lambda ib, k=k: (sgl[k * NI + ib], sgl_b),
               lambda ib, k=k: (sul[k * NI + ib], sul_b),
               lambda ib, k=k: (sdl[k * NI + ib], sdl_b),
               xn_fm, xn_fm_r.buf, em_out, su=f"sh{k}")

    for e_i in range(NE):
        idx_sb = idxw.view(mybir.dt.int32, (128, 4))
        w_sb = idxw.view(F32, (128, 4), byte_off=16)
        P.dma("sp", lambda e, e_i=e_i, idx_sb=idx_sb: e.dma_start(
            out=idx_sb, in_=idxd[e_i][0].rearrange("(st p) one -> p (st one)", p=128),
            allow_slow_non_contiguous=True),
            reads=[idxd[e_i][1]], writes=[idxw.buf], tag=f"ldix{tag}{e_i}")
        P.dma("sp", lambda e, e_i=e_i, w_sb=w_sb: e.dma_start(
            out=w_sb, in_=wvd[e_i][0].rearrange("(st p) one -> p (st one)", p=128),
            allow_slow_non_contiguous=True),
            reads=[wvd[e_i][1]], writes=[idxw.buf], tag=f"ldwv{tag}{e_i}")
        for st in range(4):
            P.dma("pool", lambda e, st=st, idx_sb=idx_sb: e.indirect_dma_start(
                out=xng_t.view(BF16, (128, 2048)), out_offset=None,
                in_=xn_tm_dram, in_offset=IndirectOffsetOnAxis(idx_sb[:, st:st + 1], 0),
                bounds_check=TCORE - 1, oob_is_err=False),
                reads=[xn_tm_dram_b, idxw.buf, xng_t.buf], writes=[xng_t.buf],
                tag=f"gx{tag}{e_i}{st}")
            xt = xng_t.view(BF16, (128, 2048))
            for g in range(4):
                bank = PS.alloc(f"xT{tag}")
                bap = bank.view(BF16, (128, 4, 128))

                def em_tr(e, bap=bap, g=g, xt=xt):
                    first = None
                    for j in range(4):
                        cc = g * 4 + j
                        last = e.transpose(bap[:, j, :], xt[:, cc * 128:(cc + 1) * 128], ident)
                        first = first or last
                    return (first, last)
                P.op("pe", em_tr, reads=[xng_t.buf, ident_b], writes=[bank.buf],
                     tag=f"xtr{tag}{e_i}{st}{g}")
                dst = xng_ap[:, g * 4:(g + 1) * 4, st * 128:(st + 1) * 128]
                P.op("dve", lambda e, bap=bap, dst=dst: e.tensor_copy(dst, bap),
                     reads=[bank.buf], writes=[xng.buf], tag=f"xtc{tag}{e_i}{st}{g}")
                PS.free(bank)

        def em_out(st, banks, pss, e_i=e_i, idx_sb=idx_sb, w_sb=w_sb):
            for hc in range(4):
                P.op("dve", lambda e, ps=pss[hc], o=eo_ap[:, hc * 512:(hc + 1) * 512],
                     w_sb=w_sb, st=st:
                     e.tensor_scalar(o, ps, w_sb[:, st:st + 1], None, ALU.mult),
                     reads=[banks[hc].buf, idxw.buf], writes=[eo.buf],
                     tag=f"eoev{tag}{e_i}{st}{hc}")
            P.dma("pool", lambda e, st=st, idx_sb=idx_sb: e.indirect_dma_start(
                out=y_dram, out_offset=IndirectOffsetOnAxis(idx_sb[:, st:st + 1], 0),
                in_=eo_ap, in_offset=None, bounds_check=TCORE - 1, oob_is_err=False,
                compute_op=ALU.add),
                reads=[eo.buf, idxw.buf, y_dram_b], writes=[y_dram_b],
                tag=f"ysc{tag}{e_i}{st}")
        expert(NI, CAP,
               lambda ib, e_i=e_i: (wgl[e_i, ib], wgl_b),
               lambda ib, e_i=e_i: (wul[e_i, ib], wul_b),
               lambda ib, e_i=e_i: (wdl[e_i, ib], wdl_b),
               xng_ap, xng.buf, em_out, su=f"e{e_i}")

    for r in wgb + wub + [hb, sg, eo, idxw, xng_t, xng, comb, slotc]:
        A.free(r)

    xs = [A.alloc(f"rx{tag}{i}", 2048 * 4) for i in range(2)]
    ys = [A.alloc(f"ry{tag}{i}", 2048 * 4) for i in range(2)]
    for tt in range(NTT):
        xb, yb = xs[tt % 2], ys[tt % 2]
        x_ap = xb.view(F32, (128, 2048))
        y_ap = yb.view(F32, (128, 2048))
        P.dma("sp", lambda e, o=x_ap, i=x_dram_ap[tt * 128:(tt + 1) * 128, :]:
              e.dma_start(out=o, in_=i), reads=[tile_buf(x_dram_buf, tt)],
              writes=[xb.buf], tag=f"res_ldx{tag}{tt}")
        P.dma("sp", lambda e, o=y_ap, i=y_dram[tt * 128:(tt + 1) * 128, :]:
              e.dma_start(out=o, in_=i), reads=[y_dram_b], writes=[yb.buf], tag=f"res_ldy{tag}{tt}")
        P.op("dve", lambda e, x_ap=x_ap, y_ap=y_ap: e.tensor_tensor(x_ap, x_ap, y_ap, ALU.add),
             reads=[xb.buf, yb.buf], writes=[xb.buf], tag=f"res_add{tag}{tt}")
        P.dma("sp", lambda e, o=out_dram_ap[tt * 128:(tt + 1) * 128, :], i=x_ap:
              e.dma_start(out=o, in_=i), reads=[xb.buf],
              writes=[tile_buf(out_dram_buf, tt)], tag=f"res_st{tag}{tt}")
    for r in xs + ys:
        A.free(r)




import contextlib
import time


def _build_program():
    nc = bass.Bass("TRN2", target_bir_lowering=False, debug=False, num_devices=8)
    P = Prog()
    es = contextlib.ExitStack()
    A = SbufArena(nc, es, "arena", 205 * 1024)
    PS = PsumPool(nc, es)
    C = Ctx(nc, P, A, PS)

    x_in, x_in_b = C.dram_in("x", [1024, 2048])
    x_out, x_out_b = C.dram_out("xo", [1024, 2048])
    xb1, xb1_b0 = C.dram_tmp("xbuf", [1024, 2048])
    xb1_b = [Buf(f"xbuf_t{t}") for t in range(8)]
    ident_in, ident_in_b = C.dram_in("ident", [128, 128], BF16)
    cv_in, cv_in_b = C.dram_in("cvals", [128, len(Ctx.CONST_VALS)])
    ma_in, ma_in_b = C.dram_in("masks_a", [128, 4, 512], BF16)
    mb_in, mb_in_b = C.dram_in("masks_b", [128, 8, 512], BF16)
    oc_in, oc_in_b = C.dram_in("ones_col", [128, 1], BF16)
    of_in, of_in_b = C.dram_in("onesf", [1, 1])
    gwt_in, gwt_in_b = C.dram_in("gate_wTl", [128, NCC, NE])

    layer_a = {
        "qT": C.dram_in("waQT", [128, NCC, 256]),
        "kT": C.dram_in("waKT", [128, NCC, 256]),
        "vT": C.dram_in("waVT", [128, NCC, 256]),
        "oT": C.dram_in("waOT", [4, 128, NCC, 512]),
    }
    layer_b = {
        "qT": C.dram_in("wbQT", [128, NCC, 256]),
        "kT": C.dram_in("wbKT", [128, NCC, 256]),
        "vT": C.dram_in("wbVT", [128, NCC, 256]),
        "vNat": C.dram_in("wbVnat", [128, 2, NCC, 128]),
        "oT": C.dram_in("wbOT", [4, 128, NCC, 512]),
        "pkT": C.dram_in("pkT", [128, 64]),
        "pvT": C.dram_in("pvT", [128, 64]),
        "uoT": C.dram_in("uoT", [64, 128]),
        "w1T": C.dram_in("w1T", [128, NCC, 512]),
        "w2T": C.dram_in("w2T", [128, 4, 1]),
        "rope_cos": C.dram_in("rope_cos", [16, 128, 4, 128], BF16),
        "rope_ss": C.dram_in("rope_ss", [16, 128, 4, 128], BF16),
    }
    layer_f = ffn_weight_inputs(C)

    C.setup_const_vals(cv_in, cv_in_b)
    C.load_const("ident", ident_in, ident_in_b, (128, 128), BF16, queue="sp")
    C.load_const("masks_a", ma_in, ma_in_b, (128, 4, 512), BF16, queue="sp")
    C.load_const("masks_b", mb_in, mb_in_b, (128, 8, 512), BF16, queue="sp")
    C.load_const("ones_col", oc_in, oc_in_b, (128, 1), BF16, queue="sp")
    C.load_const("onesf", of_in, of_in_b, (1, 1), F32, queue="sp")
    C.load_const("gate_wT", gwt_in, gwt_in_b, (128, NCC, NE), BF16, queue="pool")
    C.prologue_barrier([C.cval_buf] + [C.const[k][0].buf for k in
                                       ("ident", "masks_a", "masks_b", "ones_col",
                                        "onesf", "gate_wT")])
    grp = list(range(8))

    wgb1 = [A.alloc(f"wgf1{i}", NCC * 128 * 2) for i in range(3)]
    wub1 = [A.alloc(f"wuf1{i}", NCC * 128 * 2) for i in range(3)]
    wgl_, wgl_b_ = layer_f["wg"]
    wul_, wul_b_ = layer_f["wu"]
    for ib in range(3):
        dg = wgb1[ib].view(BF16, (128, NCC, 128))
        du = wub1[ib].view(BF16, (128, NCC, 128))
        P.dma("pool", lambda e, o=dg, i=wgl_[0, ib]: e.dma_start(out=o, in_=i),
              reads=[wgl_b_], writes=[wgb1[ib].buf], tag=f"ldwgf1e0{ib}")
        P.dma("pool", lambda e, o=du, i=wul_[0, ib]: e.dma_start(out=o, in_=i),
              reads=[wul_b_], writes=[wub1[ib].buf], tag=f"ldwuf1e0{ib}")

    xn = A.alloc("xn1", NCC * TCORE * 2)
    build_ln(C, x_in, x_in_b, xn, tag="l1")
    build_attn_a(C, layer_a, xn, x_in, x_in_b, xb1, xb1_b, core_grp=grp, tag="a")
    xn = A.alloc("xn2", NCC * TCORE * 2)
    build_ln(C, xb1, xb1_b, xn, tag="l2")
    build_ffn(C, layer_f, xn, xb1, xb1_b, xb1, xb1_b, tag="f1", pre=(wgb1, wub1, 3))
    A.free(xn)
    xn = A.alloc("xn3", NCC * TCORE * 2)
    build_ln(C, xb1, xb1_b, xn, tag="l3")
    build_attn_b(C, layer_b, xn, xb1, xb1_b, xb1, xb1_b, core_grp=grp, tag="b")
    xn = A.alloc("xn4", NCC * TCORE * 2)
    build_ln(C, xb1, xb1_b, xn, tag="l4")
    build_ffn(C, layer_f, xn, xb1, xb1_b, x_out, x_out_b, tag="f2")
    A.free(xn)

    stores = [o for o in P.ops if o.tag.startswith("res_stf2")]
    P.op("sp", lambda e: e.nop(), extra_deps=stores, tag="final")
    P.emit(nc, es)
    return nc, es


def _host_layouts(ins):
    import ml_dtypes
    out = ffn_weight_arrays(ins)

    def sliceT(W, c):
        s = W[256 * c:256 * (c + 1), :]
        return np.ascontiguousarray(s.T.reshape(NCC, 128, 256).transpose(1, 0, 2))

    out["waOT"] = np.ascontiguousarray(
        ins["L0_attn_a_Wo"].T.reshape(NCC, 128, 4, 512).transpose(2, 1, 0, 3))
    out["wbOT"] = np.ascontiguousarray(
        ins["L0_attn_b_w_o"].T.reshape(NCC, 128, 4, 512).transpose(2, 1, 0, 3))
    out["w1T"] = np.ascontiguousarray(
        ins["L0_attn_b_info_w1"].T.reshape(NCC, 128, 512).transpose(1, 0, 2))
    out["w2T"] = np.ascontiguousarray(
        ins["L0_attn_b_info_w2"].T.reshape(4, 128, 1).transpose(1, 0, 2))
    out["pkT"] = np.ascontiguousarray(ins["L0_attn_b_pk"].T)
    out["pvT"] = np.ascontiguousarray(ins["L0_attn_b_pv"].T)
    out["uoT"] = np.ascontiguousarray(ins["L0_attn_b_uo"].T)

    D_ = 128
    invf = (1.0 / 10000.0 ** (np.arange(0, D_, 2, dtype=np.float32) / np.float32(D_))
            ).astype(np.float32)
    t = np.arange(8192, dtype=np.float32)
    emb = np.concatenate([np.outer(t, invf), np.outer(t, invf)], axis=1).astype(np.float32)
    cos = np.cos(emb).astype(ml_dtypes.bfloat16)
    sign = np.where(np.arange(D_) % 2 == 0, -1.0, 1.0).astype(np.float32)
    ss = (np.sin(emb) * sign[None, :]).astype(ml_dtypes.bfloat16)
    out["rope_cos"] = np.ascontiguousarray(cos.reshape(16, 4, 128, 128).transpose(0, 2, 1, 3))
    out["rope_ss"] = np.ascontiguousarray(ss.reshape(16, 4, 128, 128).transpose(0, 2, 1, 3))

    masks_a = np.zeros((128, 4, 512), dtype=ml_dtypes.bfloat16)
    for j in range(4):
        for k in range(128):
            masks_a[k, j, :] = (np.arange(512) >= (128 * j + k)).astype(ml_dtypes.bfloat16)
    masks_b = np.zeros((128, 8, 512), dtype=ml_dtypes.bfloat16)
    for j in range(8):
        for k in range(128):
            rel = (j - 4) * 128 + k - np.arange(512)
            masks_b[k, j, :] = ((rel <= 0) & (rel >= -511)).astype(ml_dtypes.bfloat16)
    out["masks_a"] = masks_a
    out["masks_b"] = masks_b
    out["ident"] = np.eye(128, dtype=ml_dtypes.bfloat16)
    out["cvals"] = np.tile(np.array([Ctx.CONST_VALS], dtype=np.float32), (128, 1))
    out["ones_col"] = np.ones((128, 1), dtype=ml_dtypes.bfloat16)
    out["onesf"] = np.ones((1, 1), dtype=np.float32)

    per_core = {}
    for c in range(8):
        wqkv = ins["L0_attn_b_w_qkv"]
        vnat = wqkv[4096 + 256 * c:4096 + 256 * (c + 1), :]
        per_core[c] = dict(
            waQT=sliceT(ins["L0_attn_a_Wq"], c),
            waKT=sliceT(ins["L0_attn_a_Wk"], c),
            waVT=sliceT(ins["L0_attn_a_Wv"], c),
            wbQT=np.ascontiguousarray(
                wqkv[256 * c:256 * (c + 1), :].T.reshape(NCC, 128, 256).transpose(1, 0, 2)),
            wbKT=np.ascontiguousarray(
                wqkv[2048 + 256 * c:2048 + 256 * (c + 1), :].T.reshape(NCC, 128, 256).transpose(1, 0, 2)),
            wbVT=np.ascontiguousarray(
                wqkv[4096 + 256 * c:4096 + 256 * (c + 1), :].T.reshape(NCC, 128, 256).transpose(1, 0, 2)),
            wbVnat=np.ascontiguousarray(
                vnat.reshape(2, 128, NCC, 128).transpose(1, 0, 2, 3)),
        )
    return out, per_core


_CACHE = {}


def kernel(**inputs):
    from concourse.bass_utils import run_bass_kernel_spmd
    x_full = np.asarray(inputs["L0_attn_a_x"]).reshape(8192, 2048)
    shared, per_core = _host_layouts(inputs)
    if "prog" not in _CACHE:
        _CACHE["prog"] = _build_program()
    nc, es = _CACHE["prog"]
    in_maps = []
    for c in range(8):
        m = dict(shared)
        m["gate_wTl"] = shared["gate_wT"]
        del m["gate_wT"]
        m.update(per_core[c])
        m["x"] = np.ascontiguousarray(x_full[c * 1024:(c + 1) * 1024])
        in_maps.append(m)
    res = run_bass_kernel_spmd(nc, in_maps, core_ids=list(range(8)))
    out = np.concatenate([res.results[c]["xo"] for c in range(8)], axis=0)
    return out.reshape(1, 8192, 2048)


# version 21
# speedup vs baseline: 1.0141x; 1.0141x over previous
import numpy as np
import concourse.bass as bass
import concourse.mybir as mybir

F32 = mybir.dt.float32
BF16 = mybir.dt.bfloat16
U8 = mybir.dt.uint8
AF = mybir.ActivationFunctionType
ALU = mybir.AluOpType
AX = mybir.AxisListType

ENGINES = ("pe", "act", "dve", "pool", "sp")
BLOCK_ATTR = {"pe": "tensor", "act": "scalar", "dve": "vector", "pool": "gpsimd", "sp": "sync"}
DMA_SLOTS = 10

_ESIZE = {F32: 4, BF16: 2, U8: 1, mybir.dt.int32: 4}


class Buf:
    __slots__ = ("name", "last_w", "readers", "pending")

    def __init__(self, name):
        self.name = name
        self.last_w = None
        self.readers = []
        self.pending = []

    def __repr__(self):
        return f"Buf({self.name})"


class Op:
    __slots__ = ("idx", "eng", "emit", "deps", "sem_key", "amount", "cum", "tag",
                 "self_inc")

    def __init__(self, idx, eng, emit, deps, sem_key, amount, tag):
        self.idx = idx
        self.eng = eng
        self.emit = emit
        self.deps = deps
        self.sem_key = sem_key
        self.amount = amount
        self.cum = None
        self.tag = tag
        self.self_inc = False


class Prog:
    def __init__(self):
        self.ops = []
        self.eng_ops = {e: [] for e in ENGINES}
        self.dma_count = {}
        self.slot_last = {}
        self.sem_keys = set(("eng", e) for e in ENGINES)
        self.n_waits = 0

    def op(self, eng, emit, reads=(), writes=(), tag="", extra_deps=()):
        return self._record(eng, emit, reads, writes, ("eng", eng), 1, tag,
                            extra_deps=extra_deps)

    def dma(self, queue, emit, reads=(), writes=(), tag=""):
        i = self.dma_count.get(queue, 0)
        self.dma_count[queue] = i + 1
        slot = i % DMA_SLOTS
        key = ("dma", queue, slot)
        self.sem_keys.add(key)
        o = self._record(queue, emit, reads, writes, key, 16, tag,
                         extra_dep=self.slot_last.get((queue, slot)))
        self.slot_last[(queue, slot)] = o
        return o

    def dma_multi(self, queue, emit, n_dma, reads=(), writes=(), tag=""):
        i = self.dma_count.get(queue, 0)
        self.dma_count[queue] = i + 1
        slot = i % DMA_SLOTS
        key = ("dma", queue, slot)
        self.sem_keys.add(key)
        o = self._record(queue, emit, reads, writes, key, 16 * n_dma, tag,
                         extra_dep=self.slot_last.get((queue, slot)))
        o.self_inc = True
        self.slot_last[(queue, slot)] = o
        return o

    def collective(self, emit, reads=(), writes=(), tag="cc"):
        key = ("cc",)
        self.sem_keys.add(key)
        return self._record("pool", emit, reads, writes, key, 1, tag)

    def _record(self, eng, emit, reads, writes, sem_key, amount, tag, extra_dep=None,
                extra_deps=()):
        deps = set()
        for b in reads:
            if b.last_w is not None:
                deps.add(b.last_w)
            deps.update(b.pending)
        for b in writes:
            if b.last_w is not None:
                deps.add(b.last_w)
            deps.update(b.readers)
            deps.update(b.pending)
        if extra_dep is not None:
            deps.add(extra_dep)
        deps.update(extra_deps)
        o = Op(len(self.ops), eng, emit, deps, sem_key, amount, tag)
        self.ops.append(o)
        self.eng_ops[eng].append(o)
        for b in reads:
            b.readers.append(o)
        for b in writes:
            b.last_w = o
            b.readers = []
            b.pending = []
        return o

    def emit(self, nc, es):
        sems = {}
        for k in sorted(self.sem_keys, key=repr):
            sems[k] = es.enter_context(nc.semaphore("s" + "_".join(map(str, k))))
        run = {k: 0 for k in self.sem_keys}
        for o in self.ops:
            run[o.sem_key] += o.amount
            o.cum = run[o.sem_key]
        self.final_counts = dict(run)
        self.sems = sems

        block = es.enter_context(nc.Block())
        for eng_name in ENGINES:
            ops = self.eng_ops[eng_name]

            def body(engobj, ops=ops, eng_name=eng_name):
                pe_key = ("eng", "pe")
                hw = {}
                for o in ops:
                    waits = {}
                    for d in o.deps:
                        if eng_name == "pe" and d.sem_key == pe_key:
                            continue
                        if waits.get(d.sem_key, 0) < d.cum:
                            waits[d.sem_key] = d.cum
                    todo = [(k, v) for k, v in sorted(waits.items(), key=lambda kv: repr(kv[0]))
                            if hw.get(k, 0) < v]
                    if o.self_inc:
                        for k, v in todo:
                            engobj.wait_ge(sems[k], v)
                            hw[k] = v
                            self.n_waits += 1
                        o.emit(engobj, sems[o.sem_key])
                        continue
                    for k, v in todo[:-1]:
                        engobj.wait_ge(sems[k], v)
                        hw[k] = v
                        self.n_waits += 1
                    res = o.emit(engobj)
                    assert res is not None, f"op {o.tag} emit returned None"
                    first, last = res if isinstance(res, tuple) else (res, res)
                    if todo:
                        k, v = todo[-1]
                        first._wait_ge(sems[k], v)
                        hw[k] = v
                        self.n_waits += 1
                    last.then_inc(sems[o.sem_key], o.amount)

            getattr(block, BLOCK_ATTR[eng_name])(body)
        return sems


class Region:
    __slots__ = ("arena", "off", "size", "buf", "freed", "tile_bufs")

    def __init__(self, arena, off, size, name):
        self.arena = arena
        self.off = off
        self.size = size
        self.buf = Buf(name)
        self.freed = False

    def view(self, dtype, shape, byte_off=0):
        es = _ESIZE[dtype]
        n = int(np.prod(shape[1:])) * es
        assert byte_off + n <= self.size, f"{self.buf.name}: view {byte_off}+{n} > {self.size}"
        o = self.off + byte_off
        assert o % es == 0
        ap = self.arena.h.ap()[0:shape[0], o:o + n].bitcast(dtype)
        if len(shape) > 2:
            names = [f"d{i}" for i in range(len(shape) - 1)]
            ap = ap.rearrange(f"p ({' '.join(names)}) -> p {' '.join(names)}",
                              **{nm: shape[i + 1] for i, nm in enumerate(names)})
        return ap


class SbufArena:
    def __init__(self, nc, es, name, bytes_per_partition):
        self.h = es.enter_context(nc.sbuf_tensor(name, [128, bytes_per_partition], U8))
        self.cap = bytes_per_partition
        self.free_ranges = [(0, bytes_per_partition)]
        self.ghosts = []
        self.peak = 0

    def alloc(self, name, nbytes, align=64):
        nbytes = (nbytes + align - 1) // align * align
        for i, (off, size) in enumerate(self.free_ranges):
            aoff = (off + align - 1) // align * align
            if aoff + nbytes <= off + size:
                pre = (off, aoff - off)
                post = (aoff + nbytes, off + size - aoff - nbytes)
                repl = [r for r in (pre, post) if r[1] > 0]
                self.free_ranges[i:i + 1] = repl
                r = Region(self, aoff, nbytes, name)
                keep = []
                for g_off, g_size, g_deps in self.ghosts:
                    if g_off < aoff + nbytes and aoff < g_off + g_size:
                        r.buf.pending.extend(g_deps)
                        if g_off < aoff:
                            keep.append((g_off, aoff - g_off, g_deps))
                        if g_off + g_size > aoff + nbytes:
                            keep.append((aoff + nbytes, g_off + g_size - aoff - nbytes, g_deps))
                    else:
                        keep.append((g_off, g_size, g_deps))
                self.ghosts = keep
                self.peak = max(self.peak, self.used())
                return r
        raise MemoryError(f"SBUF arena: no room for {name} ({nbytes}B/part); free={self.free_ranges}")

    def free(self, region):
        assert not region.freed
        region.freed = True
        deps = list(region.buf.readers)
        if region.buf.last_w is not None:
            deps.append(region.buf.last_w)
        deps.extend(region.buf.pending)
        if deps:
            self.ghosts.append((region.off, region.size, deps))
        self.free_ranges.append((region.off, region.size))
        self.free_ranges.sort()
        merged = []
        for off, size in self.free_ranges:
            if merged and merged[-1][0] + merged[-1][1] == off:
                merged[-1] = (merged[-1][0], merged[-1][1] + size)
            else:
                merged.append((off, size))
        self.free_ranges = [tuple(x) for x in merged]

    def used(self):
        return self.cap - sum(s for _, s in self.free_ranges)


class PsumPool:

    def __init__(self, nc, es):
        self.h = es.enter_context(nc.psum_tensor("ps", [128, 8 * 512], F32))
        self.free_banks = list(range(8))
        self.ghosts = {}

    def alloc(self, name):
        assert self.free_banks, f"psum pool empty allocating {name}"
        b = self.free_banks.pop(0)
        r = PsumRegion(self, b, name)
        r.buf.pending.extend(self.ghosts.pop(b, []))
        return r

    def free(self, region):
        assert not region.freed
        region.freed = True
        deps = list(region.buf.readers)
        if region.buf.last_w is not None:
            deps.append(region.buf.last_w)
        deps.extend(region.buf.pending)
        self.ghosts[region.bank] = deps
        self.free_banks.append(region.bank)


class PsumRegion:
    __slots__ = ("pool", "bank", "buf", "freed")

    def __init__(self, pool, bank, name):
        self.pool = pool
        self.bank = bank
        self.buf = Buf(name)
        self.freed = False

    def view(self, dtype=F32, shape=(128, 512)):
        es = _ESIZE[dtype]
        n = int(np.prod(shape[1:]))
        assert n * es <= 2048
        ap = self.pool.h.ap()[0:shape[0], self.bank * 512:self.bank * 512 + (n * es) // 4]
        if dtype != F32:
            ap = ap.bitcast(dtype)
        if len(shape) > 2:
            names = [f"d{i}" for i in range(len(shape) - 1)]
            ap = ap.rearrange(f"p ({' '.join(names)}) -> p {' '.join(names)}",
                              **{nm: shape[i + 1] for i, nm in enumerate(names)})
        return ap


"""Phase builders for the mk_hybrid_d1 megakernel. All sizes per core.

Layouts:
  x (residual): DRAM [1024, 2048] f32, token-major, updated in place.
  xn_fm: SBUF [128, 16cc, 1024t] bf16 (feature-major; partition = c within cc).
  token-major tiles: [128 t, free] with 8 ttiles of 128 tokens.
"""
import numpy as np
bass_ts = bass.ts
IndirectOffsetOnAxis = bass.IndirectOffsetOnAxis


def tile_buf(b, t):
    return b[t] if isinstance(b, (list, tuple)) else b

TCORE = 1024
NTT = TCORE // 128
NCC = 16
D = 2048
NI = 11
NIS = 22
NE = 8


def ceil_div(a, b):
    return (a + b - 1) // b


class Ctx:

    def __init__(self, nc, prog, arena, psum):
        self.nc = nc
        self.P = prog
        self.A = arena
        self.PS = psum
        self.const = {}
        self.dram = {}

    def dram_in(self, name, shape, dtype=F32):
        h = self.nc.dram_tensor(name, list(shape), dtype, kind="ExternalInput")
        b = Buf("dram_" + name)
        self.dram[name] = (h, b)
        return h.ap(), b

    def dram_out(self, name, shape, dtype=F32):
        h = self.nc.dram_tensor(name, list(shape), dtype, kind="ExternalOutput")
        b = Buf("dram_" + name)
        self.dram[name] = (h, b)
        return h.ap(), b

    def dram_tmp(self, name, shape, dtype=F32, addr_space="Local"):
        h = self.nc.dram_tensor(name, list(shape), dtype, kind="Internal", addr_space=addr_space)
        b = Buf("dramtmp_" + name)
        self.dram[name] = (h, b)
        return h.ap(), b

    CONST_VALS = (1e-5, -1e30, 0.08838834764831845, 0.17677669529663687)

    def setup_const_vals(self, src_ap, src_buf):
        n = len(self.CONST_VALS)
        r = self.A.alloc("cvals", n * 4)
        ap = r.view(F32, (128, n))
        self.P.dma("sp", lambda e, o=ap, i=src_ap: e.dma_start(out=o, in_=i),
                   reads=[src_buf], writes=[r.buf], tag="ld_cvals")
        for j, v in enumerate(self.CONST_VALS):
            self.nc.const_aps.register(F32, v, ap[:, j:j + 1])
        self.cval_buf = r.buf

    def prologue_barrier(self, bufs):
        for eng in ENGINES:
            self.P.op(eng, lambda e: e.nop(), reads=list(bufs), tag=f"prolog_{eng}")

    def load_const(self, name, src_ap, src_buf, shape, dtype, queue="sp"):
        nbytes = int(np.prod(shape[1:])) * _ESIZE[dtype]
        r = self.A.alloc("const_" + name, nbytes)
        ap = r.view(dtype, shape)
        self.P.dma(queue, lambda e, o=ap, i=src_ap: e.dma_start(out=o, in_=i),
                   reads=[src_buf], writes=[r.buf], tag="ldconst_" + name)
        self.const[name] = (r, ap)
        return ap, r.buf


def build_ln(C, x_dram_ap, x_dram_buf, xn_fm_r, tag="", ldq=("sp",)):
    P, A, PS = C.P, C.A, C.PS
    ident = C.const["ident"][1]
    ident_b = C.const["ident"][0].buf

    xs = [A.alloc(f"lnx{tag}{i}", 2048 * 4) for i in range(2)]
    sc = A.alloc(f"lnsc{tag}", 32 * 4)
    xn_tm = [A.alloc(f"lnxn{tag}{i}", 2048 * 2) for i in range(2)]
    xn_fm_ap = xn_fm_r.view(BF16, (128, NCC, TCORE))

    for tt in range(NTT):
        xb = xs[tt % 2]
        x_ap = xb.view(F32, (128, 2048))
        P.dma(ldq[tt % len(ldq)], lambda e, o=x_ap, i=x_dram_ap[tt * 128:(tt + 1) * 128, :]:
              e.dma_start(out=o, in_=i), reads=[tile_buf(x_dram_buf, tt)],
              writes=[xb.buf], tag=f"ln_ldx{tag}{tt}")
        st = sc.view(F32, (128, 24))
        mv = sc.view(F32, (128, 2), byte_off=96)
        rstd = sc.view(F32, (128, 1), byte_off=104)

        def em_stats(e, x_ap=x_ap, st=st):
            first = None
            for j in range(4):
                last = e.bn_stats(st[:, j * 6:(j + 1) * 6], x_ap[:, j * 512:(j + 1) * 512])
                first = first or last
            return (first, last)
        P.op("dve", em_stats, reads=[xb.buf], writes=[sc.buf], tag=f"ln_stats{tag}{tt}")
        P.op("dve", lambda e, st=st, mv=mv: e.bn_aggr(mv, st),
             reads=[sc.buf], writes=[sc.buf], tag=f"ln_aggr{tag}{tt}")
        P.op("act", lambda e, mv=mv, rstd=rstd: e.activation(rstd, mv[:, 1:2], AF.Sqrt, bias=1e-5),
             reads=[sc.buf], writes=[sc.buf], tag=f"ln_sqrt{tag}{tt}")
        P.op("dve", lambda e, rstd=rstd: e.reciprocal(rstd, rstd),
             reads=[sc.buf], writes=[sc.buf], tag=f"ln_recip{tag}{tt}")
        xnb = xn_tm[tt % 2]
        xn_ap = xnb.view(BF16, (128, 2048))
        P.op("dve", lambda e, x_ap=x_ap, xn_ap=xn_ap, mv=mv, rstd=rstd:
             e.tensor_scalar(xn_ap, x_ap, mv[:, 0:1], rstd[:, 0:1], ALU.subtract, ALU.mult),
             reads=[xb.buf, sc.buf], writes=[xnb.buf], tag=f"ln_norm{tag}{tt}")
        for g in range(4):
            bank = PS.alloc(f"lnT{tag}{tt}{g}")
            bap = bank.view(BF16, (128, 4, 128))

            def em_tr(e, xn_ap=xn_ap, bap=bap, g=g, ident=ident):
                first = None
                for j in range(4):
                    cc = g * 4 + j
                    last = e.transpose(bap[:, j, :], xn_ap[:, cc * 128:(cc + 1) * 128], ident)
                    first = first or last
                return (first, last)
            P.op("pe", em_tr, reads=[xnb.buf, ident_b], writes=[bank.buf], tag=f"ln_tr{tag}{tt}{g}")
            dst = xn_fm_ap[:, g * 4:(g + 1) * 4, tt * 128:(tt + 1) * 128]
            P.op("dve", lambda e, bap=bap, dst=dst: e.tensor_copy(dst, bap),
                 reads=[bank.buf], writes=[xn_fm_r.buf], tag=f"ln_trc{tag}{tt}{g}")
            PS.free(bank)
    for b in xs + xn_tm + [sc]:
        A.free(b)


def build_ffn(C, layer, xn_fm_r, x_dram_ap, x_dram_buf, out_dram_ap, out_dram_buf, tag="",
              pre=None):
    P, A, PS = C.P, C.A, C.PS
    nc = C.nc
    xn_fm = xn_fm_r.view(BF16, (128, NCC, TCORE))
    w = layer

    gw_ap = C.const["gate_wT"][1]
    gw_b = C.const["gate_wT"][0].buf
    comb = A.alloc(f"comb{tag}", NTT * NE * 4)
    comb_ap = comb.view(F32, (128, NTT, NE))
    rsc = A.alloc(f"rsc{tag}", 64 * 4)
    for tt in range(NTT):
        bank = PS.alloc(f"rt{tag}{tt}")
        ps = bank.view(F32, (128, NE))

        def em_rt(e, ps=ps, tt=tt):
            first = None
            for cc in range(NCC):
                last = e.matmul(ps, xn_fm[:, cc, tt * 128:(tt + 1) * 128], gw_ap[:, cc, :],
                                start=(cc == 0), stop=(cc == NCC - 1))
                first = first or last
            return (first, last)
        P.op("pe", em_rt, reads=[xn_fm_r.buf, gw_b], writes=[bank.buf], tag=f"rt_mm{tag}{tt}")
        ex = rsc.view(F32, (128, NE))
        s1 = rsc.view(F32, (128, 1), byte_off=32)
        sc_v = rsc.view(F32, (128, NE), byte_off=48)
        m1 = rsc.view(F32, (128, 1), byte_off=96)
        msk = rsc.view(F32, (128, NE), byte_off=112)
        m2 = rsc.view(F32, (128, 1), byte_off=160)
        ge = rsc.view(F32, (128, NE), byte_off=176)
        b = rsc.buf
        cv = comb_ap[:, tt, :]
        P.op("act", lambda e, ps=ps, ex=ex: e.activation(ex, ps, AF.Exp),
             reads=[bank.buf], writes=[b], tag=f"rt_exp{tag}{tt}")
        P.op("dve", lambda e, ex=ex, s1=s1: e.reduce_sum(s1, ex, axis=AX.X), reads=[b], writes=[b])
        P.op("dve", lambda e, s1=s1: e.reciprocal(s1, s1), reads=[b], writes=[b])
        P.op("dve", lambda e, ex=ex, s1=s1, sc_v=sc_v: e.tensor_scalar(sc_v, ex, s1, None, ALU.mult),
             reads=[b], writes=[b])
        P.op("dve", lambda e, sc_v=sc_v, m1=m1: e.reduce_max(m1, sc_v, axis=AX.X), reads=[b], writes=[b])
        P.op("dve", lambda e, sc_v=sc_v, m1=m1, msk=msk: e.tensor_scalar(msk, sc_v, m1, None, ALU.is_equal),
             reads=[b], writes=[b])
        P.op("dve", lambda e, msk=msk, sc_v=sc_v: e.scalar_tensor_tensor(msk, msk, -1e30, sc_v, ALU.mult, ALU.add),
             reads=[b], writes=[b])
        P.op("dve", lambda e, msk=msk, m2=m2: e.reduce_max(m2, msk, axis=AX.X), reads=[b], writes=[b])
        P.op("dve", lambda e, sc_v=sc_v, m2=m2, ge=ge: e.tensor_scalar(ge, sc_v, m2, None, ALU.is_ge),
             reads=[b], writes=[b])
        P.op("dve", lambda e, sc_v=sc_v, ge=ge, cv=cv: e.tensor_mul(cv, sc_v, ge),
             reads=[b], writes=[comb.buf], tag=f"rt_comb{tag}{tt}")
        PS.free(bank)
    A.free(rsc)

    y = A.alloc(f"y{tag}", NTT * 2048 * 4)
    y_ap = y.view(F32, (128, NTT, 2048))
    P.op("dve", lambda e: e.memset(y_ap[:, :, :].rearrange("p a b -> p (a b)"), 0.0),
         writes=[y.buf], tag=f"y0{tag}")

    if pre is not None:
        wgb, wub, n_pre = pre
    else:
        wgb = [A.alloc(f"wg{tag}{i}", NCC * 128 * 2) for i in range(3)]
        wub = [A.alloc(f"wu{tag}{i}", NCC * 128 * 2) for i in range(3)]
        n_pre = 0
    hb = A.alloc(f"h{tag}", NI * TCORE * 2)
    sg = A.alloc(f"sg{tag}", 4 * 512 * 2)

    def expert(n_ib, wg_src, wu_src, wd_src, comb_scalar, h_r, su):
        h_ap = h_r.view(BF16, (128, n_ib, TCORE))
        wdb = [A.alloc(f"wd{tag}{su}{i}", 2048 * 2) for i in range(n_ib)]
        for ib in range(n_ib):
            wg_r = wgb[ib % 3]
            wu_r = wub[ib % 3]
            dstg = wg_r.view(BF16, (128, NCC, 128))
            dstu = wu_r.view(BF16, (128, NCC, 128))
            if not (su == "e0" and ib < n_pre):
                ap_g, bg = wg_src(ib)
                ap_u, bu = wu_src(ib)
                P.dma("pool", lambda e, o=dstg, i=ap_g: e.dma_start(out=o, in_=i),
                      reads=[bg], writes=[wg_r.buf], tag=f"ldwg{tag}{su}{ib}")
                P.dma("pool", lambda e, o=dstu, i=ap_u: e.dma_start(out=o, in_=i),
                      reads=[bu], writes=[wu_r.buf], tag=f"ldwu{tag}{su}{ib}")
            psg = [PS.alloc(f"g{su}{ib}{tc}") for tc in range(2)]
            psu = [PS.alloc(f"u{su}{ib}{tc}") for tc in range(2)]

            def em_gu(e, dstg=dstg, dstu=dstu, psg=psg, psu=psu):
                first = None
                for cc in range(NCC):
                    for tc in range(2):
                        i0 = e.matmul(psg[tc].view(F32, (128, 512)), dstg[:, cc, :],
                                      xn_fm[:, cc, tc * 512:(tc + 1) * 512],
                                      start=(cc == 0), stop=(cc == NCC - 1))
                        first = first or i0
                    for tc in range(2):
                        last = e.matmul(psu[tc].view(F32, (128, 512)), dstu[:, cc, :],
                                        xn_fm[:, cc, tc * 512:(tc + 1) * 512],
                                        start=(cc == 0), stop=(cc == NCC - 1))
                return (first, last)
            P.op("pe", em_gu, reads=[wg_r.buf, wu_r.buf, xn_fm_r.buf],
                 writes=[r.buf for r in psg + psu], tag=f"gu_mm{tag}{su}{ib}")
            for tc in range(2):
                sg_ap = sg.view(BF16, (128, 4, 512))[:, (ib * 2 + tc) % 4, :]
                P.op("act", lambda e, p=psg[tc].view(F32, (128, 512)), o=sg_ap:
                     e.activation(o, p, AF.Silu),
                     reads=[psg[tc].buf], writes=[sg.buf], tag=f"silu{tag}{su}{ib}{tc}")
                hdst = h_ap[:, ib, tc * 512:(tc + 1) * 512]
                P.op("dve", lambda e, p=psu[tc].view(F32, (128, 512)), s=sg_ap, o=hdst:
                     e.tensor_mul(o, p, s),
                     reads=[psu[tc].buf, sg.buf], writes=[h_r.buf], tag=f"hmul{tag}{su}{ib}{tc}")
            for r in psg + psu:
                PS.free(r)
        for ib in range(n_ib):
            wd_r = wdb[ib]
            wd_ap = wd_r.view(BF16, (128, 2048))
            ap_d, bd = wd_src(ib)
            P.dma("pool", lambda e, o=wd_ap, i=ap_d: e.dma_start(out=o, in_=i),
                  reads=[bd], writes=[wd_r.buf], tag=f"ldwd{tag}{su}{ib}")
        for tt in range(NTT):
            banks = [PS.alloc(f"eo{su}{tt}{hc}") for hc in range(4)]
            pss = [b.view(F32, (128, 512)) for b in banks]

            def em_dn(e, pss=pss, tt=tt, n_ib=n_ib, h_ap=h_ap):
                first = None
                for ib in range(n_ib):
                    wd_ap = wdb[ib].view(BF16, (128, 2048))
                    for hc in range(4):
                        last = e.matmul(pss[hc], h_ap[:, ib, tt * 128:(tt + 1) * 128],
                                        wd_ap[:, hc * 512:(hc + 1) * 512],
                                        start=(ib == 0), stop=(ib == n_ib - 1))
                        first = first or last
                return (first, last)
            P.op("pe", em_dn, reads=[h_r.buf] + [wdb[ib].buf for ib in range(n_ib)],
                 writes=[b.buf for b in banks], tag=f"dn_mm{tag}{su}{tt}")
            for hc in range(4):
                yv = y_ap[:, tt, hc * 512:(hc + 1) * 512]
                if comb_scalar is not None:
                    cs = comb_scalar(tt)
                    P.op("dve", lambda e, ps=pss[hc], yv=yv, cs=cs:
                         e.scalar_tensor_tensor(yv, ps, cs, yv, ALU.mult, ALU.add),
                         reads=[banks[hc].buf, comb.buf, y.buf], writes=[y.buf],
                         tag=f"dn_acc{tag}{su}{tt}{hc}")
                else:
                    P.op("dve", lambda e, ps=pss[hc], yv=yv: e.tensor_tensor(yv, ps, yv, ALU.add),
                         reads=[banks[hc].buf, y.buf], writes=[y.buf],
                         tag=f"dn_acc{tag}{su}{tt}{hc}")
            for b in banks:
                PS.free(b)
        for r in wdb:
            A.free(r)

    wgl, wgl_b = w["wg"]
    wul, wul_b = w["wu"]
    wdl, wdl_b = w["wd"]
    for e_i in range(NE):
        expert(NI,
               lambda ib, e_i=e_i: (wgl[e_i, ib], wgl_b),
               lambda ib, e_i=e_i: (wul[e_i, ib], wul_b),
               lambda ib, e_i=e_i: (wdl[e_i, ib], wdl_b),
               lambda tt, e_i=e_i: comb_ap[:, tt, e_i:e_i + 1],
               hb, su=f"e{e_i}")
    sgl, sgl_b = w["swg"]
    sul, sul_b = w["swu"]
    sdl, sdl_b = w["swd"]
    for k in range(2):
        expert(NI,
               lambda ib, k=k: (sgl[k * NI + ib], sgl_b),
               lambda ib, k=k: (sul[k * NI + ib], sul_b),
               lambda ib, k=k: (sdl[k * NI + ib], sdl_b),
               None, hb, su=f"sh{k}")
    for r in [hb] + wgb + wub + [sg, comb]:
        A.free(r)

    xs = [A.alloc(f"rx{tag}{i}", 2048 * 4) for i in range(2)]
    for tt in range(NTT):
        xb = xs[tt % 2]
        x_ap = xb.view(F32, (128, 2048))
        P.dma("sp", lambda e, o=x_ap, i=x_dram_ap[tt * 128:(tt + 1) * 128, :]:
              e.dma_start(out=o, in_=i), reads=[tile_buf(x_dram_buf, tt)],
              writes=[xb.buf], tag=f"res_ldx{tag}{tt}")
        P.op("dve", lambda e, x_ap=x_ap, yv=y_ap[:, tt, :]: e.tensor_tensor(x_ap, x_ap, yv, ALU.add),
             reads=[xb.buf, y.buf], writes=[xb.buf], tag=f"res_add{tag}{tt}")
        P.dma("sp", lambda e, o=out_dram_ap[tt * 128:(tt + 1) * 128, :], i=x_ap:
              e.dma_start(out=o, in_=i), reads=[xb.buf],
              writes=[tile_buf(out_dram_buf, tt)], tag=f"res_st{tag}{tt}")
    for r in xs + [y]:
        A.free(r)


def ffn_weight_inputs(C):
    wg, wg_b = C.dram_in("wgl", [NE, NI, 128, NCC, 128])
    wu, wu_b = C.dram_in("wul", [NE, NI, 128, NCC, 128])
    wd, wd_b = C.dram_in("wdl", [NE, NI, 128, 2048])
    sg_, sg_b = C.dram_in("swgl", [NIS, 128, NCC, 128])
    su_, su_b = C.dram_in("swul", [NIS, 128, NCC, 128])
    sd_, sd_b = C.dram_in("swdl", [NIS, 128, 2048])
    return {
        "wg": (wg, wg_b), "wu": (wu, wu_b), "wd": (wd, wd_b),
        "swg": (sg_, sg_b), "swu": (su_, su_b), "swd": (sd_, sd_b),
    }


def ffn_weight_arrays(ins, pfx="L0_ffn_"):
    w_gate = ins[pfx + "w_gate"]; w_up = ins[pfx + "w_up"]; w_down = ins[pfx + "w_down"]
    swg = ins[pfx + "sw_gate"]; swu = ins[pfx + "sw_up"]; swd = ins[pfx + "sw_down"]
    out = {}
    out["wgl"] = np.ascontiguousarray(w_gate.reshape(NE, NCC, 128, NI, 128).transpose(0, 3, 2, 1, 4))
    out["wul"] = np.ascontiguousarray(w_up.reshape(NE, NCC, 128, NI, 128).transpose(0, 3, 2, 1, 4))
    out["wdl"] = np.ascontiguousarray(w_down.reshape(NE, NI, 128, 2048))
    out["swgl"] = np.ascontiguousarray(swg.reshape(NCC, 128, NIS, 128).transpose(2, 1, 0, 3))
    out["swul"] = np.ascontiguousarray(swu.reshape(NCC, 128, NIS, 128).transpose(2, 1, 0, 3))
    out["swdl"] = np.ascontiguousarray(swd.reshape(NIS, 128, 2048))
    out["gate_wT"] = np.ascontiguousarray(ins[pfx + "gate_w"].T.reshape(NCC, 128, NE).transpose(1, 0, 2))
    return out


NH_CORE = 2
SEQ = 8192
NCH = SEQ // 512


def build_attn_a(C, layer, xn_fm_r, x_dram_ap, x_dram_buf, out_dram_ap, out_dram_buf,
                 core_grp, tag="a"):
    P, A, PS = C.P, C.A, C.PS
    nc = C.nc
    ident = C.const["ident"][1]
    ident_b = C.const["ident"][0].buf
    ones_col = C.const["ones_col"][1]
    ones_col_b = C.const["ones_col"][0].buf
    onesf = C.const["onesf"][1]
    onesf_b = C.const["onesf"][0].buf
    masksa = C.const["masks_a"][1]
    masksa_b = C.const["masks_a"][0].buf

    SC1 = 0.08838834764831845

    HGW = NCC * TCORE // 2
    agi = [C.dram_tmp(f"agin{tag}{h}", [128, HGW], BF16) for h in range(2)]
    ago = [C.dram_tmp(f"agout{tag}{h}", [128 * 8, HGW], BF16, addr_space="Shared")
           for h in range(2)]
    a2_in, a2_in_b = C.dram_tmp(f"a2in{tag}", [SEQ, NH_CORE * 128], BF16)
    a2_out, a2_out_b = C.dram_tmp(f"a2out{tag}", [8 * SEQ, NH_CORE * 128], BF16,
                                  addr_space="Shared")
    agov = [ago[h][0].rearrange("(r p) (cc t) -> r p cc t", p=128, t=TCORE)
            for h in range(2)]

    xn_flat = xn_fm_r.view(BF16, (128, NCC * TCORE))
    for h in range(2):
        P.dma("sp", lambda e, h=h: e.dma_start(out=agi[h][0], in_=xn_flat[:, h * HGW:(h + 1) * HGW]),
              reads=[xn_fm_r.buf], writes=[agi[h][1]], tag=f"ctr{tag}{h}")
        P.collective(lambda e, h=h: e.collective_compute(
            "AllGather", ALU.bypass, replica_groups=[core_grp],
            ins=[agi[h][0].opt()], outs=[ago[h][0].opt()]),
            reads=[agi[h][1]], writes=[ago[h][1]], tag=f"ag{tag}{h}")
    A.free(xn_fm_r)

    qT, qT_b = layer["qT"]; kT, kT_b = layer["kT"]; vT, vT_b = layer["vT"]
    wq = A.alloc(f"wq{tag}", NCC * 256 * 2)
    wk = A.alloc(f"wk{tag}", NCC * 256 * 2)
    wv = A.alloc(f"wv{tag}", NCC * 256 * 2)
    for (r, src, sb) in ((wq, qT, qT_b), (wk, kT, kT_b), (wv, vT, vT_b)):
        P.dma("pool", lambda e, o=r.view(BF16, (128, NCC, 256)), i=src: e.dma_start(out=o, in_=i),
              reads=[sb], writes=[r.buf], tag=f"ldw{tag}")
    wq_ap = wq.view(BF16, (128, NCC, 256))
    wk_ap = wk.view(BF16, (128, NCC, 256))
    wv_ap = wv.view(BF16, (128, NCC, 256))

    kfm = A.alloc(f"kfm{tag}", NH_CORE * SEQ * 2)
    vtm = A.alloc(f"vtm{tag}", (SEQ // 128) * 256 * 2)
    kfm_ap = kfm.view(BF16, (128, NH_CORE, SEQ))
    vtm_ap = vtm.view(BF16, (128, SEQ // 128, 256))
    xnc = [[A.alloc(f"xnc{tag}{i}{h}", NCC * 256 * 2) for h in range(2)] for i in range(3)]

    def load_chunk(ch):
        bs = xnc[ch % 3]
        r, t0 = ch // 2, (ch % 2) * 512
        aps = []
        for h in range(2):
            ap = bs[h].view(BF16, (128, NCC // 2, 512))
            P.dma("sp", lambda e, o=ap, i=agov[h][r, :, :, t0:t0 + 512], h=h:
                  e.dma_start(out=o, in_=i),
                  reads=[ago[h][1]], writes=[bs[h].buf], tag=f"ldxn{tag}{ch}{h}")
            aps.append(ap)
        return bs, aps

    _kvcache = {}

    def get_chunk(ch):
        if ch not in _kvcache:
            _kvcache[ch] = load_chunk(ch)
        return _kvcache[ch]

    for ch in range(NCH):
        bs, aps = get_chunk(ch)
        if ch + 1 < NCH:
            get_chunk(ch + 1)
        for fb in range(NH_CORE):
            bank = PS.alloc(f"k{tag}{ch}{fb}")
            ps = bank.view(F32, (128, 512))
            for h in range(2):
                def em_k(e, ps=ps, xap=aps[h], fb=fb, h=h):
                    first = None
                    for c8 in range(8):
                        cc = h * 8 + c8
                        last = e.matmul(ps, wk_ap[:, cc, fb * 128:(fb + 1) * 128],
                                        xap[:, c8, :],
                                        start=(cc == 0), stop=(cc == NCC - 1))
                        first = first or last
                    return (first, last)
                P.op("pe", em_k, reads=[wk.buf, bs[h].buf], writes=[bank.buf],
                     tag=f"kmm{tag}{ch}{fb}{h}")
            dst = kfm_ap[:, fb, ch * 512:(ch + 1) * 512]
            P.op("act", lambda e, ps=ps, dst=dst: e.copy(dst, ps),
                 reads=[bank.buf], writes=[kfm.buf], tag=f"kev{tag}{ch}{fb}")
            PS.free(bank)
        for tb in range(4):
            bank = PS.alloc(f"v{tag}{ch}{tb}")
            ps = bank.view(F32, (128, 256))
            for h in range(2):
                def em_v(e, ps=ps, xap=aps[h], tb=tb, h=h):
                    first = None
                    for c8 in range(8):
                        cc = h * 8 + c8
                        last = e.matmul(ps, xap[:, c8, tb * 128:(tb + 1) * 128], wv_ap[:, cc, :],
                                        start=(cc == 0), stop=(cc == NCC - 1))
                        first = first or last
                    return (first, last)
                P.op("pe", em_v, reads=[wv.buf, bs[h].buf], writes=[bank.buf],
                     tag=f"vmm{tag}{ch}{tb}{h}")
            dst = vtm_ap[:, ch * 4 + tb, :]
            P.op("dve", lambda e, ps=ps, dst=dst: e.tensor_copy(dst, ps),
                 reads=[bank.buf], writes=[vtm.buf], tag=f"vev{tag}{ch}{tb}")
            PS.free(bank)

    qsb = A.alloc(f"q{tag}", NH_CORE * 512 * 2)
    q_ap = qsb.view(BF16, (128, NH_CORE, 512))
    psb = [A.alloc(f"p{tag}{i}", 512 * 2) for i in range(6)]
    osb = A.alloc(f"o{tag}", 512 * 2)
    lsb = A.alloc(f"l{tag}", 512 * 4)
    lcb = A.alloc(f"lc{tag}", 4 * 4)
    zsb = A.alloc(f"z{tag}", 4 * 256 * 2)
    z_ap = zsb.view(BF16, (128, 4, 256))

    _flcache = {}

    def get_chunk_f(qc):
        if qc not in _flcache:
            _flcache[qc] = load_chunk(qc)
        return _flcache[qc]

    for qc in range(NCH):
        bs, aps = get_chunk_f(qc)
        if qc + 1 < NCH:
            get_chunk_f(qc + 1)
        for fb in range(NH_CORE):
            bank = PS.alloc(f"q{tag}{qc}{fb}")
            ps = bank.view(F32, (128, 512))
            for h in range(2):
                def em_q(e, ps=ps, xap=aps[h], fb=fb, h=h):
                    first = None
                    for c8 in range(8):
                        cc = h * 8 + c8
                        last = e.matmul(ps, wq_ap[:, cc, fb * 128:(fb + 1) * 128],
                                        xap[:, c8, :],
                                        start=(cc == 0), stop=(cc == NCC - 1))
                        first = first or last
                    return (first, last)
                P.op("pe", em_q, reads=[wq.buf, bs[h].buf], writes=[bank.buf],
                     tag=f"qmm{tag}{qc}{fb}{h}")
            P.op("act", lambda e, ps=ps, dst=q_ap[:, fb, :]: e.copy(dst, ps),
                 reads=[bank.buf], writes=[qsb.buf], tag=f"qev{tag}{qc}{fb}")
            PS.free(bank)
        nkb = 4 * qc + 4
        for h in range(NH_CORE):
            obank = PS.alloc(f"o{tag}{qc}{h}")
            lbank = PS.alloc(f"l{tag}{qc}{h}")
            ops_v = obank.view(F32, (128, 512))
            lps_v = lbank.view(F32, (1, 512))
            sbanks = [PS.alloc(f"s{tag}{qc}{h}{i}") for i in range(6)]
            G = 3
            groups = [list(range(g, min(g + G, nkb))) for g in range(0, nkb, G)]

            def s_grp(gi, h=h, qc=qc, sbanks=sbanks, groups=groups):
                ks = groups[gi]

                def em(e, ks=ks, h=h):
                    first = None
                    for kb in ks:
                        sv = sbanks[kb % 6].view(F32, (128, 512))
                        last = e.matmul(sv, kfm_ap[:, h, kb * 128:(kb + 1) * 128],
                                        q_ap[:, h, :], start=True, stop=True)
                        first = first or last
                    return (first, last)
                P.op("pe", em, reads=[kfm.buf, qsb.buf],
                     writes=[sbanks[kb % 6].buf for kb in ks],
                     tag=f"sgrp{tag}{qc}{h}{gi}")
            s_grp(0)
            if len(groups) > 1:
                s_grp(1)
            for gi, ks in enumerate(groups):
                def em_exp(e, ks=ks, sbanks=sbanks):
                    first = None
                    for kb in ks:
                        sv = sbanks[kb % 6].view(F32, (128, 512))
                        p_ap = psb[kb % 6].view(BF16, (128, 512))
                        last = e.activation(p_ap, sv, AF.Exp, scale=SC1)
                        first = first or last
                    return (first, last)
                P.op("act", em_exp,
                     reads=[sbanks[kb % 6].buf for kb in ks],
                     writes=[psb[kb % 6].buf for kb in ks], tag=f"egrp{tag}{qc}{h}{gi}")
                mks = [kb for kb in ks if kb >= 4 * qc]
                if mks:
                    def em_msk(e, mks=mks, qc=qc):
                        first = None
                        for kb in mks:
                            p_ap = psb[kb % 6].view(BF16, (128, 512))
                            last = e.tensor_mul(p_ap, p_ap, masksa[:, kb - 4 * qc, :])
                            first = first or last
                        return (first, last)
                    P.op("dve", em_msk,
                         reads=[psb[kb % 6].buf for kb in mks] + [masksa_b],
                         writes=[psb[kb % 6].buf for kb in mks],
                         tag=f"mgrp{tag}{qc}{h}{gi}")
                if gi + 2 < len(groups):
                    s_grp(gi + 2)

                def em_pvl(e, ks=ks, h=h, nkb=nkb, ops_v=ops_v, lps_v=lps_v):
                    first = None
                    for kb in ks:
                        p_ap = psb[kb % 6].view(BF16, (128, 512))
                        i0 = e.matmul(ops_v, vtm_ap[:, kb, h * 128:(h + 1) * 128], p_ap,
                                      start=(kb == 0), stop=(kb == nkb - 1))
                        first = first or i0
                        last = e.matmul(lps_v, ones_col, p_ap,
                                        start=(kb == 0), stop=(kb == nkb - 1))
                    return (first, last)
                P.op("pe", em_pvl,
                     reads=[vtm.buf, ones_col_b] + [psb[kb % 6].buf for kb in ks],
                     writes=[obank.buf, lbank.buf], tag=f"pvlg{tag}{qc}{h}{gi}")
            for r in sbanks:
                PS.free(r)
            o_ap = osb.view(BF16, (128, 512))
            l_ap = lsb.view(F32, (1, 512))
            P.op("act", lambda e, o_ap=o_ap, ops_v=ops_v: e.copy(o_ap, ops_v),
                 reads=[obank.buf], writes=[osb.buf], tag=f"oev{tag}{qc}{h}")
            P.op("act", lambda e, l_ap=l_ap, lps_v=lps_v: e.copy(l_ap, lps_v),
                 reads=[lbank.buf], writes=[lsb.buf], tag=f"lev{tag}{qc}{h}")
            lcbank = PS.alloc(f"lc{tag}{qc}{h}")
            lc_ps = lcbank.view(F32, (128, 4))

            def em_lc(e, l_ap=l_ap, lc_ps=lc_ps):
                first = None
                for j in range(4):
                    last = e.matmul(lc_ps[:, j:j + 1], l_ap[0:1, j * 128:(j + 1) * 128],
                                    onesf, start=True, stop=True)
                    first = first or last
                return (first, last)
            P.op("pe", em_lc, reads=[lsb.buf, onesf_b], writes=[lcbank.buf],
                 tag=f"lc{tag}{qc}{h}")
            lc_ap = lcb.view(F32, (128, 4))
            P.op("dve", lambda e, lc_ap=lc_ap, lc_ps=lc_ps: e.reciprocal(lc_ap, lc_ps),
                 reads=[lcbank.buf], writes=[lcb.buf], tag=f"lcr{tag}{qc}{h}")
            PS.free(lcbank)
            ztbank = PS.alloc(f"zt{tag}{qc}{h}")
            zt_ps = ztbank.view(BF16, (128, 4, 128))

            def em_zt(e, o_ap=o_ap, zt_ps=zt_ps):
                first = None
                for j in range(4):
                    last = e.transpose(zt_ps[:, j, :], o_ap[:, j * 128:(j + 1) * 128], ident)
                    first = first or last
                return (first, last)
            P.op("pe", em_zt, reads=[osb.buf, ident_b], writes=[ztbank.buf],
                 tag=f"zt{tag}{qc}{h}")

            def em_zn(e, zt_ps=zt_ps, lc_ap=lc_ap, h=h):
                first = None
                for j in range(4):
                    last = e.tensor_scalar(z_ap[:, j, h * 128:(h + 1) * 128], zt_ps[:, j, :],
                                           lc_ap[:, j:j + 1], None, ALU.mult)
                    first = first or last
                return (first, last)
            P.op("dve", em_zn, reads=[ztbank.buf, lcb.buf], writes=[zsb.buf],
                 tag=f"zn{tag}{qc}{h}")
            PS.free(ztbank)
            for r in [obank, lbank]:
                PS.free(r)
        a2v = a2_in.rearrange("(qc tb p) f -> qc p tb f", tb=4, p=128)
        P.dma("sp", lambda e, o=a2v[qc], i=z_ap: e.dma_start(out=o, in_=i),
              reads=[zsb.buf], writes=[a2_in_b], tag=f"zctr{tag}{qc}")

    for r in [kfm, vtm, qsb, osb, lsb, lcb, zsb] + psb + [x for pair in xnc for x in pair] + [wq, wk, wv]:
        A.free(r)

    oT_ap_l, oT_b_l = layer["oT"]
    wo_pre = [A.alloc(f"wo{tag}{i}", NCC * 512 * 2) for i in range(2)]
    for oc in range(2):
        wap = wo_pre[oc].view(BF16, (128, NCC, 512))
        P.dma("pool", lambda e, o=wap, i=oT_ap_l[oc]: e.dma_start(out=o, in_=i),
              reads=[oT_b_l], writes=[wo_pre[oc].buf], tag=f"ldwo{tag}{oc}")
    if not hasattr(C, "wo_pre"):
        C.wo_pre = {}
    C.wo_pre[tag] = wo_pre

    P.collective(lambda e: e.collective_compute(
        "AllGather", ALU.bypass, replica_groups=[core_grp],
        ins=[a2_in.opt()], outs=[a2_out.opt()]),
        reads=[a2_in_b], writes=[a2_out_b], tag=f"zag{tag}")

    build_outproj(C, layer["oT"], a2_out, a2_out_b, x_dram_ap, x_dram_buf,
                  out_dram_ap, out_dram_buf, tag=tag)


def build_outproj(C, oT_src, z_dram, z_dram_b, x_dram_ap, x_dram_buf,
                  out_dram_ap, out_dram_buf, tag):
    P, A, PS = C.P, C.A, C.PS
    ident = C.const["ident"][1]
    ident_b = C.const["ident"][0].buf
    oT_ap, oT_b = oT_src

    nc = C.nc
    wo_pre = getattr(C, "wo_pre", {}).pop(tag, None)
    zown = A.alloc(f"zow{tag}", NTT * 8 * 256 * 2)
    zown_ap = zown.view(BF16, (128, NTT, 8, 256))
    zfm = A.alloc(f"zfm{tag}", NCC * TCORE * 2)
    zfm_ap = zfm.view(BF16, (128, NCC, TCORE))
    zv = z_dram.rearrange("(r c tt p) f -> r c tt p f", c=8, tt=NTT, p=128)

    def em_ldz(e, sem):
        import concourse.mybir as _mb
        pid = nc.partition_id([_mb.EngineType.SP])
        for tt in range(NTT):
            iv = zv[:, bass_ts(pid, 1), tt].rearrange("r one p f -> p (r one) f")
            e.dma_start(out=zown_ap[:, tt, :, :], in_=iv).then_inc(sem, 16)
    P.dma_multi("sp", em_ldz, NTT, reads=[z_dram_b], writes=[zown.buf],
                tag=f"ldz{tag}")
    for tt in range(NTT):
        zt_flat = zown_ap[:, tt, :, :].rearrange("p r f -> p (r f)")
        for g in range(4):
            bank = PS.alloc(f"zT{tag}{tt}{g}")
            bap = bank.view(BF16, (128, 4, 128))

            def em_tr(e, bap=bap, g=g, zt_flat=zt_flat):
                first = None
                for j in range(4):
                    cc = g * 4 + j
                    last = e.transpose(bap[:, j, :], zt_flat[:, cc * 128:(cc + 1) * 128], ident)
                    first = first or last
                return (first, last)
            P.op("pe", em_tr, reads=[zown.buf, ident_b], writes=[bank.buf],
                 tag=f"ztr{tag}{tt}{g}")
            dst = zfm_ap[:, g * 4:(g + 1) * 4, tt * 128:(tt + 1) * 128]
            P.op("dve", lambda e, bap=bap, dst=dst: e.tensor_copy(dst, bap),
                 reads=[bank.buf], writes=[zfm.buf], tag=f"ztrc{tag}{tt}{g}")
            PS.free(bank)
    A.free(zown)

    xacc = A.alloc(f"xac{tag}", NTT * 2048 * 4)
    xacc_ap = xacc.view(F32, (128, NTT, 2048))
    for tt in range(NTT):
        P.dma("sp", lambda e, o=xacc_ap[:, tt, :], i=x_dram_ap[tt * 128:(tt + 1) * 128, :]:
              e.dma_start(out=o, in_=i), reads=[tile_buf(x_dram_buf, tt)], writes=[xacc.buf],
              tag=f"opx{tag}{tt}")
    if wo_pre is not None:
        wo = wo_pre
    else:
        wo = [A.alloc(f"wo{tag}{i}", NCC * 512 * 2) for i in range(2)]
    for oc in range(4):
        wr = wo[oc % 2]
        wap = wr.view(BF16, (128, NCC, 512))
        if wo_pre is None or oc >= 2:
            P.dma("pool", lambda e, o=wap, i=oT_ap[oc]: e.dma_start(out=o, in_=i),
                  reads=[oT_b], writes=[wr.buf], tag=f"ldwo{tag}{oc}")
        for tt in range(NTT):
            bank = PS.alloc(f"op{tag}{oc}{tt}")
            ps = bank.view(F32, (128, 512))

            def em_op(e, ps=ps, wap=wap, tt=tt):
                first = None
                for cc in range(NCC):
                    last = e.matmul(ps, zfm_ap[:, cc, tt * 128:(tt + 1) * 128], wap[:, cc, :],
                                    start=(cc == 0), stop=(cc == NCC - 1))
                    first = first or last
                return (first, last)
            P.op("pe", em_op, reads=[zfm.buf, wr.buf], writes=[bank.buf],
                 tag=f"opmm{tag}{oc}{tt}")
            dst = xacc_ap[:, tt, oc * 512:(oc + 1) * 512]
            P.op("dve", lambda e, ps=ps, dst=dst: e.tensor_tensor(dst, dst, ps, ALU.add),
                 reads=[bank.buf, xacc.buf], writes=[xacc.buf], tag=f"opacc{tag}{oc}{tt}")
            PS.free(bank)
    for tt in range(NTT):
        P.dma("sp", lambda e, o=out_dram_ap[tt * 128:(tt + 1) * 128, :], i=xacc_ap[:, tt, :]:
              e.dma_start(out=o, in_=i), reads=[xacc.buf], writes=[tile_buf(out_dram_buf, tt)],
              tag=f"opst{tag}{tt}")
    for r in [zfm, xacc] + wo:
        A.free(r)


def build_attn_b(C, layer, xn_fm_r, x_dram_ap, x_dram_buf, out_dram_ap, out_dram_buf,
                 core_grp, tag="b"):
    P, A, PS = C.P, C.A, C.PS
    nc = C.nc
    ident = C.const["ident"][1]
    ident_b = C.const["ident"][0].buf
    ones_col = C.const["ones_col"][1]
    ones_col_b = C.const["ones_col"][0].buf
    onesf = C.const["onesf"][1]
    onesf_b = C.const["onesf"][0].buf
    masksb = C.const["masks_b"][1]
    masksb_b = C.const["masks_b"][0].buf
    SC1 = 0.08838834764831845
    SC2 = 0.17677669529663687

    GW = NCC * TCORE
    HGW = GW // 2
    agi = [C.dram_tmp(f"agin{tag}{h}", [128, HGW + 8 * h], BF16) for h in range(2)]
    ago = [C.dram_tmp(f"agout{tag}{h}", [128 * 8, HGW + 8 * h], BF16,
                      addr_space="Shared") for h in range(2)]
    z_in, z_in_b = C.dram_tmp(f"zin{tag}", [SEQ, NH_CORE * 128], BF16)
    z_out, z_out_b = C.dram_tmp(f"zout{tag}", [8 * SEQ, NH_CORE * 128], BF16,
                                  addr_space="Shared")
    agov = [ago[h][0].rearrange("(r p) g -> r p g", p=128) for h in range(2)]

    def ldconst(nm, src_key, shape):
        sap, sb = layer[src_key]
        r = A.alloc(f"{nm}{tag}", int(np.prod(shape[1:])) * 2)
        ap = r.view(BF16, shape)
        P.dma("pool", lambda e, o=ap, i=sap: e.dma_start(out=o, in_=i),
              reads=[sb], writes=[r.buf], tag=f"ld{nm}{tag}")
        return r, ap

    wq_r, wq_ap = ldconst("wq", "qT", (128, NCC, 256))
    wk_r, wk_ap = ldconst("wk", "kT", (128, NCC, 256))
    wv_r, wv_ap = ldconst("wv", "vT", (128, NCC, 256))
    vn_r, vn_ap = ldconst("vn", "vNat", (128, NH_CORE, NCC, 128))
    pk_r, pk_ap = ldconst("pk", "pkT", (128, 64))
    pv_r, pv_ap = ldconst("pv", "pvT", (128, 64))
    uo_r, uo_ap = ldconst("uo", "uoT", (64, 128))
    w1_r, w1_ap = ldconst("w1", "w1T", (128, NCC, 512))
    w2_r, w2_ap = ldconst("w2", "w2T", (128, 4, 1))

    xn_fm = xn_fm_r.view(BF16, (128, NCC, TCORE))
    hi_sb = A.alloc(f"hi{tag}", 512 * 2)
    hifm = A.alloc(f"hifm{tag}", 4 * 128 * 2)
    grow = A.alloc(f"grow{tag}", TCORE * 4)
    gcol = A.alloc(f"gcol{tag}", 8 * 2)
    grow_ap = grow.view(F32, (1, TCORE))
    gp_banks = [PS.alloc(f"gp{tag}{i}") for i in range(2)]
    for tt in range(NTT):
        bank = PS.alloc(f"hi{tag}{tt}")
        ps = bank.view(F32, (128, 512))

        def em_hi(e, ps=ps, tt=tt):
            first = None
            for cc in range(NCC):
                last = e.matmul(ps, xn_fm[:, cc, tt * 128:(tt + 1) * 128], w1_ap[:, cc, :],
                                start=(cc == 0), stop=(cc == NCC - 1))
                first = first or last
            return (first, last)
        P.op("pe", em_hi, reads=[xn_fm_r.buf, w1_r.buf], writes=[bank.buf], tag=f"hi_mm{tag}{tt}")
        hi_ap = hi_sb.view(BF16, (128, 512))
        P.op("act", lambda e, ps=ps, hi_ap=hi_ap: e.activation(hi_ap, ps, AF.Gelu),
             reads=[bank.buf], writes=[hi_sb.buf], tag=f"hi_gelu{tag}{tt}")
        PS.free(bank)
        trb = PS.alloc(f"hit{tag}{tt}")
        tr_ps = trb.view(BF16, (128, 4, 128))

        def em_htr(e, tr_ps=tr_ps, hi_ap=hi_ap):
            first = None
            for j in range(4):
                last = e.transpose(tr_ps[:, j, :], hi_ap[:, j * 128:(j + 1) * 128], ident)
                first = first or last
            return (first, last)
        P.op("pe", em_htr, reads=[hi_sb.buf, ident_b], writes=[trb.buf], tag=f"hi_tr{tag}{tt}")
        hifm_ap = hifm.view(BF16, (128, 4, 128))
        P.op("dve", lambda e, tr_ps=tr_ps, hifm_ap=hifm_ap: e.tensor_copy(hifm_ap, tr_ps),
             reads=[trb.buf], writes=[hifm.buf], tag=f"hi_trc{tag}{tt}")
        PS.free(trb)
        gpb = gp_banks[tt // 4]
        gp = gpb.view(F32, (1, 512))

        def em_sc(e, gp=gp, hifm_ap=hifm_ap, tt=tt):
            first = None
            for j in range(4):
                last = e.matmul(gp[:, (tt % 4) * 128:(tt % 4 + 1) * 128],
                                w2_ap[:, j, :], hifm_ap[:, j, :],
                                start=(j == 0), stop=(j == 3))
                first = first or last
            return (first, last)
        P.op("pe", em_sc, reads=[hifm.buf, w2_r.buf], writes=[gpb.buf], tag=f"hi_sc{tag}{tt}")
    for half in range(2):
        gp = gp_banks[half].view(F32, (1, 512))
        P.op("act", lambda e, gp=gp, o=grow_ap[:, half * 512:(half + 1) * 512]:
             e.activation(o, gp, AF.Sigmoid),
             reads=[gp_banks[half].buf], writes=[grow.buf], tag=f"gate_sig{tag}{half}")
        PS.free(gp_banks[half])
    P.op("dve", lambda e: e.tensor_single_scalar(grow_ap, grow_ap, 0.75, ALU.is_gt),
         reads=[grow.buf], writes=[grow.buf], tag=f"gate_gt{tag}")
    gcb = PS.alloc(f"gc{tag}")
    gc_ps = gcb.view(F32, (128, 8))

    def em_gc(e):
        first = None
        for tt in range(NTT):
            last = e.matmul(gc_ps[:, tt:tt + 1], grow_ap[0:1, tt * 128:(tt + 1) * 128],
                            onesf, start=True, stop=True)
            first = first or last
        return (first, last)
    P.op("pe", em_gc, reads=[grow.buf, onesf_b], writes=[gcb.buf], tag=f"gate_col{tag}")
    gcol_ap = gcol.view(BF16, (128, 8))
    P.op("dve", lambda e: e.tensor_copy(gcol_ap, gc_ps),
         reads=[gcb.buf], writes=[gcol.buf], tag=f"gate_colc{tag}")
    PS.free(gcb)
    for r in (hi_sb, hifm, grow):
        A.free(r)

    xn_flat = xn_fm_r.view(BF16, (128, GW))
    P.dma("sp", lambda e: e.dma_start(out=agi[0][0], in_=xn_flat[:, 0:HGW]),
          reads=[xn_fm_r.buf], writes=[agi[0][1]], tag=f"ctr{tag}0")
    P.dma("sp", lambda e: e.dma_start(out=agi[1][0][:, 0:HGW], in_=xn_flat[:, HGW:GW]),
          reads=[xn_fm_r.buf], writes=[agi[1][1]], tag=f"ctr{tag}1")
    P.dma("sp", lambda e: e.dma_start(out=agi[1][0][:, HGW:HGW + 8], in_=gcol_ap),
          reads=[gcol.buf], writes=[agi[1][1]], tag=f"ctrg{tag}")
    for h in range(2):
        P.collective(lambda e, h=h: e.collective_compute(
            "AllGather", ALU.bypass, replica_groups=[core_grp],
            ins=[agi[h][0].opt()], outs=[ago[h][0].opt()]),
            reads=[agi[h][1]], writes=[ago[h][1]], tag=f"ag{tag}{h}")
    A.free(gcol)
    A.free(xn_fm_r)
    A.free(w1_r)
    A.free(w2_r)

    wvr = A.alloc(f"wvr{tag}", NH_CORE * NCC * 64 * 2)
    wvr_ap = wvr.view(BF16, (128, NH_CORE, NCC, 64))
    for h in range(NH_CORE):
        for g in range(4):
            bank = PS.alloc(f"wvr{tag}{h}{g}")
            bap = bank.view(F32, (128, 4, 64))

            def em_wvr(e, bap=bap, h=h, g=g):
                first = None
                for j in range(4):
                    cc = g * 4 + j
                    last = e.matmul(bap[:, j, :], vn_ap[:, h, cc, :], pv_ap,
                                    start=True, stop=True)
                    first = first or last
                return (first, last)
            P.op("pe", em_wvr, reads=[vn_r.buf, pv_r.buf], writes=[bank.buf],
                 tag=f"wvr_mm{tag}{h}{g}")
            dst = wvr_ap[:, h, g * 4:(g + 1) * 4, :]
            P.op("dve", lambda e, bap=bap, dst=dst: e.tensor_copy(dst, bap),
                 reads=[bank.buf], writes=[wvr.buf], tag=f"wvr_c{tag}{h}{g}")
            PS.free(bank)
    A.free(vn_r)
    A.free(pv_r)

    kfm = A.alloc(f"kfm{tag}", NH_CORE * SEQ * 2)
    vtm = A.alloc(f"vtm{tag}", (SEQ // 128) * 256 * 2)
    krf = A.alloc(f"krf{tag}", SEQ * 2)
    vrt = A.alloc(f"vrt{tag}", (SEQ // 128) * 130 * 2)
    kfm_ap = kfm.view(BF16, (128, NH_CORE, SEQ))
    vtm_ap = vtm.view(BF16, (128, SEQ // 128, 256))
    krf_ap = krf.view(BF16, (128, SEQ))
    vrt_ap = vrt.view(BF16, (128, SEQ // 128, 130))
    P.op("dve", lambda e: e.memset(vrt.view(BF16, (128, (SEQ // 128) * 130)), 1.0),
         writes=[vrt.buf], tag=f"vrt1{tag}")

    xnc = [[A.alloc(f"xnc{tag}{i}{h}", NCC * 256 * 2) for h in range(2)] for i in range(2)]
    ropeb = [A.alloc(f"rope{tag}{i}", 2 * 4 * 128 * 2) for i in range(3)]
    rsc = A.alloc(f"rsc{tag}", 3 * 256 * 4)
    rcos, rcos_b = layer["rope_cos"]
    rss, rss_b = layer["rope_ss"]

    def load_chunk(ch):
        bs = xnc[ch % 2]
        r, t0 = ch // 2, (ch % 2) * 512
        aps = []
        for h in range(2):
            ap = bs[h].view(BF16, (128, NCC // 2, 512))
            srcv = agov[h][r, :, 0:HGW].rearrange("p (cc t) -> p cc t", t=TCORE)[:, :, t0:t0 + 512]
            P.dma("sp", lambda e, o=ap, i=srcv: e.dma_start(out=o, in_=i),
                  reads=[ago[h][1]], writes=[bs[h].buf], tag=f"ldxn{tag}{ch}{h}")
            aps.append(ap)
        rb = ropeb[ch % 3]
        return bs, aps, rb

    def rope_tiles(rb):
        cos_ap = rb.view(BF16, (128, 2, 4, 128))
        return cos_ap[:, 0], cos_ap[:, 1]

    def load_rope(ch, rb):
        dst = rb.view(BF16, (128, 2, 4, 128))
        P.dma("sp", lambda e, o=dst[:, 0], i=rcos[ch]: e.dma_start(out=o, in_=i),
              reads=[rcos_b], writes=[rb.buf], tag=f"ldcos{tag}{ch}")
        P.dma("sp", lambda e, o=dst[:, 1], i=rss[ch]: e.dma_start(out=o, in_=i),
              reads=[rss_b], writes=[rb.buf], tag=f"ldss{tag}{ch}")

    def project_tm(aps, bufs, w_ap, w_buf, tb, n):
        bank = PS.alloc("ptm")
        ps = bank.view(F32, (128, n))
        for h in range(2):
            def em(e, ps=ps, xap=aps[h], tb=tb, h=h):
                first = None
                for c8 in range(8):
                    cc = h * 8 + c8
                    last = e.matmul(ps, xap[:, c8, tb * 128:(tb + 1) * 128], w_ap[:, cc, 0:n],
                                    start=(cc == 0), stop=(cc == NCC - 1))
                    first = first or last
                return (first, last)
            P.op("pe", em, reads=[bufs[h], w_buf], writes=[bank.buf], tag=f"ptm{h}")
        return bank, ps

    def rope_apply(ps_v, bank, cos_ap, ss_ap, rb, h, out_ap, out_buf):
        t1 = rsc.view(F32, (128, 256))[:, 0:128]
        qs = rsc.view(F32, (128, 256), byte_off=1024)[:, 0:128]
        kv = ps_v[:, h * 128:(h + 1) * 128]
        P.op("dve", lambda e: e.tensor_mul(t1, kv, cos_ap),
             reads=[bank.buf, rb.buf], writes=[rsc.buf], tag="rp1")

        def em_sw(e):
            i0 = e.tensor_copy(qs[:, 0::2], kv[:, 1::2])
            i1 = e.tensor_copy(qs[:, 1::2], kv[:, 0::2])
            return (i0, i1)
        P.op("dve", em_sw, reads=[bank.buf], writes=[rsc.buf], tag="rp2")
        P.op("dve", lambda e: e.tensor_mul(qs, qs, ss_ap),
             reads=[rsc.buf, rb.buf], writes=[rsc.buf], tag="rp3")
        P.op("dve", lambda e: e.tensor_add(out_ap, t1, qs),
             reads=[rsc.buf], writes=[out_buf], tag="rp4")

    ksc = A.alloc(f"ksc{tag}", 256 * 2)

    _kvc = {}

    def get_chunk(ch):
        if ch not in _kvc:
            bs, aps, rb = load_chunk(ch)
            load_rope(ch, rb)
            _kvc[ch] = (bs, aps, rb)
        return _kvc[ch]

    for ch in range(NCH):
        bs, aps, rb = get_chunk(ch)
        if ch + 1 < NCH:
            get_chunk(ch + 1)
        bufs = [bs[0].buf, bs[1].buf]
        cos_all, ss_all = rope_tiles(rb)
        for tb in range(4):
            kbank, kps = project_tm(aps, bufs, wk_ap, wk_r.buf, tb, 256)
            vbank, vps = project_tm(aps, bufs, wv_ap, wv_r.buf, tb, 256)
            P.op("dve", lambda e, vps=vps, dst=vtm_ap[:, ch * 4 + tb, :]:
                 e.tensor_copy(dst, vps),
                 reads=[vbank.buf], writes=[vtm.buf], tag=f"vev{tag}{ch}{tb}")
            PS.free(vbank)
            for h in range(NH_CORE):
                k_ap = ksc.view(BF16, (128, 256))[:, h * 128:(h + 1) * 128]
                rope_apply(kps, kbank, cos_all[:, tb, :], ss_all[:, tb, :], rb, h,
                           k_ap, ksc.buf)
            PS.free(kbank)
            trb2 = PS.alloc(f"ktr{tag}")
            tr_ps = trb2.view(BF16, (128, 2, 128))

            def em_ktr(e, tr_ps=tr_ps):
                i0 = e.transpose(tr_ps[:, 0, :], ksc.view(BF16, (128, 256))[:, 0:128], ident)
                i1 = e.transpose(tr_ps[:, 1, :], ksc.view(BF16, (128, 256))[:, 128:256], ident)
                return (i0, i1)
            P.op("pe", em_ktr, reads=[ksc.buf, ident_b], writes=[trb2.buf],
                 tag=f"ktr{tag}{ch}{tb}")

            def em_kc(e, tr_ps=tr_ps, ch=ch, tb=tb):
                i0 = e.tensor_copy(kfm_ap[:, 0, (ch * 4 + tb) * 128:(ch * 4 + tb + 1) * 128],
                                   tr_ps[:, 0, :])
                i1 = e.tensor_copy(kfm_ap[:, 1, (ch * 4 + tb) * 128:(ch * 4 + tb + 1) * 128],
                                   tr_ps[:, 1, :])
                return (i0, i1)
            P.op("dve", em_kc, reads=[trb2.buf], writes=[kfm.buf], tag=f"kc{tag}{ch}{tb}")
            PS.free(trb2)
        for h in range(NH_CORE):
            bank = PS.alloc(f"kr{tag}")
            ps = bank.view(F32, (64, 512))
            P.op("pe", lambda e, ps=ps, h=h, ch=ch:
                 e.matmul(ps, pk_ap, kfm_ap[:, h, ch * 512:(ch + 1) * 512],
                          start=True, stop=True),
                 reads=[kfm.buf, pk_r.buf], writes=[bank.buf], tag=f"kr_mm{tag}{ch}{h}")
            dst = krf_ap[64 * h:64 * (h + 1), ch * 512:(ch + 1) * 512]
            P.op("dve", lambda e, ps=ps, dst=dst: e.tensor_copy(dst, ps),
                 reads=[bank.buf], writes=[krf.buf], tag=f"kr_c{tag}{ch}{h}")
            PS.free(bank)
        for h in range(NH_CORE):
            bank = PS.alloc(f"vr{tag}")
            ps = bank.view(F32, (64, 512))
            for hh in range(2):
                def em_vr(e, ps=ps, xap=aps[hh], h=h, hh=hh):
                    first = None
                    for c8 in range(8):
                        cc = hh * 8 + c8
                        last = e.matmul(ps, wvr_ap[:, h, cc, :], xap[:, c8, :],
                                        start=(cc == 0), stop=(cc == NCC - 1))
                        first = first or last
                    return (first, last)
                P.op("pe", em_vr, reads=[wvr.buf, bs[hh].buf], writes=[bank.buf],
                     tag=f"vr_mm{tag}{ch}{h}{hh}")
            vrf = rsc.view(BF16, (128, 512), byte_off=2048)[0:64, :]
            P.op("dve", lambda e, ps=ps, vrf=vrf: e.tensor_copy(vrf, ps),
                 reads=[bank.buf], writes=[rsc.buf], tag=f"vr_ev{tag}{ch}{h}")
            PS.free(bank)
            trb3 = PS.alloc(f"vrt{tag}")
            tr_ps = trb3.view(BF16, (128, 4, 128))[:, :, 0:64]

            def em_vtr(e, tr_ps=tr_ps, vrf=vrf):
                first = None
                for j in range(4):
                    last = e.transpose(tr_ps[:, j, :], vrf[:, j * 128:(j + 1) * 128],
                                       ident[0:64, 0:64])
                    first = first or last
                return (first, last)
            P.op("pe", em_vtr, reads=[rsc.buf, ident_b], writes=[trb3.buf],
                 tag=f"vr_tr{tag}{ch}{h}")

            def em_vrc(e, tr_ps=tr_ps, ch=ch, h=h):
                first = None
                for j in range(4):
                    last = e.tensor_copy(vrt_ap[:, ch * 4 + j, h * 65:h * 65 + 64],
                                         tr_ps[:, j, :])
                    first = first or last
                return (first, last)
            P.op("dve", em_vrc, reads=[trb3.buf], writes=[vrt.buf], tag=f"vr_c{tag}{ch}{h}")
            PS.free(trb3)

    qfm = A.alloc(f"qfm{tag}", NH_CORE * 512 * 2)
    qfm_ap = qfm.view(BF16, (128, NH_CORE, 512))
    qr = A.alloc(f"qr{tag}", 512 * 2)
    qr_ap = qr.view(BF16, (128, 512))
    qsc = A.alloc(f"qsc{tag}", 256 * 2)
    psb = [A.alloc(f"p{tag}{i}", 512 * 2) for i in range(6)]
    olsb = A.alloc(f"ol{tag}", 2 * 512 * 2)
    ogsb = A.alloc(f"og{tag}", 512 * 2)
    cgsb = A.alloc(f"cg{tag}", 512 * 4 + 512 * 2)
    lsb = A.alloc(f"l{tag}", 2048 * 4)
    lcb = A.alloc(f"lc{tag}", 8 * 4)
    gq = A.alloc(f"gq{tag}", 4 * 2)
    grg = A.alloc(f"grg{tag}", 4 * 4)
    zsb = A.alloc(f"z{tag}", 4 * 256 * 2)
    z_ap = zsb.view(BF16, (128, 4, 256))

    _flc = {}

    def get_chunk_f(qc):
        if qc not in _flc:
            bs, aps, rb = load_chunk(qc)
            load_rope(qc, rb)
            _flc[qc] = (bs, aps, rb)
        return _flc[qc]

    for qc in range(NCH):
        bs, aps, rb = get_chunk_f(qc)
        if qc + 1 < NCH:
            get_chunk_f(qc + 1)
        bufs = [bs[0].buf, bs[1].buf]
        cos_all, ss_all = rope_tiles(rb)
        r_, t0_ = qc // 2, (qc % 2) * 4
        gq_ap = gq.view(BF16, (128, 4))
        P.dma("sp", lambda e, o=gq_ap, i=agov[1][r_, :, HGW + t0_:HGW + t0_ + 4]:
              e.dma_start(out=o, in_=i), reads=[ago[1][1]], writes=[gq.buf],
              tag=f"ldgq{tag}{qc}")
        for tb in range(4):
            qbank, qps = project_tm(aps, bufs, wq_ap, wq_r.buf, tb, 256)
            for h in range(NH_CORE):
                q_ap = qsc.view(BF16, (128, 256))[:, h * 128:(h + 1) * 128]
                rope_apply(qps, qbank, cos_all[:, tb, :], ss_all[:, tb, :], rb, h,
                           q_ap, qsc.buf)
            PS.free(qbank)
            trbq = PS.alloc(f"qtr{tag}")
            tr_ps = trbq.view(BF16, (128, 2, 128))

            def em_qtr(e, tr_ps=tr_ps):
                i0 = e.transpose(tr_ps[:, 0, :], qsc.view(BF16, (128, 256))[:, 0:128], ident)
                i1 = e.transpose(tr_ps[:, 1, :], qsc.view(BF16, (128, 256))[:, 128:256], ident)
                return (i0, i1)
            P.op("pe", em_qtr, reads=[qsc.buf, ident_b], writes=[trbq.buf],
                 tag=f"qtr{tag}{qc}{tb}")

            def em_qc2(e, tr_ps=tr_ps, tb=tb):
                i0 = e.tensor_copy(qfm_ap[:, 0, tb * 128:(tb + 1) * 128], tr_ps[:, 0, :])
                i1 = e.tensor_copy(qfm_ap[:, 1, tb * 128:(tb + 1) * 128], tr_ps[:, 1, :])
                return (i0, i1)
            P.op("dve", em_qc2, reads=[trbq.buf], writes=[qfm.buf], tag=f"qc{tag}{qc}{tb}")
            PS.free(trbq)
        for h in range(NH_CORE):
            bank = PS.alloc(f"qr{tag}")
            ps = bank.view(F32, (64, 512))
            P.op("pe", lambda e, ps=ps, h=h:
                 e.matmul(ps, pk_ap, qfm_ap[:, h, :], start=True, stop=True),
                 reads=[qfm.buf, pk_r.buf], writes=[bank.buf], tag=f"qr_mm{tag}{qc}{h}")
            dst = qr_ap[64 * h:64 * (h + 1), :]
            P.op("dve", lambda e, ps=ps, dst=dst: e.tensor_copy(dst, ps),
                 reads=[bank.buf], writes=[qr.buf], tag=f"qr_c{tag}{qc}{h}")
            PS.free(bank)

        for h in range(NH_CORE):
            j0 = 4 if qc == 0 else 0
            obank = PS.alloc(f"ol{tag}")
            lbank = PS.alloc(f"ll{tag}")
            ov = obank.view(F32, (128, 512))
            lv = lbank.view(F32, (1, 512))
            sbanks = [PS.alloc(f"s{tag}{i}") for i in range(6)]
            groups = [list(range(g, min(g + 3, 8))) for g in range(j0, 8, 3)]

            def sl_grp(gi, h=h, qc=qc, sbanks=sbanks, groups=groups):
                js = groups[gi]

                def em(e, js=js, h=h, qc=qc):
                    first = None
                    for j in js:
                        kbg = qc * 4 + j - 4
                        sv = sbanks[j % 6].view(F32, (128, 512))
                        last = e.matmul(sv, kfm_ap[:, h, kbg * 128:(kbg + 1) * 128],
                                        qfm_ap[:, h, :], start=True, stop=True)
                        first = first or last
                    return (first, last)
                P.op("pe", em, reads=[kfm.buf, qfm.buf],
                     writes=[sbanks[j % 6].buf for j in js], tag=f"slg{tag}{qc}{h}{gi}")
            sl_grp(0)
            if len(groups) > 1:
                sl_grp(1)
            for gi, js in enumerate(groups):
                def em_exp(e, js=js, sbanks=sbanks):
                    first = None
                    for j in js:
                        sv = sbanks[j % 6].view(F32, (128, 512))
                        p_ap = psb[j % 6].view(BF16, (128, 512))
                        last = e.activation(p_ap, sv, AF.Exp, scale=SC1)
                        first = first or last
                    return (first, last)
                P.op("act", em_exp, reads=[sbanks[j % 6].buf for j in js],
                     writes=[psb[j % 6].buf for j in js], tag=f"lexpg{tag}{qc}{h}{gi}")

                def em_msk(e, js=js):
                    first = None
                    for j in js:
                        p_ap = psb[j % 6].view(BF16, (128, 512))
                        last = e.tensor_mul(p_ap, p_ap, masksb[:, j, :])
                        first = first or last
                    return (first, last)
                P.op("dve", em_msk, reads=[psb[j % 6].buf for j in js] + [masksb_b],
                     writes=[psb[j % 6].buf for j in js], tag=f"lmskg{tag}{qc}{h}{gi}")
                if gi + 2 < len(groups):
                    sl_grp(gi + 2)

                def em_pvl(e, js=js, h=h, qc=qc, ov=ov, lv=lv, j0=j0):
                    first = None
                    for j in js:
                        kbg = qc * 4 + j - 4
                        p_ap = psb[j % 6].view(BF16, (128, 512))
                        i0 = e.matmul(ov, vtm_ap[:, kbg, h * 128:(h + 1) * 128], p_ap,
                                      start=(j == j0), stop=(j == 7))
                        first = first or i0
                        last = e.matmul(lv, ones_col, p_ap, start=(j == j0), stop=(j == 7))
                    return (first, last)
                P.op("pe", em_pvl,
                     reads=[vtm.buf, ones_col_b] + [psb[j % 6].buf for j in js],
                     writes=[obank.buf, lbank.buf], tag=f"lpvlg{tag}{qc}{h}{gi}")
            ol_ap = olsb.view(BF16, (128, 2, 512))[:, h, :]
            P.op("act", lambda e, ol_ap=ol_ap, ov=ov: e.copy(ol_ap, ov),
                 reads=[obank.buf], writes=[olsb.buf], tag=f"olev{tag}{qc}{h}")
            l_ap = lsb.view(F32, (1, 2048))[:, h * 512:(h + 1) * 512]
            P.op("act", lambda e, l_ap=l_ap, lv=lv: e.copy(l_ap, lv),
                 reads=[lbank.buf], writes=[lsb.buf], tag=f"llev{tag}{qc}{h}")
            for r in sbanks + [obank, lbank]:
                PS.free(r)

        NKB = SEQ // 128
        ogbanks = [PS.alloc(f"og{tag}{i}") for i in range(4)]
        ogvs = [b.view(F32, (65, 512)) for b in ogbanks]
        sgbanks = [PS.alloc(f"sg{tag}{i}") for i in range(4)]

        def sg_pair(kb, qc=qc, sgbanks=sgbanks):
            def em(e, kb=kb, sgbanks=sgbanks):
                i0 = e.matmul(sgbanks[(kb % 2) * 2 + 0].view(F32, (128, 512)),
                              krf_ap[0:64, kb * 128:(kb + 1) * 128],
                              qr_ap[0:64, :], start=True, stop=True)
                i1 = e.matmul(sgbanks[(kb % 2) * 2 + 1].view(F32, (128, 512)),
                              krf_ap[64:128, kb * 128:(kb + 1) * 128],
                              qr_ap[64:128, :], start=True, stop=True)
                return (i0, i1)
            P.op("pe", em, reads=[krf.buf, qr.buf],
                 writes=[sgbanks[(kb % 2) * 2].buf, sgbanks[(kb % 2) * 2 + 1].buf],
                 tag=f"sgp{tag}{qc}{kb}")
        sg_pair(0)
        sg_pair(1)
        for kb in range(NKB):
            par = (kb % 2) * 2

            def em_exp(e, par=par, sgbanks=sgbanks):
                i0 = e.activation(psb[par].view(BF16, (128, 512)),
                                  sgbanks[par].view(F32, (128, 512)), AF.Exp, scale=SC2)
                i1 = e.activation(psb[par + 1].view(BF16, (128, 512)),
                                  sgbanks[par + 1].view(F32, (128, 512)), AF.Exp, scale=SC2)
                return (i0, i1)
            P.op("act", em_exp, reads=[sgbanks[par].buf, sgbanks[par + 1].buf],
                 writes=[psb[par].buf, psb[par + 1].buf], tag=f"gexp{tag}{qc}{kb}")
            if kb + 2 < NKB:
                sg_pair(kb + 2)

            def em_gpv(e, kb=kb, par=par, ogvs=ogvs):
                first = None
                for h in range(NH_CORE):
                    p_ap = psb[par + h].view(BF16, (128, 512))
                    for kh in range(2):
                        last = e.matmul(ogvs[h * 2 + kh],
                                        vrt_ap[64 * kh:64 * (kh + 1), kb, h * 65:(h + 1) * 65],
                                        p_ap[64 * kh:64 * (kh + 1), :],
                                        start=(kb == 0), stop=(kb == NKB - 1))
                        first = first or last
                return (first, last)
            P.op("pe", em_gpv,
                 reads=[vrt.buf, psb[par].buf, psb[par + 1].buf],
                 writes=[b.buf for b in ogbanks], tag=f"gpv{tag}{qc}{kb}")
        for r in sgbanks:
            PS.free(r)

        for h in range(NH_CORE):
            ol_ap = olsb.view(BF16, (128, 2, 512))[:, h, :]
            og_ap = ogsb.view(BF16, (64, 512))
            oga, ogb_ = ogvs[h * 2], ogvs[h * 2 + 1]
            og_f32 = cgsb.view(F32, (64, 512))
            P.op("dve", lambda e, og_f32=og_f32, oga=oga: e.tensor_copy(og_f32, oga[0:64, :]),
                 reads=[ogbanks[h * 2].buf], writes=[cgsb.buf], tag=f"oge1{tag}{qc}{h}")
            P.op("dve", lambda e, og_f32=og_f32, ogb_=ogb_, og_ap=og_ap:
                 e.tensor_tensor(og_ap, og_f32, ogb_[0:64, :], ALU.add),
                 reads=[ogbanks[h * 2 + 1].buf, cgsb.buf], writes=[ogsb.buf],
                 tag=f"oge2{tag}{qc}{h}")
            lg_ap = lsb.view(F32, (1, 2048))[:, 1024 + h * 512:1024 + (h + 1) * 512]
            P.op("act", lambda e, lg_ap=lg_ap, oga=oga: e.copy(lg_ap, oga[64:65, :]),
                 reads=[ogbanks[h * 2].buf], writes=[lsb.buf], tag=f"lge1{tag}{qc}{h}")
            P.op("dve", lambda e, lg_ap=lg_ap, ogb_=ogb_:
                 e.tensor_tensor(lg_ap, lg_ap, ogb_[64:65, :], ALU.add),
                 reads=[ogbanks[h * 2 + 1].buf, lsb.buf], writes=[lsb.buf],
                 tag=f"lge2{tag}{qc}{h}")
            cgbank = PS.alloc(f"cg{tag}")
            cgv = cgbank.view(F32, (128, 512))
            P.op("pe", lambda e, cgv=cgv, og_ap=og_ap: e.matmul(cgv, uo_ap, og_ap, start=True, stop=True),
                 reads=[ogsb.buf, uo_r.buf], writes=[cgbank.buf], tag=f"cup{tag}{qc}{h}")
            cg_ap = cgsb.view(BF16, (128, 512), byte_off=2048)
            P.op("act", lambda e, cg_ap=cg_ap, cgv=cgv: e.copy(cg_ap, cgv),
                 reads=[cgbank.buf], writes=[cgsb.buf], tag=f"cupev{tag}{qc}{h}")
            PS.free(cgbank)
            lcbank = PS.alloc(f"lc{tag}")
            lc_ps = lcbank.view(F32, (128, 8))
            lboth = lsb.view(F32, (1, 2048))

            def em_lc(e, lc_ps=lc_ps, lboth=lboth, h=h):
                first = None
                for row in range(2):
                    base = h * 512 if row == 0 else 1024 + h * 512
                    for j in range(4):
                        last = e.matmul(lc_ps[:, row * 4 + j:row * 4 + j + 1],
                                        lboth[0:1, base + j * 128:base + (j + 1) * 128],
                                        onesf, start=True, stop=True)
                        first = first or last
                return (first, last)
            P.op("pe", em_lc, reads=[lsb.buf, onesf_b], writes=[lcbank.buf],
                 tag=f"lc{tag}{qc}{h}")
            lc_ap = lcb.view(F32, (128, 8))
            P.op("dve", lambda e, lc_ap=lc_ap, lc_ps=lc_ps: e.reciprocal(lc_ap, lc_ps),
                 reads=[lcbank.buf], writes=[lcb.buf], tag=f"lcr{tag}{qc}{h}")
            PS.free(lcbank)
            grg_ap = grg.view(F32, (128, 4))
            P.op("dve", lambda e, grg_ap=grg_ap, lc_ap=lc_ap, gq_ap=gq_ap:
                 e.tensor_mul(grg_ap, lc_ap[:, 4:8], gq_ap),
                 reads=[lcb.buf, gq.buf], writes=[grg.buf], tag=f"grg{tag}{qc}{h}")
            t1bank = PS.alloc(f"olT{tag}")
            t2bank = PS.alloc(f"cgT{tag}")
            t1 = t1bank.view(BF16, (128, 4, 128))
            t2 = t2bank.view(BF16, (128, 4, 128))

            def em_trs(e, t1=t1, t2=t2, ol_ap=ol_ap, cg_ap=cg_ap):
                first = None
                for j in range(4):
                    i0 = e.transpose(t1[:, j, :], ol_ap[:, j * 128:(j + 1) * 128], ident)
                    first = first or i0
                for j in range(4):
                    last = e.transpose(t2[:, j, :], cg_ap[:, j * 128:(j + 1) * 128], ident)
                return (first, last)
            P.op("pe", em_trs, reads=[olsb.buf, cgsb.buf, ident_b],
                 writes=[t1bank.buf, t2bank.buf], tag=f"ctrs{tag}{qc}{h}")

            def em_comb(e, t1=t1, lc_ap=lc_ap, h=h):
                first = None
                for j in range(4):
                    i0 = e.tensor_scalar(z_ap[:, j, h * 128:(h + 1) * 128], t1[:, j, :],
                                         lc_ap[:, j:j + 1], None, ALU.mult)
                    first = first or i0
                return (first, i0)
            P.op("dve", em_comb, reads=[t1bank.buf, lcb.buf], writes=[zsb.buf],
                 tag=f"comb1{tag}{qc}{h}")

            def em_comb2(e, t2=t2, grg_ap=grg_ap, h=h):
                first = None
                for j in range(4):
                    i0 = e.scalar_tensor_tensor(z_ap[:, j, h * 128:(h + 1) * 128], t2[:, j, :],
                                                grg_ap[:, j:j + 1],
                                                z_ap[:, j, h * 128:(h + 1) * 128],
                                                ALU.mult, ALU.add)
                    first = first or i0
                return (first, i0)
            P.op("dve", em_comb2, reads=[t2bank.buf, grg.buf, zsb.buf], writes=[zsb.buf],
                 tag=f"comb2{tag}{qc}{h}")
            PS.free(t1bank)
            PS.free(t2bank)
        for r in ogbanks:
            PS.free(r)
        zc_v = z_in.rearrange("(qc tb p) f -> qc p tb f", tb=4, p=128)
        P.dma("sp", lambda e, o=zc_v[qc], i=z_ap: e.dma_start(out=o, in_=i),
              reads=[zsb.buf], writes=[z_in_b], tag=f"zctr{tag}{qc}")

    for r in [kfm, vtm, krf, vrt, qfm, qr, qsc, olsb, ogsb, cgsb, lsb, lcb, gq, grg,
              zsb, ksc, rsc, wvr, wq_r, wk_r, wv_r, pk_r, uo_r] + psb + \
             [x for pair in xnc for x in pair] + ropeb:
        A.free(r)

    oT_ap_l, oT_b_l = layer["oT"]
    wo_pre = [A.alloc(f"wo{tag}{i}", NCC * 512 * 2) for i in range(2)]
    for oc in range(2):
        wap = wo_pre[oc].view(BF16, (128, NCC, 512))
        P.dma("pool", lambda e, o=wap, i=oT_ap_l[oc]: e.dma_start(out=o, in_=i),
              reads=[oT_b_l], writes=[wo_pre[oc].buf], tag=f"ldwo{tag}{oc}")
    if not hasattr(C, "wo_pre"):
        C.wo_pre = {}
    C.wo_pre[tag] = wo_pre

    P.collective(lambda e: e.collective_compute(
        "AllGather", ALU.bypass, replica_groups=[core_grp],
        ins=[z_in.opt()], outs=[z_out.opt()]),
        reads=[z_in_b], writes=[z_out_b], tag=f"zag{tag}")
    build_outproj(C, layer["oT"], z_out, z_out_b, x_dram_ap, x_dram_buf,
                  out_dram_ap, out_dram_buf, tag=tag)


CAP = 512


def build_ln2(C, x_dram_ap, x_dram_buf, xn_fm_r, xn_tm_dram, xn_tm_dram_b, tag=""):
    P, A, PS = C.P, C.A, C.PS
    ident = C.const["ident"][1]
    ident_b = C.const["ident"][0].buf
    xs = [A.alloc(f"lnx{tag}{i}", 2048 * 4) for i in range(2)]
    sc = A.alloc(f"lnsc{tag}", 32 * 4)
    xn_tm = [A.alloc(f"lnxn{tag}{i}", 2048 * 2) for i in range(2)]
    xn_fm_ap = xn_fm_r.view(BF16, (128, NCC, TCORE))
    for tt in range(NTT):
        xb = xs[tt % 2]
        x_ap = xb.view(F32, (128, 2048))
        P.dma(ldq[tt % len(ldq)], lambda e, o=x_ap, i=x_dram_ap[tt * 128:(tt + 1) * 128, :]:
              e.dma_start(out=o, in_=i), reads=[tile_buf(x_dram_buf, tt)],
              writes=[xb.buf], tag=f"ln_ldx{tag}{tt}")
        st = sc.view(F32, (128, 24))
        mv = sc.view(F32, (128, 2), byte_off=96)
        rstd = sc.view(F32, (128, 1), byte_off=104)

        def em_stats(e, x_ap=x_ap, st=st):
            first = None
            for j in range(4):
                last = e.bn_stats(st[:, j * 6:(j + 1) * 6], x_ap[:, j * 512:(j + 1) * 512])
                first = first or last
            return (first, last)
        P.op("dve", em_stats, reads=[xb.buf], writes=[sc.buf], tag=f"ln_st{tag}{tt}")
        P.op("dve", lambda e, st=st, mv=mv: e.bn_aggr(mv, st),
             reads=[sc.buf], writes=[sc.buf])
        P.op("act", lambda e, mv=mv, rstd=rstd: e.activation(rstd, mv[:, 1:2], AF.Sqrt, bias=1e-5),
             reads=[sc.buf], writes=[sc.buf])
        P.op("dve", lambda e, rstd=rstd: e.reciprocal(rstd, rstd),
             reads=[sc.buf], writes=[sc.buf])
        xnb = xn_tm[tt % 2]
        xn_ap = xnb.view(BF16, (128, 2048))
        P.op("dve", lambda e, x_ap=x_ap, xn_ap=xn_ap, mv=mv, rstd=rstd:
             e.tensor_scalar(xn_ap, x_ap, mv[:, 0:1], rstd[:, 0:1], ALU.subtract, ALU.mult),
             reads=[xb.buf, sc.buf], writes=[xnb.buf], tag=f"ln_nm{tag}{tt}")
        P.dma("sp", lambda e, o=xn_tm_dram[tt * 128:(tt + 1) * 128, :], i=xn_ap:
              e.dma_start(out=o, in_=i), reads=[xnb.buf], writes=[xn_tm_dram_b],
              tag=f"ln_sttm{tag}{tt}")
        for g in range(4):
            bank = PS.alloc(f"lnT{tag}{tt}{g}")
            bap = bank.view(BF16, (128, 4, 128))

            def em_tr(e, xn_ap=xn_ap, bap=bap, g=g):
                first = None
                for j in range(4):
                    cc = g * 4 + j
                    last = e.transpose(bap[:, j, :], xn_ap[:, cc * 128:(cc + 1) * 128], ident)
                    first = first or last
                return (first, last)
            P.op("pe", em_tr, reads=[xnb.buf, ident_b], writes=[bank.buf], tag=f"ln_tr{tag}{tt}{g}")
            dst = xn_fm_ap[:, g * 4:(g + 1) * 4, tt * 128:(tt + 1) * 128]
            P.op("dve", lambda e, bap=bap, dst=dst: e.tensor_copy(dst, bap),
                 reads=[bank.buf], writes=[xn_fm_r.buf], tag=f"ln_tc{tag}{tt}{g}")
            PS.free(bank)
    for b in xs + xn_tm + [sc]:
        A.free(b)


def build_ffn_sparse(C, layer, xn_fm_r, xn_tm_dram, xn_tm_dram_b,
                     x_dram_ap, x_dram_buf, out_dram_ap, out_dram_buf, tag=""):
    P, A, PS = C.P, C.A, C.PS
    nc = C.nc
    xn_fm = xn_fm_r.view(BF16, (128, NCC, TCORE))
    w = layer
    ident = C.const["ident"][1]
    ident_b = C.const["ident"][0].buf
    onesf = C.const["onesf"][1]
    onesf_b = C.const["onesf"][0].buf
    iota_c = C.const["iota_col"][1]
    iota_b = C.const["iota_col"][0].buf
    zrow = C.const["zrow"][1]
    zrow_b = C.const["zrow"][0].buf
    oobrow = C.const["oobrow"][1]
    oobrow_b = C.const["oobrow"][0].buf

    y_dram, y_dram_b = C.dram_tmp(f"ydr{tag}", [TCORE, 2048], F32)
    idxd = [C.dram_tmp(f"idx{tag}{e}", [CAP, 1], mybir.dt.int32) for e in range(NE)]
    wvd = [C.dram_tmp(f"wvd{tag}{e}", [CAP, 1], F32) for e in range(NE)]

    gw_ap = C.const["gate_wT"][1]
    gw_b = C.const["gate_wT"][0].buf
    comb = A.alloc(f"comb{tag}", NTT * NE * 4)
    comb_ap = comb.view(F32, (128, NTT, NE))
    rsc = A.alloc(f"rsc{tag}", 64 * 4)
    for tt in range(NTT):
        bank = PS.alloc(f"rt{tag}{tt}")
        ps = bank.view(F32, (128, NE))

        def em_rt(e, ps=ps, tt=tt):
            first = None
            for cc in range(NCC):
                last = e.matmul(ps, xn_fm[:, cc, tt * 128:(tt + 1) * 128], gw_ap[:, cc, :],
                                start=(cc == 0), stop=(cc == NCC - 1))
                first = first or last
            return (first, last)
        P.op("pe", em_rt, reads=[xn_fm_r.buf, gw_b], writes=[bank.buf], tag=f"rt_mm{tag}{tt}")
        ex = rsc.view(F32, (128, NE))
        s1 = rsc.view(F32, (128, 1), byte_off=32)
        sc_v = rsc.view(F32, (128, NE), byte_off=48)
        m1 = rsc.view(F32, (128, 1), byte_off=96)
        msk = rsc.view(F32, (128, NE), byte_off=112)
        m2 = rsc.view(F32, (128, 1), byte_off=160)
        ge = rsc.view(F32, (128, NE), byte_off=176)
        b = rsc.buf
        cv = comb_ap[:, tt, :]
        P.op("act", lambda e, ps=ps, ex=ex: e.activation(ex, ps, AF.Exp),
             reads=[bank.buf], writes=[b], tag=f"rt_exp{tag}{tt}")
        P.op("dve", lambda e, ex=ex, s1=s1: e.reduce_sum(s1, ex, axis=AX.X), reads=[b], writes=[b])
        P.op("dve", lambda e, s1=s1: e.reciprocal(s1, s1), reads=[b], writes=[b])
        P.op("dve", lambda e, ex=ex, s1=s1, sc_v=sc_v: e.tensor_scalar(sc_v, ex, s1, None, ALU.mult),
             reads=[b], writes=[b])
        P.op("dve", lambda e, sc_v=sc_v, m1=m1: e.reduce_max(m1, sc_v, axis=AX.X), reads=[b], writes=[b])
        P.op("dve", lambda e, sc_v=sc_v, m1=m1, msk=msk: e.tensor_scalar(msk, sc_v, m1, None, ALU.is_equal),
             reads=[b], writes=[b])
        P.op("dve", lambda e, msk=msk, sc_v=sc_v: e.scalar_tensor_tensor(msk, msk, -1e30, sc_v, ALU.mult, ALU.add),
             reads=[b], writes=[b])
        P.op("dve", lambda e, msk=msk, m2=m2: e.reduce_max(m2, msk, axis=AX.X), reads=[b], writes=[b])
        P.op("dve", lambda e, sc_v=sc_v, m2=m2, ge=ge: e.tensor_scalar(ge, sc_v, m2, None, ALU.is_ge),
             reads=[b], writes=[b])
        P.op("dve", lambda e, sc_v=sc_v, ge=ge, cv=cv: e.tensor_mul(cv, sc_v, ge),
             reads=[b], writes=[comb.buf], tag=f"rt_comb{tag}{tt}")
        PS.free(bank)
    A.free(rsc)

    cfm = A.alloc(f"cfm{tag}", NE * TCORE * 4)
    cfm_ap = cfm.view(F32, (NE, TCORE))
    for half in range(2):
        cfb = PS.alloc(f"cf{tag}{half}")
        cf_ps = cfb.view(F32, (NE, 4, 128))

        identf = C.const["identf"][1]
        identf_b = C.const["identf"][0].buf

        def em_ctr(e, cf_ps=cf_ps, half=half, identf=identf):
            first = None
            for j in range(4):
                last = e.transpose(cf_ps[:, j, :], comb_ap[:, half * 4 + j, :], identf)
                first = first or last
            return (first, last)
        P.op("pe", em_ctr, reads=[comb.buf, identf_b], writes=[cfb.buf], tag=f"ctr{tag}{half}")
        dst = cfm_ap.rearrange("p (h t) -> p h t", h=2)[:, half, :]
        P.op("dve", lambda e, cf_ps=cf_ps, dst=dst:
             e.tensor_copy(dst, cf_ps.rearrange("p a b -> p (a b)")),
             reads=[cfb.buf], writes=[cfm.buf], tag=f"ctrc{tag}{half}")
        PS.free(cfb)

    ssc = A.alloc(f"ssc{tag}", 2 * TCORE * 4)
    slotc = A.alloc(f"slc{tag}", NE * NTT * 4)
    slotc_i = slotc.view(mybir.dt.int32, (128, NE, NTT))
    identf = C.const["identf"][1]
    identf_b = C.const["identf"][0].buf
    mrow = ssc.view(F32, (NE, TCORE))
    trow = ssc.view(F32, (NE, TCORE), byte_off=TCORE * 4)
    P.op("dve", lambda e: e.tensor_single_scalar(mrow, cfm_ap, 0.0, ALU.is_gt),
         reads=[cfm.buf], writes=[ssc.buf], tag=f"smask{tag}")
    P.op("dve", lambda e: e.tensor_tensor_scan(trow, mrow, zrow, 0.0, ALU.add, ALU.add),
         reads=[ssc.buf, zrow_b], writes=[ssc.buf], tag=f"sscan{tag}")
    P.op("dve", lambda e: e.tensor_scalar_add(trow, trow, -4097.0),
         reads=[ssc.buf], writes=[ssc.buf])
    P.op("dve", lambda e: e.tensor_mul(trow, trow, mrow),
         reads=[ssc.buf], writes=[ssc.buf])
    P.op("dve", lambda e: e.tensor_scalar_add(trow, trow, 4096.0),
         reads=[ssc.buf], writes=[ssc.buf], tag=f"sfix{tag}")
    for tt in range(NTT):
        scb = PS.alloc(f"sc{tag}")
        sc_ps = scb.view(F32, (128, NE))
        P.op("pe", lambda e, sc_ps=sc_ps, tt=tt:
             e.transpose(sc_ps, trow[:, tt * 128:(tt + 1) * 128], identf[0:NE, 0:NE]),
             reads=[ssc.buf, identf_b], writes=[scb.buf], tag=f"scol{tag}{tt}")
        P.op("dve", lambda e, sc_ps=sc_ps, tt=tt:
             e.tensor_copy(slotc_i[:, :, tt], sc_ps),
             reads=[scb.buf], writes=[slotc.buf], tag=f"scolc{tag}{tt}")
        PS.free(scb)
    for e_i in range(NE):
        P.dma("sp", lambda e, e_i=e_i: e.dma_start(out=wvd[e_i][0].rearrange("c one -> one c"),
                                                   in_=zrow[0:1, 0:CAP]),
              reads=[zrow_b], writes=[wvd[e_i][1]], tag=f"wz{tag}{e_i}")
        P.dma("pool", lambda e, e_i=e_i: e.dma_start(out=idxd[e_i][0].rearrange("c one -> one c"),
                                                     in_=zrow[0:1, 0:CAP], accum_op=ALU.bypass)
              if False else
              e.dma_start(out=idxd[e_i][0].rearrange("c one -> one c"), in_=oobrow[0:1, 0:CAP]),
              reads=[oobrow_b], writes=[idxd[e_i][1]], tag=f"iz{tag}{e_i}")
        for tt in range(NTT):
            off = IndirectOffsetOnAxis(slotc_i[:, e_i, tt:tt + 1], 0)
            P.dma("pool", lambda e, e_i=e_i, tt=tt, off=off:
                  e.indirect_dma_start(out=idxd[e_i][0], out_offset=off,
                                       in_=iota_c[:, tt:tt + 1], in_offset=None,
                                       bounds_check=CAP - 1, oob_is_err=False),
                  reads=[slotc.buf, iota_b], writes=[idxd[e_i][1]], tag=f"sidx{tag}{e_i}{tt}")
            P.dma("pool", lambda e, e_i=e_i, tt=tt, off=off:
                  e.indirect_dma_start(out=wvd[e_i][0], out_offset=off,
                                       in_=comb.view(F32, (128, NTT * NE))[:, tt * NE + e_i:tt * NE + e_i + 1],
                                       in_offset=None,
                                       bounds_check=CAP - 1, oob_is_err=False),
                  reads=[slotc.buf, comb.buf, wvd[e_i][1]], writes=[wvd[e_i][1]],
                  tag=f"swv{tag}{e_i}{tt}")
    A.free(ssc)
    A.free(cfm)

    wgb = [A.alloc(f"wg{tag}{i}", NCC * 128 * 2) for i in range(3)]
    wub = [A.alloc(f"wu{tag}{i}", NCC * 128 * 2) for i in range(3)]
    hb = A.alloc(f"h{tag}", NI * TCORE * 2)
    sg = A.alloc(f"sg{tag}", 4 * 512 * 2)
    eo = A.alloc(f"eo{tag}", 2048 * 4)
    eo_ap = eo.view(F32, (128, 2048))
    idxw = A.alloc(f"ixw{tag}", 2 * 4 * 4)
    xng_t = A.alloc(f"xgt{tag}", 2048 * 2)
    xng = A.alloc(f"xng{tag}", NCC * CAP * 2)
    xng_ap = xng.view(BF16, (128, NCC, CAP))
    P.op("dve", lambda e: e.memset(xng_t.view(BF16, (128, 2048)), 0.0),
         writes=[xng_t.buf], tag=f"xgz{tag}")

    wgl, wgl_b = w["wg"]
    wul, wul_b = w["wu"]
    wdl, wdl_b = w["wd"]
    sgl, sgl_b = w["swg"]
    sul, sul_b = w["swu"]
    sdl, sdl_b = w["swd"]

    def expert(n_ib, n_tok, wg_src, wu_src, wd_src, xn_src_ap, xn_src_buf,
               emit_out, su):
        h_ap = hb.view(BF16, (128, n_ib, TCORE))[:, :, 0:n_tok]
        wdb = [A.alloc(f"wd{tag}{su}{i}", 2048 * 2) for i in range(n_ib)]
        ntc = n_tok // 512
        for ib in range(n_ib):
            wg_r = wgb[ib % 3]
            wu_r = wub[ib % 3]
            dstg = wg_r.view(BF16, (128, NCC, 128))
            dstu = wu_r.view(BF16, (128, NCC, 128))
            if not (su == "e0" and ib < n_pre):
                ap_g, bg = wg_src(ib)
                ap_u, bu = wu_src(ib)
                P.dma("pool", lambda e, o=dstg, i=ap_g: e.dma_start(out=o, in_=i),
                      reads=[bg], writes=[wg_r.buf], tag=f"ldwg{tag}{su}{ib}")
                P.dma("pool", lambda e, o=dstu, i=ap_u: e.dma_start(out=o, in_=i),
                      reads=[bu], writes=[wu_r.buf], tag=f"ldwu{tag}{su}{ib}")
            psg = [PS.alloc(f"g{su}{ib}{tc}") for tc in range(ntc)]
            psu = [PS.alloc(f"u{su}{ib}{tc}") for tc in range(ntc)]

            def em_gu(e, dstg=dstg, dstu=dstu, psg=psg, psu=psu, ntc=ntc,
                      xn_src_ap=xn_src_ap):
                first = None
                for cc in range(NCC):
                    for tc in range(ntc):
                        i0 = e.matmul(psg[tc].view(F32, (128, 512)), dstg[:, cc, :],
                                      xn_src_ap[:, cc, tc * 512:(tc + 1) * 512],
                                      start=(cc == 0), stop=(cc == NCC - 1))
                        first = first or i0
                    for tc in range(ntc):
                        last = e.matmul(psu[tc].view(F32, (128, 512)), dstu[:, cc, :],
                                        xn_src_ap[:, cc, tc * 512:(tc + 1) * 512],
                                        start=(cc == 0), stop=(cc == NCC - 1))
                return (first, last)
            P.op("pe", em_gu, reads=[wg_r.buf, wu_r.buf, xn_src_buf],
                 writes=[r.buf for r in psg + psu], tag=f"gu{tag}{su}{ib}")
            for tc in range(ntc):
                sg_ap = sg.view(BF16, (128, 4, 512))[:, (ib * ntc + tc) % 4, :]
                P.op("act", lambda e, p=psg[tc].view(F32, (128, 512)), o=sg_ap:
                     e.activation(o, p, AF.Silu),
                     reads=[psg[tc].buf], writes=[sg.buf], tag=f"si{tag}{su}{ib}{tc}")
                hdst = h_ap[:, ib, tc * 512:(tc + 1) * 512]
                P.op("dve", lambda e, p=psu[tc].view(F32, (128, 512)), s=sg_ap, o=hdst:
                     e.tensor_mul(o, p, s),
                     reads=[psu[tc].buf, sg.buf], writes=[hb.buf], tag=f"hm{tag}{su}{ib}{tc}")
            for r in psg + psu:
                PS.free(r)
        for ib in range(n_ib):
            wd_r = wdb[ib]
            ap_d, bd = wd_src(ib)
            P.dma("pool", lambda e, o=wd_r.view(BF16, (128, 2048)), i=ap_d:
                  e.dma_start(out=o, in_=i),
                  reads=[bd], writes=[wd_r.buf], tag=f"ldwd{tag}{su}{ib}")
        for tt in range(n_tok // 128):
            banks = [PS.alloc(f"eo{su}{tt}{hc}") for hc in range(4)]
            pss = [b.view(F32, (128, 512)) for b in banks]

            def em_dn(e, pss=pss, tt=tt, n_ib=n_ib, h_ap=h_ap, wdb=wdb):
                first = None
                for ib in range(n_ib):
                    wd_ap = wdb[ib].view(BF16, (128, 2048))
                    for hc in range(4):
                        last = e.matmul(pss[hc], h_ap[:, ib, tt * 128:(tt + 1) * 128],
                                        wd_ap[:, hc * 512:(hc + 1) * 512],
                                        start=(ib == 0), stop=(ib == n_ib - 1))
                        first = first or last
                return (first, last)
            P.op("pe", em_dn, reads=[hb.buf] + [r.buf for r in wdb],
                 writes=[b.buf for b in banks], tag=f"dn{tag}{su}{tt}")
            emit_out(tt, banks, pss)
            for b in banks:
                PS.free(b)
        for r in wdb:
            A.free(r)

    for k in range(2):
        def em_out(tt, banks, pss, k=k):
            for hc in range(4):
                P.op("dve", lambda e, ps=pss[hc], o=eo_ap[:, hc * 512:(hc + 1) * 512]:
                     e.tensor_copy(o, ps),
                     reads=[banks[hc].buf], writes=[eo.buf], tag=f"soev{tag}{k}{tt}{hc}")
            if k == 0:
                P.dma("sp", lambda e, tt=tt: e.dma_start(
                    out=y_dram[tt * 128:(tt + 1) * 128, :], in_=eo_ap),
                    reads=[eo.buf], writes=[y_dram_b], tag=f"yw{tag}{tt}")
            else:
                P.dma("pool", lambda e, tt=tt: e.dma_start(
                    out=y_dram[tt * 128:(tt + 1) * 128, :], in_=eo_ap, accum_op=ALU.add),
                    reads=[eo.buf, y_dram_b], writes=[y_dram_b], tag=f"ya{tag}{tt}")
        expert(NI, TCORE,
               lambda ib, k=k: (sgl[k * NI + ib], sgl_b),
               lambda ib, k=k: (sul[k * NI + ib], sul_b),
               lambda ib, k=k: (sdl[k * NI + ib], sdl_b),
               xn_fm, xn_fm_r.buf, em_out, su=f"sh{k}")

    for e_i in range(NE):
        idx_sb = idxw.view(mybir.dt.int32, (128, 4))
        w_sb = idxw.view(F32, (128, 4), byte_off=16)
        P.dma("sp", lambda e, e_i=e_i, idx_sb=idx_sb: e.dma_start(
            out=idx_sb, in_=idxd[e_i][0].rearrange("(st p) one -> p (st one)", p=128),
            allow_slow_non_contiguous=True),
            reads=[idxd[e_i][1]], writes=[idxw.buf], tag=f"ldix{tag}{e_i}")
        P.dma("sp", lambda e, e_i=e_i, w_sb=w_sb: e.dma_start(
            out=w_sb, in_=wvd[e_i][0].rearrange("(st p) one -> p (st one)", p=128),
            allow_slow_non_contiguous=True),
            reads=[wvd[e_i][1]], writes=[idxw.buf], tag=f"ldwv{tag}{e_i}")
        for st in range(4):
            P.dma("pool", lambda e, st=st, idx_sb=idx_sb: e.indirect_dma_start(
                out=xng_t.view(BF16, (128, 2048)), out_offset=None,
                in_=xn_tm_dram, in_offset=IndirectOffsetOnAxis(idx_sb[:, st:st + 1], 0),
                bounds_check=TCORE - 1, oob_is_err=False),
                reads=[xn_tm_dram_b, idxw.buf, xng_t.buf], writes=[xng_t.buf],
                tag=f"gx{tag}{e_i}{st}")
            xt = xng_t.view(BF16, (128, 2048))
            for g in range(4):
                bank = PS.alloc(f"xT{tag}")
                bap = bank.view(BF16, (128, 4, 128))

                def em_tr(e, bap=bap, g=g, xt=xt):
                    first = None
                    for j in range(4):
                        cc = g * 4 + j
                        last = e.transpose(bap[:, j, :], xt[:, cc * 128:(cc + 1) * 128], ident)
                        first = first or last
                    return (first, last)
                P.op("pe", em_tr, reads=[xng_t.buf, ident_b], writes=[bank.buf],
                     tag=f"xtr{tag}{e_i}{st}{g}")
                dst = xng_ap[:, g * 4:(g + 1) * 4, st * 128:(st + 1) * 128]
                P.op("dve", lambda e, bap=bap, dst=dst: e.tensor_copy(dst, bap),
                     reads=[bank.buf], writes=[xng.buf], tag=f"xtc{tag}{e_i}{st}{g}")
                PS.free(bank)

        def em_out(st, banks, pss, e_i=e_i, idx_sb=idx_sb, w_sb=w_sb):
            for hc in range(4):
                P.op("dve", lambda e, ps=pss[hc], o=eo_ap[:, hc * 512:(hc + 1) * 512],
                     w_sb=w_sb, st=st:
                     e.tensor_scalar(o, ps, w_sb[:, st:st + 1], None, ALU.mult),
                     reads=[banks[hc].buf, idxw.buf], writes=[eo.buf],
                     tag=f"eoev{tag}{e_i}{st}{hc}")
            P.dma("pool", lambda e, st=st, idx_sb=idx_sb: e.indirect_dma_start(
                out=y_dram, out_offset=IndirectOffsetOnAxis(idx_sb[:, st:st + 1], 0),
                in_=eo_ap, in_offset=None, bounds_check=TCORE - 1, oob_is_err=False,
                compute_op=ALU.add),
                reads=[eo.buf, idxw.buf, y_dram_b], writes=[y_dram_b],
                tag=f"ysc{tag}{e_i}{st}")
        expert(NI, CAP,
               lambda ib, e_i=e_i: (wgl[e_i, ib], wgl_b),
               lambda ib, e_i=e_i: (wul[e_i, ib], wul_b),
               lambda ib, e_i=e_i: (wdl[e_i, ib], wdl_b),
               xng_ap, xng.buf, em_out, su=f"e{e_i}")

    for r in wgb + wub + [hb, sg, eo, idxw, xng_t, xng, comb, slotc]:
        A.free(r)

    xs = [A.alloc(f"rx{tag}{i}", 2048 * 4) for i in range(2)]
    ys = [A.alloc(f"ry{tag}{i}", 2048 * 4) for i in range(2)]
    for tt in range(NTT):
        xb, yb = xs[tt % 2], ys[tt % 2]
        x_ap = xb.view(F32, (128, 2048))
        y_ap = yb.view(F32, (128, 2048))
        P.dma("sp", lambda e, o=x_ap, i=x_dram_ap[tt * 128:(tt + 1) * 128, :]:
              e.dma_start(out=o, in_=i), reads=[tile_buf(x_dram_buf, tt)],
              writes=[xb.buf], tag=f"res_ldx{tag}{tt}")
        P.dma("sp", lambda e, o=y_ap, i=y_dram[tt * 128:(tt + 1) * 128, :]:
              e.dma_start(out=o, in_=i), reads=[y_dram_b], writes=[yb.buf], tag=f"res_ldy{tag}{tt}")
        P.op("dve", lambda e, x_ap=x_ap, y_ap=y_ap: e.tensor_tensor(x_ap, x_ap, y_ap, ALU.add),
             reads=[xb.buf, yb.buf], writes=[xb.buf], tag=f"res_add{tag}{tt}")
        P.dma("sp", lambda e, o=out_dram_ap[tt * 128:(tt + 1) * 128, :], i=x_ap:
              e.dma_start(out=o, in_=i), reads=[xb.buf],
              writes=[tile_buf(out_dram_buf, tt)], tag=f"res_st{tag}{tt}")
    for r in xs + ys:
        A.free(r)






import contextlib
import time


def _build_program():
    nc = bass.Bass("TRN2", target_bir_lowering=False, debug=False, num_devices=8)
    P = Prog()
    es = contextlib.ExitStack()
    A = SbufArena(nc, es, "arena", 205 * 1024)
    PS = PsumPool(nc, es)
    C = Ctx(nc, P, A, PS)

    x_in, x_in_b = C.dram_in("x", [1024, 2048])
    x_out, x_out_b = C.dram_out("xo", [1024, 2048])
    xb1, xb1_b0 = C.dram_tmp("xbuf", [1024, 2048])
    xb1_b = [Buf(f"xbuf_t{t}") for t in range(8)]
    ident_in, ident_in_b = C.dram_in("ident", [128, 128], BF16)
    cv_in, cv_in_b = C.dram_in("cvals", [128, len(Ctx.CONST_VALS)])
    ma_in, ma_in_b = C.dram_in("masks_a", [128, 4, 512], BF16)
    mb_in, mb_in_b = C.dram_in("masks_b", [128, 8, 512], BF16)
    oc_in, oc_in_b = C.dram_in("ones_col", [128, 1], BF16)
    of_in, of_in_b = C.dram_in("onesf", [1, 1])
    gwt_in, gwt_in_b = C.dram_in("gate_wTl", [128, NCC, NE])

    layer_a = {
        "qT": C.dram_in("waQT", [128, NCC, 256]),
        "kT": C.dram_in("waKT", [128, NCC, 256]),
        "vT": C.dram_in("waVT", [128, NCC, 256]),
        "oT": C.dram_in("waOT", [4, 128, NCC, 512]),
    }
    layer_b = {
        "qT": C.dram_in("wbQT", [128, NCC, 256]),
        "kT": C.dram_in("wbKT", [128, NCC, 256]),
        "vT": C.dram_in("wbVT", [128, NCC, 256]),
        "vNat": C.dram_in("wbVnat", [128, 2, NCC, 128]),
        "oT": C.dram_in("wbOT", [4, 128, NCC, 512]),
        "pkT": C.dram_in("pkT", [128, 64]),
        "pvT": C.dram_in("pvT", [128, 64]),
        "uoT": C.dram_in("uoT", [64, 128]),
        "w1T": C.dram_in("w1T", [128, NCC, 512]),
        "w2T": C.dram_in("w2T", [128, 4, 1]),
        "rope_cos": C.dram_in("rope_cos", [16, 128, 4, 128], BF16),
        "rope_ss": C.dram_in("rope_ss", [16, 128, 4, 128], BF16),
    }
    layer_f = ffn_weight_inputs(C)

    C.setup_const_vals(cv_in, cv_in_b)
    C.load_const("ident", ident_in, ident_in_b, (128, 128), BF16, queue="sp")
    C.load_const("masks_a", ma_in, ma_in_b, (128, 4, 512), BF16, queue="sp")
    C.load_const("masks_b", mb_in, mb_in_b, (128, 8, 512), BF16, queue="sp")
    C.load_const("ones_col", oc_in, oc_in_b, (128, 1), BF16, queue="sp")
    C.load_const("onesf", of_in, of_in_b, (1, 1), F32, queue="sp")
    C.load_const("gate_wT", gwt_in, gwt_in_b, (128, NCC, NE), BF16, queue="pool")
    C.prologue_barrier([C.cval_buf] + [C.const[k][0].buf for k in
                                       ("ident", "masks_a", "masks_b", "ones_col",
                                        "onesf", "gate_wT")])
    grp = list(range(8))

    wgb1 = [A.alloc(f"wgf1{i}", NCC * 128 * 2) for i in range(3)]
    wub1 = [A.alloc(f"wuf1{i}", NCC * 128 * 2) for i in range(3)]
    wgl_, wgl_b_ = layer_f["wg"]
    wul_, wul_b_ = layer_f["wu"]
    for ib in range(3):
        dg = wgb1[ib].view(BF16, (128, NCC, 128))
        du = wub1[ib].view(BF16, (128, NCC, 128))
        P.dma("pool", lambda e, o=dg, i=wgl_[0, ib]: e.dma_start(out=o, in_=i),
              reads=[wgl_b_], writes=[wgb1[ib].buf], tag=f"ldwgf1e0{ib}")
        P.dma("pool", lambda e, o=du, i=wul_[0, ib]: e.dma_start(out=o, in_=i),
              reads=[wul_b_], writes=[wub1[ib].buf], tag=f"ldwuf1e0{ib}")

    xn = A.alloc("xn1", NCC * TCORE * 2)
    build_ln(C, x_in, x_in_b, xn, tag="l1", ldq=("sp", "act", "pool"))
    build_attn_a(C, layer_a, xn, x_in, x_in_b, xb1, xb1_b, core_grp=grp, tag="a")
    xn = A.alloc("xn2", NCC * TCORE * 2)
    build_ln(C, xb1, xb1_b, xn, tag="l2")
    build_ffn(C, layer_f, xn, xb1, xb1_b, xb1, xb1_b, tag="f1", pre=(wgb1, wub1, 3))
    A.free(xn)
    xn = A.alloc("xn3", NCC * TCORE * 2)
    build_ln(C, xb1, xb1_b, xn, tag="l3")
    build_attn_b(C, layer_b, xn, xb1, xb1_b, xb1, xb1_b, core_grp=grp, tag="b")
    xn = A.alloc("xn4", NCC * TCORE * 2)
    build_ln(C, xb1, xb1_b, xn, tag="l4")
    build_ffn(C, layer_f, xn, xb1, xb1_b, x_out, x_out_b, tag="f2")
    A.free(xn)

    stores = [o for o in P.ops if o.tag.startswith("res_stf2")]
    P.op("sp", lambda e: e.nop(), extra_deps=stores, tag="final")
    P.emit(nc, es)
    return nc, es


def _host_layouts(ins):
    import ml_dtypes
    out = ffn_weight_arrays(ins)

    def sliceT(W, c):
        s = W[256 * c:256 * (c + 1), :]
        return np.ascontiguousarray(s.T.reshape(NCC, 128, 256).transpose(1, 0, 2))

    out["waOT"] = np.ascontiguousarray(
        ins["L0_attn_a_Wo"].T.reshape(NCC, 128, 4, 512).transpose(2, 1, 0, 3))
    out["wbOT"] = np.ascontiguousarray(
        ins["L0_attn_b_w_o"].T.reshape(NCC, 128, 4, 512).transpose(2, 1, 0, 3))
    out["w1T"] = np.ascontiguousarray(
        ins["L0_attn_b_info_w1"].T.reshape(NCC, 128, 512).transpose(1, 0, 2))
    out["w2T"] = np.ascontiguousarray(
        ins["L0_attn_b_info_w2"].T.reshape(4, 128, 1).transpose(1, 0, 2))
    out["pkT"] = np.ascontiguousarray(ins["L0_attn_b_pk"].T)
    out["pvT"] = np.ascontiguousarray(ins["L0_attn_b_pv"].T)
    out["uoT"] = np.ascontiguousarray(ins["L0_attn_b_uo"].T)

    D_ = 128
    invf = (1.0 / 10000.0 ** (np.arange(0, D_, 2, dtype=np.float32) / np.float32(D_))
            ).astype(np.float32)
    t = np.arange(8192, dtype=np.float32)
    emb = np.concatenate([np.outer(t, invf), np.outer(t, invf)], axis=1).astype(np.float32)
    cos = np.cos(emb).astype(ml_dtypes.bfloat16)
    sign = np.where(np.arange(D_) % 2 == 0, -1.0, 1.0).astype(np.float32)
    ss = (np.sin(emb) * sign[None, :]).astype(ml_dtypes.bfloat16)
    out["rope_cos"] = np.ascontiguousarray(cos.reshape(16, 4, 128, 128).transpose(0, 2, 1, 3))
    out["rope_ss"] = np.ascontiguousarray(ss.reshape(16, 4, 128, 128).transpose(0, 2, 1, 3))

    masks_a = np.zeros((128, 4, 512), dtype=ml_dtypes.bfloat16)
    for j in range(4):
        for k in range(128):
            masks_a[k, j, :] = (np.arange(512) >= (128 * j + k)).astype(ml_dtypes.bfloat16)
    masks_b = np.zeros((128, 8, 512), dtype=ml_dtypes.bfloat16)
    for j in range(8):
        for k in range(128):
            rel = (j - 4) * 128 + k - np.arange(512)
            masks_b[k, j, :] = ((rel <= 0) & (rel >= -511)).astype(ml_dtypes.bfloat16)
    out["masks_a"] = masks_a
    out["masks_b"] = masks_b
    out["ident"] = np.eye(128, dtype=ml_dtypes.bfloat16)
    out["cvals"] = np.tile(np.array([Ctx.CONST_VALS], dtype=np.float32), (128, 1))
    out["ones_col"] = np.ones((128, 1), dtype=ml_dtypes.bfloat16)
    out["onesf"] = np.ones((1, 1), dtype=np.float32)

    per_core = {}
    for c in range(8):
        wqkv = ins["L0_attn_b_w_qkv"]
        vnat = wqkv[4096 + 256 * c:4096 + 256 * (c + 1), :]
        per_core[c] = dict(
            waQT=sliceT(ins["L0_attn_a_Wq"], c),
            waKT=sliceT(ins["L0_attn_a_Wk"], c),
            waVT=sliceT(ins["L0_attn_a_Wv"], c),
            wbQT=np.ascontiguousarray(
                wqkv[256 * c:256 * (c + 1), :].T.reshape(NCC, 128, 256).transpose(1, 0, 2)),
            wbKT=np.ascontiguousarray(
                wqkv[2048 + 256 * c:2048 + 256 * (c + 1), :].T.reshape(NCC, 128, 256).transpose(1, 0, 2)),
            wbVT=np.ascontiguousarray(
                wqkv[4096 + 256 * c:4096 + 256 * (c + 1), :].T.reshape(NCC, 128, 256).transpose(1, 0, 2)),
            wbVnat=np.ascontiguousarray(
                vnat.reshape(2, 128, NCC, 128).transpose(1, 0, 2, 3)),
        )
    return out, per_core


_CACHE = {}


def kernel(**inputs):
    from concourse.bass_utils import run_bass_kernel_spmd
    x_full = np.asarray(inputs["L0_attn_a_x"]).reshape(8192, 2048)
    shared, per_core = _host_layouts(inputs)
    if "prog" not in _CACHE:
        _CACHE["prog"] = _build_program()
    nc, es = _CACHE["prog"]
    in_maps = []
    for c in range(8):
        m = dict(shared)
        m["gate_wTl"] = shared["gate_wT"]
        del m["gate_wT"]
        m.update(per_core[c])
        m["x"] = np.ascontiguousarray(x_full[c * 1024:(c + 1) * 1024])
        in_maps.append(m)
    res = run_bass_kernel_spmd(nc, in_maps, core_ids=list(range(8)))
    out = np.concatenate([res.results[c]["xo"] for c in range(8)], axis=0)
    return out.reshape(1, 8192, 2048)
